# version 06
# baseline (speedup 1.0000x reference)
import numpy as np
import ml_dtypes
import concourse.bass as bass
import concourse.mybir as mybir
from concourse.bass_utils import run_bass_kernel_spmd
from contextlib import ExitStack

F32 = mybir.dt.float32
BF16 = mybir.dt.bfloat16
AX = mybir.AxisListType.X
AF = mybir.ActivationFunctionType
ALU = mybir.AluOpType

B, L, D = 2, 8192, 1024
TOK = 2048
FF = 2048
NE = 8

PHASES = 9
CUT = ""
DEBUG = ()



class Buf:
    __slots__ = ("name", "wop", "rops")
    def __init__(self, name):
        self.name = name
        self.wop = None
        self.rops = []

class Op:
    __slots__ = ("oid", "eng", "fn", "deps", "is_dma", "q", "marked", "sem", "val", "tag")
    def __init__(self, oid, eng, fn, is_dma, q, tag):
        self.oid = oid; self.eng = eng; self.fn = fn
        self.deps = set(); self.is_dma = is_dma; self.q = q
        self.marked = False; self.sem = None; self.val = None; self.tag = tag

class Pg:
    ENGS = ("t", "v", "s", "g", "y")

    def __init__(self, nc):
        self.nc = nc
        self.ops = []
        self.per_eng = {e: [] for e in self.ENGS}
        self._barrier_tail = None
        self.last_on = {e: None for e in self.ENGS}

    def buf(self, name):
        return Buf(name)

    def _record(self, op, reads, writes, accum):
        for b in reads:
            if b.wop is not None:
                op.deps.add(b.wop)
            b.rops.append(op.oid)
        for b in writes:
            if b.wop is not None and not accum:
                op.deps.add(b.wop)
            if not accum:
                for r in b.rops:
                    op.deps.add(r)
                b.rops = []
            b.wop = op.oid
        if self._barrier_tail is not None and self.last_on[op.eng] is None:
            op.deps.update(self._barrier_tail)
        op.deps.discard(op.oid)
        self.last_on[op.eng] = op.oid
        self.ops.append(op)
        self.per_eng[op.eng].append(op)

    def op(self, eng, fn, reads=(), writes=(), accum=False, tag=""):
        o = Op(len(self.ops), eng, fn, False, None, tag)
        self._record(o, reads, writes, accum)
        return o

    def dma(self, q, fn, reads=(), writes=(), tag=""):
        o = Op(len(self.ops), q, fn, True, q, tag)
        self._record(o, reads, writes, False)
        return o

    def cc(self, fn, reads=(), writes=(), tag="cc"):
        o = Op(len(self.ops), "g", fn, True, "cc", tag)
        self._record(o, reads, writes, False)
        return o

    def barrier(self):
        tail = set(o for o in (self.last_on[e] for e in self.ENGS) if o is not None)
        self._barrier_tail = tail
        for e in self.ENGS:
            self.last_on[e] = None

    def emit(self, block, es):
        nc = self.nc
        sems = {}
        for e in ("t", "v", "s", "g"):
            sems["tk_" + e] = es.enter_context(nc.semaphore("tk_" + e))
        for q in ("y", "g", "v", "s"):
            sems["dm_" + q] = es.enter_context(nc.semaphore("dm_" + q))
        sems["cc"] = es.enter_context(nc.semaphore("cc"))

        byid = self.ops
        for o in self.ops:
            for d in o.deps:
                a = byid[d]
                if a.is_dma:
                    continue
                if a.eng == "t" and o.eng == "t":
                    continue
                a.marked = True

        tick = {e: 0 for e in ("t", "v", "s", "g")}
        dmac = {q: 0 for q in ("y", "g", "v", "s")}
        ccc = 0
        for o in self.ops:
            if o.is_dma:
                if o.q == "cc":
                    ccc += 1
                    o.sem, o.val = sems["cc"], ccc
                else:
                    dmac[o.q] += 16
                    o.sem, o.val = sems["dm_" + o.q], dmac[o.q]
            elif o.marked:
                assert o.eng != "y", f"sync-engine compute op as dep: {o.tag}"
                tick[o.eng] += 1
                o.sem, o.val = sems["tk_" + o.eng], tick[o.eng]

        waits = {}
        wm = {e: {} for e in self.ENGS}
        for o in self.ops:
            need = {}
            if o.is_dma and o.q in ("y", "g", "v", "s") and o.val > 16:
                need[id(o.sem)] = (o.sem, o.val - 16)
            for d in sorted(o.deps):
                a = byid[d]
                if a.sem is None:
                    continue
                val = a.val
                key = id(a.sem)
                if wm[o.eng].get(key, 0) >= val:
                    continue
                if key not in need or need[key][1] < val:
                    need[key] = (a.sem, val)
            wl = list(need.values())
            for (sem, val) in wl:
                wm[o.eng][id(sem)] = val
            waits[o.oid] = wl

        finals = [("tk_t", tick["t"]), ("tk_v", tick["v"]), ("tk_s", tick["s"]),
                  ("tk_g", tick["g"]), ("dm_y", dmac["y"]), ("dm_g", dmac["g"]),
                  ("dm_v", dmac["v"]), ("dm_s", dmac["s"]), ("cc", ccc)]

        eng_attr = {"t": "tensor", "v": "vector", "s": "scalar", "g": "gpsimd", "y": "sync"}
        for e in self.ENGS:
            ops = self.per_eng[e]
            if not ops and e != "g":
                continue

            def body(engine, ops=ops, e=e):
                for o in ops:
                    for (sem, val) in waits[o.oid]:
                        engine.wait_ge(sem, val)
                    ins = o.fn(engine)
                    if o.sem is not None:
                        ins.then_inc(o.sem, 16 if (o.is_dma and o.q != "cc") else 1)
                if e == "g":
                    for nm, v in finals:
                        if v > 0 and nm != "tk_g":
                            engine.wait_ge(sems[nm], v)
            getattr(block, eng_attr[e])(body)



def build_program(debug=()):
    nc = bass.Bass("TRN2", target_bir_lowering=False)
    pg = Pg(nc)
    c = {}
    c["nc"], c["pg"] = nc, pg
    c["debug"] = set(debug)

    I = {}
    def IN(name, shape, dt=F32):
        I[name] = (nc.dram_tensor(name, shape, dt, kind="ExternalInput").ap(), pg.buf(name))
    c["I"] = I

    IN("x_fm", [D, TOK])
    IN("fWq", [D, D]); IN("fWk", [D, D]); IN("fWv", [D, D]); IN("fWo", [D, D])
    IN("fbq", [128, 8]); IN("fbk", [128, 8]); IN("fbv", [1, D]); IN("fbo", [128, 8])
    IN("gWr", [128, 64]); IN("gb", [1, 8])
    IN("eW1own", [D, FF]); IN("eW2own", [FF, D])
    IN("b1r", [1, NE * FF]); IN("b2e", [NE, D])
    IN("ln1g", [128, 8]); IN("ln1b", [128, 8]); IN("ln2g", [128, 8]); IN("ln2b", [128, 8])
    IN("down_w", [D, 256]); IN("down_b", [128, 2])
    IN("up_wr", [128, 2 * D]); IN("up_b", [128, 8])
    IN("wo_tr", [128, 2 * 256])
    IN("post_w", [128, 2]); IN("post_b", [128, 2])
    IN("lnt_w", [128, 2]); IN("lnt_b", [128, 2])
    IN("scan_wq", [256, 128]); IN("scan_wk", [256, 128]); IN("scan_wv", [256, 128])
    IN("scan_W1", [128, 128]); IN("scan_b1", [1, 128])
    IN("scan_g", [1, 128]); IN("scan_gb", [1, 128])
    IN("scan_lrw", [128, 2]); IN("scan_lrb", [1, 1])
    IN("tok_idx", [1, 16])
    IN("ident", [128, 128])
    IN("sel8", [8, 1024], BF16)
    IN("rarange", [1, 16])
    IN("masku", [16, L])
    IN("oidx", [128, 32], mybir.dt.int32)

    O = {}
    def ON(name, shape, dt=F32):
        O[name] = (nc.dram_tensor(name, shape, dt, kind="ExternalOutput").ap(), pg.buf(name))
    c["O"] = O
    ON("out_fm", [D, TOK])
    DBG_SHAPES = {
        "dbg_src": ([D, TOK], BF16), "dbg_q": ([D, TOK], BF16),
        "dbg_kall": ([4 * D, TOK], BF16), "dbg_vall": ([L, D], BF16),
        "dbg_attn": ([D, TOK], BF16), "dbg_x1": ([D, TOK], F32),
        "dbg_y1": ([D, TOK], F32), "dbg_hinall": ([D, TOK], BF16),
        "dbg_scanout": ([L, 128], F32), "dbg_y2": ([D, TOK], F32),
        "dbg_w": ([8, TOK], BF16),
        "dbg_xq": ([128, L], BF16), "dbg_xkt": ([L, 128], BF16),
        "dbg_tn": ([L, 128], F32), "dbg_pn": ([L, 128], F32),
        "dbg_g12": ([16, L], BF16), "dbg_le": ([16, 512], F32), "dbg_probe": ([128, 2048], F32), "dbg_kin": ([D, TOK], BF16), "dbg_p2": ([128, 2048], F32), "dbg_p3": ([128, 2048], F32), "dbg_p4": ([128, 4096], BF16), "dbg_p5": ([128, 2048], F32), "dbg_p6": ([128, 2048], F32), "dbg_p8": ([128, 4096], BF16),
    }
    for name in debug:
        sh, dt = DBG_SHAPES[name]
        O[name] = (nc.dram_tensor(name, sh, dt, kind="ExternalOutput").ap(), pg.buf(name))

    DR = {}
    def DRT(name, shape, dt=F32):
        DR[name] = (nc.dram_tensor(name, shape, dt, kind="Internal").ap(), pg.buf(name))
    c["DR"] = DR

    DRT("dr_w1own", [D, FF], BF16)
    DRT("dr_w2own", [FF, D], BF16)
    DRT("dr_w1all", [NE * D, FF], BF16)
    DRT("dr_w2all", [NE * FF, D], BF16)
    DRT("dr_wattn", [4 * D, D], BF16)
    DRT("dr_k_in", [D, TOK], BF16)
    DRT("dr_k_all", [4 * D, TOK], BF16)
    DRT("dr_v_in", [TOK, D], BF16)
    DRT("dr_v_all", [L, D], BF16)
    DRT("dr_x1", [D, TOK])
    DRT("dr_y1", [D, TOK])
    DRT("dr_y2", [D, TOK])
    DRT("dr_hin", [256, TOK], BF16)
    DRT("dr_hin_all", [D, TOK], BF16)
    DRT("dr_xktok", [L, 128], BF16)
    DRT("dr_tn", [L, 128])
    DRT("dr_pn", [L, 128])
    DRT("dr_out", [L, 128])
    DRT("dr_out_all", [4 * L, 128])
    DRT("dr_bar_in", [1, 8])
    DRT("dr_bar_out", [1, 8])

    SB = {}
    SBH = {}
    def SBT(name, shape, dt=F32):
        h = nc.alloc_sbuf_tensor(name, shape, dt)
        SBH[name] = h
        SB[name] = (h.ap(), pg.buf(name))
        return SB[name]
    c["SB"], c["SBT"], c["SBH"] = SB, SBT, SBH

    PS = {}
    def PST(name, shape, dt=F32):
        PS[name] = (nc.alloc_psum_tensor(name, shape, dt).ap(), pg.buf(name))
    c["PS"] = PS
    PST("pss0", [128, 1024])
    PST("pss1", [128, 1024])
    PST("pw0", [128, 512])
    PST("pw1", [128, 512])
    PST("pw2", [128, 512])
    PST("pw3", [128, 512])

    SBT("AR_q", [128, 16384], BF16)
    SBT("AR_s", [128, 16384], BF16)
    SBT("AR_a", [128, 16384], BF16)
    SBT("AR_x", [128, 16384], BF16)
    SBT("FA", [128, 4096])
    SBT("xsb", [128, 1024])
    SBT("wproj", [128, 8192], BF16)
    SBT("P0", [128, 1024], BF16); SBT("P1", [128, 1024], BF16)

    SBT("onescol", [128, 1])
    SBT("onesrow", [1, 128])
    SBT("onesrow_bf", [1, 512], BF16)
    SBT("eps5", [1, 1]); SBT("eps6r", [128, 1])
    SBT("ident_f", [128, 128])
    SBT("sq0", [128, 512]); SBT("sq1", [128, 512])
    SBT("r_mu", [1, 512]); SBT("r_t", [1, 512]); SBT("r_t2", [1, 512])
    SBT("evA", [128, 512]); SBT("evB", [128, 512])
    SBT("ldtmp", [128, 1024])

    build_all(c)

    with ExitStack() as es:
        block = es.enter_context(nc.Block())
        pg.emit(block, es)
    return nc



def helpers(c):
    nc, pg = c["nc"], c["pg"]

    def op(eng, fn, reads=(), writes=(), accum=False, tag=""):
        return pg.op(eng, fn, reads, writes, accum, tag)

    def dmac(q, dst, src, reads, writes, tag="dma"):
        return pg.dma(q, lambda e, dst=dst, src=src: e.dma_start(out=dst, in_=src),
                      reads, writes, tag)

    def mm(psd, psb, lhsT, rhs, rbufs, start, stop, tp=None, tag="mm"):
        return op("t", lambda t, psd=psd, lhsT=lhsT, rhs=rhs, start=start, stop=stop,
                  tp=tp: t.matmul(psd, lhsT, rhs, start=start, stop=stop, tile_position=tp),
                  reads=rbufs, writes=(psb,), accum=not start, tag=tag)

    def vtt(alu, dst, db, a, ab, b2, bb, tag, eng="v"):
        return op(eng, lambda v, dst=dst, a=a, b2=b2, alu=alu: v.tensor_tensor(dst, a, b2, alu),
                  reads=(ab, bb), writes=(db,), tag=tag)

    def vts(dst, db, a, ab, s1, s1b, op0, tag, s2=None, s2b=None, op1=ALU.mult):
        rd = [ab] + ([s1b] if s1b is not None else []) + ([s2b] if s2b is not None else [])
        if s2 is None:
            return op("v", lambda v, dst=dst, a=a, s1=s1, op0=op0:
                      v.tensor_single_scalar(dst, a, s1, op0),
                      reads=tuple(rd), writes=(db,), tag=tag)
        return op("v", lambda v, dst=dst, a=a, s1=s1, s2=s2, op0=op0, op1=op1:
                  v.tensor_scalar(dst, a, s1, s2, op0, op1),
                  reads=tuple(rd), writes=(db,), tag=tag)

    def vstt(dst, db, a, ab, sc, scb, b2, bb, op0, op1, tag, acc=None, accb=None):
        rd = [ab, bb] + ([scb] if scb is not None else [])
        wr = [db] + ([accb] if accb is not None else [])
        return op("v", lambda v, dst=dst, a=a, sc=sc, b2=b2, op0=op0, op1=op1, acc=acc:
                  v.scalar_tensor_tensor(dst, a, sc, b2, op0, op1, accum_out=acc),
                  reads=tuple(rd), writes=tuple(wr), tag=tag)

    def vcopy(dst, db, a, ab, tag, eng="v"):
        return op(eng, lambda v, dst=dst, a=a: v.tensor_copy(dst, a),
                  reads=(ab,), writes=(db,), tag=tag)

    def vrecip(dst, db, a, ab, tag):
        return op("v", lambda v, dst=dst, a=a: v.reciprocal(dst, a),
                  reads=(ab,), writes=(db,), tag=tag)

    def vrmax(dst, db, a, ab, tag):
        return op("v", lambda v, dst=dst, a=a: v.reduce_max(dst, a, AX),
                  reads=(ab,), writes=(db,), tag=tag)

    def vmemset(dst, db, val, tag):
        return op("v", lambda v, dst=dst, val=val: v.memset(dst, val),
                  writes=(db,), tag=tag)

    def sact(dst, db, a, ab, func, tag, bias=None, biasb=None, scale=None, scaleb=None,
             acc=None, accb=None):
        rd = [ab] + ([biasb] if biasb is not None else []) + \
             ([scaleb] if scaleb is not None else [])
        wr = [db] + ([accb] if accb is not None else [])
        kw = {}
        if bias is not None:
            kw["bias"] = bias
        if scale is not None:
            kw["scale"] = scale
        if acc is not None:
            kw["accum_out"] = acc
        if func == AF.Copy and ("bias" in kw or "scale" in kw):
            func = AF.Identity
        return op("s", lambda s, dst=dst, a=a, func=func, kw=kw:
                  s.activation(dst, a, func, **kw),
                  reads=tuple(rd), writes=tuple(wr), tag=tag)

    for nm, f in [("op", op), ("dma", dmac), ("mm", mm), ("vtt", vtt), ("vts", vts),
                  ("vstt", vstt), ("vcopy", vcopy), ("vrecip", vrecip), ("vrmax", vrmax),
                  ("vmemset", vmemset), ("sact", sact)]:
        c[nm] = f

    SB, PS = c["SB"], c["PS"]

    def ln_rows(chunks, w, nfeat, outA, outB, tag, ps0="pss0", ps1="pss1"):
        oc, oc_b = SB["onescol"]
        eps5, eps5_b = SB["eps5"]
        psum_t, psum_b = PS[ps0]; psq_t, psq_b = PS[ps1]
        psum = psum_t[0:1, 0:w]; psq = psq_t[0:1, 0:w]
        sqs = [SB["sq0"], SB["sq1"]]
        mu, mu_b = SB["r_mu"]; t1, t1_b = SB["r_t"]; t2, t2_b = SB["r_t2"]
        n = len(chunks)
        for i, ch in enumerate(chunks):
            cap, cb = ch() if callable(ch) else ch
            mm(psum, psum_b, oc, cap, (oc_b, cb), i == 0, i == n - 1, tag=tag + ".s")
            sq, sq_b = sqs[i % 2]
            sact(sq[:, 0:w], sq_b, cap, cb, AF.Square, tag + ".sq")
            mm(psq, psq_b, oc, sq[:, 0:w], (oc_b, sq_b), i == 0, i == n - 1, tag=tag + ".q")
        inv = 1.0 / nfeat
        vts(mu[:, 0:w], mu_b, psum, psum_b, inv, None, ALU.mult, tag + ".mu")
        vts(t1[:, 0:w], t1_b, psq, psq_b, inv, None, ALU.mult, tag + ".ss")
        vtt(ALU.mult, t2[:, 0:w], t2_b, mu[:, 0:w], mu_b, mu[:, 0:w], mu_b, tag + ".m2")
        vtt(ALU.subtract, t1[:, 0:w], t1_b, t1[:, 0:w], t1_b, t2[:, 0:w], t2_b, tag + ".var")
        sact(t2[:, 0:w], t2_b, t1[:, 0:w], t1_b, AF.Sqrt, tag + ".std",
             bias=eps5, biasb=eps5_b)
        Aap, Ab = outA
        Bap, Bb = outB
        vrecip(Aap, Ab, t2[:, 0:w], t2_b, tag + ".A")
        vstt(Bap, Bb, mu[:, 0:w], mu_b, -1.0, None, Aap, Ab, ALU.mult, ALU.mult, tag + ".B")
    c["ln_rows"] = ln_rows

    DR = c["DR"]
    def cc_barrier(grp, reads, tag):
        bi, bi_b = DR["dr_bar_in"]
        bo, bo_b = DR["dr_bar_out"]
        pg.cc(lambda g, bi=bi, bo=bo, grp=grp: g.collective_compute(
            "AllReduce", ALU.add, replica_groups=grp, ins=[bi], outs=[bo]),
            reads=tuple(reads) + (bi_b,), writes=(bo_b,), tag="bar." + tag)
    c["cc_barrier"] = cc_barrier

    def bcast(row_ap, row_b, w, psname, lo=0, tag="bc"):
        orow, orow_b = SB["onesrow"]
        pst, psb = PS[psname]
        dst = pst[:, lo:lo + w]
        mm(dst, psb, orow, row_ap, (orow_b, row_b), True, True, tag=tag)
        return dst, psb
    c["bcast"] = bcast

    def sbt_get(name, shape, dt=F32):
        if name not in SB:
            c["SBT"](name, shape, dt)
        return SB[name]
    c["sbt_get"] = sbt_get

    def carve(name, shape, dt=F32):
        key = "cv_" + name
        if key in SB:
            return SB[key]
        SBH = c["SBH"]
        if dt != BF16:
            view = SBH["AR_x"].bitcast(dt).ap()
            cur = c.setdefault("carve_f32", 4160)
            n = shape[1]
            ap = view[0:shape[0], cur:cur + n]
            c["carve_f32"] = cur + n
            assert c["carve_f32"] <= 7040, f"carve f32 overflow {name}"
        else:
            view = SBH["AR_x"].ap()
            cur = c.setdefault("carve_bf16", 14080)
            n = shape[1]
            ap = view[0:shape[0], cur:cur + n]
            c["carve_bf16"] = cur + n
            assert c["carve_bf16"] <= 16384, f"carve bf16 overflow {name}"
        SB[key] = (ap, pg.buf(key))
        return SB[key]
    c["carve"] = carve


def build_all(c):
    helpers(c)
    nc, pg, op, dma = c["nc"], c["pg"], c["op"], c["dma"]
    I, SB = c["I"], c["SB"]
    vmemset = c["vmemset"]

    for nm, val in [("onescol", 1.0), ("onesrow", 1.0), ("onesrow_bf", 1.0),
                    ("eps5", 1e-5), ("eps6r", 1e-6)]:
        ap, b = SB[nm]
        vmemset(ap, b, val, "c." + nm)
    idf, idf_b = SB["ident_f"]
    dma("y", idf, I["ident"][0], (I["ident"][1],), (idf_b,), "ident")

    phase_weights(c)
    if PHASES >= 1:
        ffn_call(c, 0)
        pg.barrier()
    if PHASES >= 2:
        ttt_prestage(c)
        pg.barrier()
    if PHASES >= 3:
        ttt_scan(c)
        pg.barrier()
    if PHASES >= 4:
        post_ttt(c)
        pg.barrier()
    if PHASES >= 5:
        ffn_call(c, 1)
        pg.barrier()
    finalize_debug(c)


def phase_weights(c):
    nc, pg, op, dma = c["nc"], c["pg"], c["op"], c["dma"]
    I, DR, SB = c["I"], c["DR"], c["SB"]

    wat, wat_b = DR["dr_wattn"]
    for i, nm in enumerate(["fWq", "fWk", "fWv", "fWo"]):
        src, sb = I[nm]
        dma("g", wat[i * D:(i + 1) * D, :], src, (sb,), (wat_b,), "wat")
    w1o, w1o_b = DR["dr_w1own"]; w2o, w2o_b = DR["dr_w2own"]
    dma("g", w1o, I["eW1own"][0], (I["eW1own"][1],), (w1o_b,), "w1c")
    dma("g", w2o, I["eW2own"][0], (I["eW2own"][1],), (w2o_b,), "w2c")
    w1a, w1a_b = DR["dr_w1all"]; w2a, w2a_b = DR["dr_w2all"]
    c["cc_barrier"]([list(range(8))], (w1o_b, w2o_b), "w")
    pg.cc(lambda g, a=w1o, b=w1a: g.collective_compute(
        "AllGather", ALU.bypass, replica_groups=[list(range(8))], ins=[a], outs=[b]),
        reads=(w1o_b,), writes=(w1a_b,), tag="ag_w1")
    pg.cc(lambda g, a=w2o, b=w2a: g.collective_compute(
        "AllGather", ALU.bypass, replica_groups=[list(range(8))], ins=[a], outs=[b]),
        reads=(w2o_b,), writes=(w2a_b,), tag="ag_w2")



def ffn_call(c, call):
    nc, pg, op, dma, mm = c["nc"], c["pg"], c["op"], c["dma"], c["mm"]
    I, O, DR, SB, PS, SBT = c["I"], c["O"], c["DR"], c["SB"], c["PS"], c["SBT"]
    ln_rows, bcast, sbt_get = c["ln_rows"], c["bcast"], c["sbt_get"]
    vtt, vts, vstt, vcopy = c["vtt"], c["vts"], c["vstt"], c["vcopy"]
    vrecip, vrmax, vmemset, sact = c["vrecip"], c["vrmax"], c["vmemset"], c["sact"]
    sfx = f".{call}"

    xsrc, xsrc_b = I["x_fm"] if call == 0 else DR["dr_y2"]
    oc, oc_b = SB["onescol"]
    orow, orow_b = SB["onesrow"]

    AR_q, _ = SB["AR_q"]; AR_s, _ = SB["AR_s"]; AR_a, _ = SB["AR_a"]; AR_x, _ = SB["AR_x"]
    FA, _ = SB["FA"]
    q_bufs = [pg.buf(f"q{call}_{i}") for i in range(8)]
    src_bufs = [pg.buf(f"src{call}_{i}") for i in range(8)]
    attn_bufs = [pg.buf(f"attn{call}_{i}") for i in range(8)]
    x1n_bufs = [pg.buf(f"x1n{call}_{i}") for i in range(8)]
    xsb, _ = SB["xsb"]
    xsb_bufs = [pg.buf(f"xsb{call}_{i}") for i in range(2)]
    x1sb = xsb
    x1sb_bufs = xsb_bufs
    wproj, _ = SB["wproj"]
    rwA, rwA_b = sbt_get("rwA", [1, 512])
    rwB, rwB_b = sbt_get("rwB", [1, 512])

    def src_sl(kc, t0, w):
        return AR_s[:, 2048 * kc + t0: 2048 * kc + t0 + w]
    def q_sl(kc, t0, w):
        return AR_q[:, 2048 * kc + t0: 2048 * kc + t0 + w]
    def at_sl(kc, t0, w):
        return AR_a[:, 2048 * kc + t0: 2048 * kc + t0 + w]
    def x1n_sl(kc, t0, w):
        return AR_x[:, 2048 * kc + t0: 2048 * kc + t0 + w]

    fa_bufs = [pg.buf(f"fa{call}_{i}") for i in range(8)]
    wproj_buf = pg.buf(f"wproj{call}")

    wat, wat_b = DR["dr_wattn"]
    pwn = ["pw0", "pw1", "pw2", "pw3"]

    for tc in range(4):
        t0 = 512 * tc
        chunks = []
        for kc in range(8):
            fa_sl = FA[:, 512 * kc: 512 * kc + 512]
            fb = fa_bufs[kc]
            dma("y", fa_sl, xsrc[128 * kc:128 * kc + 128, t0:t0 + 512],
                (xsrc_b,), (fb,), "xld" + sfx)
            chunks.append((fa_sl, fb))
        ln_rows(chunks, 512, D, (rwA, rwA_b), (rwB, rwB_b), "mln" + sfx)
        Abc, Abc_b = bcast(rwA, rwA_b, 512, "pss0", 0, "mlA" + sfx)
        Bbc, Bbc_b = bcast(rwB, rwB_b, 512, "pss0", 512, "mlB" + sfx)
        if tc == 0 and call == 0 and "dbg_probe" in c["debug"]:
            dp, dpb = O["dbg_probe"]
            dma("g", dp[0:128, 0:512], chunks[0][0], (chunks[0][1],), (dpb,), "dbgp")
            dma("g", dp[0:1, 512:1024], rwA, (rwA_b,), (dpb,), "dbgp")
            dma("g", dp[1:2, 512:1024], rwB, (rwB_b,), (dpb,), "dbgp")
            dma("g", dp[0:1, 1024:1536], SB["r_mu"][0], (SB["r_mu"][1],), (dpb,), "dbgp")
            dma("g", dp[1:2, 1024:1536], SB["r_t"][0], (SB["r_t"][1],), (dpb,), "dbgp")
        ev, ev_b = SB["evA"]
        for kc in range(8):
            cap, cb = chunks[kc]
            vtt(ALU.mult, ev, ev_b, cap, cb, Abc, Abc_b, "mla" + sfx)
            vtt(ALU.add, src_sl(kc, t0, 512), src_bufs[kc], ev, ev_b, Bbc, Bbc_b,
                "mlb" + sfx)

    if "dbg_src" in c["debug"] and call == 0:
        dap, db = O["dbg_src"]
        for kc in range(8):
            dma("g", dap[128 * kc:128 * (kc + 1), :], src_sl(kc, 0, 2048),
                (src_bufs[kc],), (db,), "dbg")

    if CUT == "a1":
        return
    def load_w(widx, tag):
        wb = wproj_buf
        src = wat[widx * D: (widx + 1) * D, :].rearrange("(kc p) d -> p kc d", p=128)
        dma("y", wproj, src, (wat_b,), (wb,), "wld." + tag)
        return wb

    wb = load_w(0, "q" + sfx)
    bq_sb, bq_b = sbt_get(f"bq{call}", [128, 8])
    dma("y", bq_sb, I["fbq"][0], (I["fbq"][1],), (bq_b,), "bq")
    qn, qn_b = sbt_get("qn", [1, 16])
    kn, kn_b = sbt_get("kn", [1, 16])
    nrm, nrm_b = sbt_get("nrm", [1, 4])
    vmemset(qn, qn_b, -1e30, "qn0")
    vmemset(kn, kn_b, -1e30, "kn0")

    for dc in range(8):
        for tc in range(4):
            pst, psb = PS[pwn[tc % 4]]
            ps = pst[:, 0:512]
            for kc in range(8):
                mm(ps, psb, wproj[:, 1024 * kc + 128 * dc: 1024 * kc + 128 * dc + 128],
                   src_sl(kc, 512 * tc, 512), (wb, src_bufs[kc]), kc == 0, kc == 7,
                   tag="qmm" + sfx)
            dst = q_sl(dc, 512 * tc, 512)
            sact(dst, q_bufs[dc], ps, psb, AF.Copy, "qev" + sfx,
                 bias=bq_sb[:, dc:dc + 1], biasb=bq_b)
            sq, sq_b = SB["sq0"]
            sact(sq, sq_b, dst, q_bufs[dc], AF.Square, "qsq" + sfx)
            for h in range(2):
                pn2t, pn2b = PS["pss0"]
                pn2 = pn2t[0:1, 0:512]
                mm(pn2, pn2b, oc[64 * h:64 * h + 64, :], sq[64 * h:64 * h + 64, :],
                   (oc_b, sq_b), True, True, tag="qn" + sfx)
                vrmax(nrm[:, 0:1], nrm_b, pn2, pn2b, "qnm" + sfx)
                hh = 2 * dc + h
                vtt(ALU.max, qn[:, hh:hh + 1], qn_b, qn[:, hh:hh + 1], qn_b,
                    nrm[:, 0:1], nrm_b, "qnu" + sfx)

    if "dbg_q" in c["debug"] and call == 0:
        dap, db = O["dbg_q"]
        for kc in range(8):
            dma("g", dap[128 * kc:128 * (kc + 1), :], q_sl(kc, 0, 2048),
                (q_bufs[kc],), (db,), "dbg")

    if CUT == "q":
        return
    wb = load_w(1, "k" + sfx)
    bk_sb, bk_b = sbt_get(f"bk{call}", [128, 8])
    dma("y", bk_sb, I["fbk"][0], (I["fbk"][1],), (bk_b,), "bk")
    kin, kin_b = DR["dr_k_in"]
    kev = AR_a[:, 0:2048]
    kev_b = pg.buf(f"kev{call}")
    for dc in range(8):
        for tc in range(4):
            pst, psb = PS[pwn[tc % 4]]
            ps = pst[:, 0:512]
            for kc in range(8):
                mm(ps, psb, wproj[:, 1024 * kc + 128 * dc: 1024 * kc + 128 * dc + 128],
                   src_sl(kc, 512 * tc, 512), (wb, src_bufs[kc]), kc == 0, kc == 7,
                   tag="kmm" + sfx)
            dst = kev[:, 512 * (tc % 4): 512 * (tc % 4) + 512]
            sact(dst, kev_b, ps, psb, AF.Copy, "kev" + sfx,
                 bias=bk_sb[:, dc:dc + 1], biasb=bk_b)
            dma("g", kin[128 * dc:128 * dc + 128, 512 * tc:512 * tc + 512], dst,
                (kev_b,), (kin_b,), "kst" + sfx)

    if CUT == "k":
        return
    wb = load_w(2, "v" + sfx)
    ldt, _ = SB["ldtmp"]
    bvbc, bvbc_b = sbt_get("bvbc", [128, 1024])
    bvr_b = pg.buf(f"bvr{call}")
    for j in range(2):
        bvr = ldt[0:1, 0:512]
        dma("y", bvr, I["fbv"][0][0:1, 512 * j:512 * j + 512], (I["fbv"][1],), (bvr_b,), "bv")
        pbc, pbc_b = bcast(bvr, bvr_b, 512, "pss0", 0, "bvb")
        sact(bvbc[:, 512 * j:512 * j + 512], bvbc_b, pbc, pbc_b, AF.Copy, "bve")
    vin, vin_b = DR["dr_v_in"]
    for ts in range(16):
        for j in range(2):
            pst, psb = PS[pwn[(2 * ts + j) % 4]]
            ps = pst[:, 0:512]
            for kc in range(8):
                mm(ps, psb, src_sl(kc, 128 * ts, 128),
                   wproj[:, 1024 * kc + 512 * j: 1024 * kc + 512 * j + 512],
                   (wb, src_bufs[kc]), kc == 0, kc == 7, tag="vmm" + sfx)
            dst = kev[:, 1024 * (ts % 2) + 512 * j: 1024 * (ts % 2) + 512 * j + 512]
            vtt(ALU.add, dst, kev_b, ps, psb, bvbc[:, 512 * j:512 * j + 512], bvbc_b,
                "vev" + sfx)
            dma("g", vin[128 * ts:128 * ts + 128, 512 * j:512 * j + 512], dst,
                (kev_b,), (vin_b,), "vst" + sfx)

    if "dbg_kin" in c["debug"] and call == 0:
        dma("g", O["dbg_kin"][0], kin, (kin_b,), (O["dbg_kin"][1],), "dbg")
    if CUT == "v":
        return
    kall, kall_b = DR["dr_k_all"]; vall, vall_b = DR["dr_v_all"]
    grp = [[0, 1, 2, 3], [4, 5, 6, 7]]
    c["cc_barrier"](grp, (kin_b, vin_b), "kv" + sfx)
    pg.cc(lambda g, a=kin, b=kall, grp=grp: g.collective_compute(
        "AllGather", ALU.bypass, replica_groups=grp, ins=[a], outs=[b]),
        reads=(kin_b,), writes=(kall_b,), tag="ag_k" + sfx)
    pg.cc(lambda g, a=vin, b=vall, grp=grp: g.collective_compute(
        "AllGather", ALU.bypass, replica_groups=grp, ins=[a], outs=[b]),
        reads=(vin_b,), writes=(vall_b,), tag="ag_v" + sfx)

    if "dbg_kall" in c["debug"] and call == 0:
        dma("g", O["dbg_kall"][0], kall, (kall_b,), (O["dbg_kall"][1],), "dbg")
    if "dbg_vall" in c["debug"] and call == 0:
        dma("g", O["dbg_vall"][0], vall, (vall_b,), (O["dbg_vall"][1],), "dbg")

    if CUT == "ag":
        return
    ksq, ksq_b = SB["sq1"]
    kb = pg.buf(f"knl{call}")
    for r in range(4):
        for dc in range(8):
            ksl = AR_a[:, 0:2048]
            dma("y", ksl, kall[1024 * r + 128 * dc: 1024 * r + 128 * dc + 128, :],
                (kall_b,), (kb,), "knl" + sfx)
            for tc in range(4):
                sact(ksq, ksq_b, ksl[:, 512 * tc:512 * tc + 512], kb, AF.Square,
                     "ksq" + sfx)
                for h in range(2):
                    pn2t, pn2b = PS["pss0"]
                    pn2 = pn2t[0:1, 0:512]
                    mm(pn2, pn2b, oc[64 * h:64 * h + 64, :], ksq[64 * h:64 * h + 64, :],
                       (oc_b, ksq_b), True, True, tag="kn" + sfx)
                    vrmax(nrm[:, 0:1], nrm_b, pn2, pn2b, "knm" + sfx)
                    hh = 2 * dc + h
                    vtt(ALU.max, kn[:, hh:hh + 1], kn_b, kn[:, hh:hh + 1], kn_b,
                        nrm[:, 0:1], nrm_b, "knu" + sfx)
    ncr, ncr_b = sbt_get("ncr", [1, 16])
    vtt(ALU.mult, ncr, ncr_b, qn, qn_b, kn, kn_b, "cc1")
    sact(ncr, ncr_b, ncr, ncr_b, AF.Sqrt, "cc2")
    vts(ncr, ncr_b, ncr, ncr_b, -0.125, None, ALU.mult, "cc3")
    negCb, negCb_b = sbt_get("negCb", [128, 16])
    pbc, pbc_b = bcast(ncr, ncr_b, 16, "pss0", 0, "ccb")
    sact(negCb, negCb_b, pbc, pbc_b, AF.Copy, "cce")

    if CUT == "norm":
        return
    pg.barrier()

    khp = AR_s
    vhp = AR_x
    P_sb = [SB["P0"], SB["P1"]]
    idf, idf_b = SB["ident_f"]

    khp_bufs = [pg.buf(f"khp{call}_{r}") for r in range(4)]
    vhp_bufs = [pg.buf(f"vhp{call}_{r}") for r in range(4)]
    vhpo_b = pg.buf(f"vhpo{call}")
    pv_bufs = [pg.buf(f"pv{call}_{i}") for i in range(2)]
    p_bufs = [pg.buf(f"pp{call}_{i}") for i in range(4)]
    sc_bufs = [pg.buf(f"sc{call}_{i}") for i in range(4)]
    sct, sct0_b = PS["pss0"]
    sct1, sct1_b = PS["pss1"]
    sc_slots = [sct[:, 0:512], sct[:, 512:1024], sct1[:, 0:512], sct1[:, 512:1024]]
    pvt = [PS["pw0"], PS["pw1"]]

    for hp in range(8):
        vh = vhp[:, 0:8320].rearrange("p (kt ss) -> p kt ss", ss=130)
        vmemset(vh[:, :, 64:65], vhpo_b, 1.0, "vone")
        vmemset(vh[:, :, 129:130], vhpo_b, 1.0, "vone")
        for r in range(4):
            dma("y", khp[:, 2048 * r:2048 * r + 2048],
                kall[1024 * r + 128 * hp: 1024 * r + 128 * hp + 128, :],
                (kall_b,), (khp_bufs[r],), "khl" + sfx)
            vsrc = vall[2048 * r:2048 * r + 2048, 128 * hp:128 * hp + 128]
            vsrc_r = vsrc.rearrange("(kt p) d -> p kt d", p=128)
            for h in range(2):
                dmao = vhp[:, 0:8320].rearrange("p (kt ss) -> p kt ss", ss=130)[
                    :, 16 * r:16 * r + 16, 65 * h: 65 * h + 64]
                dma("g", dmao, vsrc_r[:, :, 64 * h:64 * h + 64],
                    (vall_b, vhpo_b), (vhp_bufs[r],), "vhl" + sfx)
        for qt in range(4):
            rhs_base = 2048 * hp + 512 * qt
            def smm(kt, h):
                si = (2 * kt + h) % 4
                sc = sc_slots[si]
                rhs = AR_q[64 * h:64 * h + 64, rhs_base:rhs_base + 512]
                mm(sc, sc_bufs[si],
                   khp[64 * h:64 * h + 64, 128 * kt:128 * kt + 128], rhs,
                   (khp_bufs[kt // 16], q_bufs[hp]), True, True, tp=(64 * h, 0),
                   tag="smm" + sfx)

            def expo(kt, h):
                si = (2 * kt + h) % 4
                P, _ = P_sb[h]
                Pd = P[:, 512 * (kt % 2):512 * (kt % 2) + 512]
                pi = 2 * h + kt % 2
                hh = 2 * hp + h
                sact(Pd, p_bufs[pi], sc_slots[si], sc_bufs[si], AF.Exp, "exp" + sfx,
                     bias=negCb[:, hh:hh + 1], biasb=negCb_b, scale=0.125)

            def pvm(kt, h):
                P, _ = P_sb[h]
                Pd = P[:, 512 * (kt % 2):512 * (kt % 2) + 512]
                pi = 2 * h + kt % 2
                pv, _ = pvt[h]
                mm(pv[0:65, :], pv_bufs[h],
                   vhp[:, 130 * kt + 65 * h: 130 * kt + 65 * h + 65], Pd,
                   (vhp_bufs[kt // 16], vhpo_b, p_bufs[pi]),
                   kt == 0, kt == 63, tag="pvm" + sfx)

            smm(0, 0); smm(0, 1)
            expo(0, 0); expo(0, 1)
            for kt in range(1, 64):
                smm(kt, 0); smm(kt, 1)
                expo(kt, 0); expo(kt, 1)
                pvm(kt - 1, 0); pvm(kt - 1, 1)
            pvm(63, 0); pvm(63, 1)
            for h in range(2):
                pv, _ = pvt[h]
                rr, rr_b = SB["r_t"]
                vrecip(rr[:, 0:512], rr_b, pv[64:65, :], pv_bufs[h], "arc" + sfx)
                rbct, rbcb = PS["pw2"]
                mm(rbct[0:64, 0:512], rbcb, orow[:, 0:64], rr[:, 0:512],
                   (orow_b, rr_b), True, True, tag="abc" + sfx)
                ev, ev_b = SB["evA"]
                sact(ev[0:64, :], ev_b, pv[0:64, :], pv_bufs[h], AF.Copy,
                     "aev" + sfx)
                dst = AR_a[64 * h:64 * h + 64, rhs_base:rhs_base + 512]
                vtt(ALU.mult, dst, attn_bufs[hp], ev[0:64, :], ev_b,
                    rbct[0:64, 0:512], rbcb, "adv" + sfx)

    if "dbg_attn" in c["debug"] and call == 0:
        dap, db = O["dbg_attn"]
        for kc in range(8):
            dma("g", dap[128 * kc:128 * (kc + 1), :], at_sl(kc, 0, 2048),
                (attn_bufs[kc],), (db,), "dbg")

    if CUT == "attn":
        return
    pg.barrier()

    wb = load_w(3, "o" + sfx)
    bo_sb, bo_b = sbt_get(f"bo{call}", [128, 8])
    dma("y", bo_sb, I["fbo"][0], (I["fbo"][1],), (bo_b,), "bo")
    g1_sb, g1_b = sbt_get("g1s", [128, 8])
    b1_sb, b1s_b = sbt_get("b1s", [128, 8])
    dma("y", g1_sb, I["ln1g"][0], (I["ln1g"][1],), (g1_b,), "g1")
    dma("y", b1_sb, I["ln1b"][0], (I["ln1b"][1],), (b1s_b,), "b1")
    l1A, l1A_b = sbt_get("l1A", [1, 512])
    l1B, l1B_b = sbt_get("l1B", [1, 512])
    x1pre_bufs = [pg.buf(f"x1p{call}_{i}") for i in range(8)]

    x1f, x1f_b = DR["dr_x1"]
    for tc in range(4):
        t0 = 512 * tc
        def mk_thunk(kc, t0=t0):
            def th():
                xc = xsb[:, 512 * (kc % 2):512 * (kc % 2) + 512]
                xcb = xsb_bufs[kc % 2]
                dma("y", xc, xsrc[128 * kc:128 * kc + 128, t0:t0 + 512], (xsrc_b,),
                    (xcb,), "xst" + sfx)
                return xc, xcb
            return th
        ln_rows([mk_thunk(kc) for kc in range(8)], 512, D, (rwA, rwA_b), (rwB, rwB_b),
                "mr" + sfx)
        Abc, Abc_b = bcast(rwA, rwA_b, 512, "pss0", 0, "wA" + sfx)
        Bbc, Bbc_b = bcast(rwB, rwB_b, 512, "pss0", 512, "wB" + sfx)
        x1chunks = []
        for dc in range(8):
            pst, psb = PS[pwn[dc % 2]]
            ps = pst[:, 0:512]
            for kc in range(8):
                mm(ps, psb, wproj[:, 1024 * kc + 128 * dc: 1024 * kc + 128 * dc + 128],
                   at_sl(kc, t0, 512), (wb, attn_bufs[kc]), kc == 0, kc == 7,
                   tag="omm" + sfx)
            ev, ev_b = SB["evA"]
            sact(ev, ev_b, ps, psb, AF.Copy, "oev" + sfx,
                 bias=bo_sb[:, dc:dc + 1], biasb=bo_b)
            xc = xsb[:, 512 * (dc % 2):512 * (dc % 2) + 512]
            xcb = xsb_bufs[dc % 2]
            dma("y", xc, xsrc[128 * dc:128 * dc + 128, t0:t0 + 512], (xsrc_b,), (xcb,),
                "xrl" + sfx)
            ev2, ev2_b = SB["evB"]
            vtt(ALU.mult, ev2, ev2_b, xc, xcb, Abc, Abc_b, "sr1" + sfx)
            vtt(ALU.add, ev2, ev2_b, ev2, ev2_b, Bbc, Bbc_b, "sr2" + sfx)
            dst = FA[:, 512 * dc:512 * dc + 512]
            vtt(ALU.add, dst, x1pre_bufs[dc], ev, ev_b, ev2, ev2_b, "x1p" + sfx)
            x1chunks.append((dst, x1pre_bufs[dc]))
        ln_rows(x1chunks, 512, D, (l1A, l1A_b), (l1B, l1B_b), "ln1" + sfx)
        Abc1, Abc1_b = bcast(l1A, l1A_b, 512, "pss1", 0, "l1a" + sfx)
        Bbc1, Bbc1_b = bcast(l1B, l1B_b, 512, "pss1", 512, "l1b" + sfx)
        for dc in range(8):
            cap, cb = x1chunks[dc]
            ev, ev_b = SB["evA"]
            vtt(ALU.mult, ev, ev_b, cap, cb, Abc1, Abc1_b, "x1a" + sfx)
            vtt(ALU.add, ev, ev_b, ev, ev_b, Bbc1, Bbc1_b, "x1b" + sfx)
            ev2, ev2_b = SB["evB"]
            sact(ev2, ev2_b, ev, ev_b, AF.Copy, "x1g" + sfx,
                 bias=b1_sb[:, dc:dc + 1], biasb=b1s_b,
                 scale=g1_sb[:, dc:dc + 1], scaleb=g1_b)
            dma("g", x1f[128 * dc:128 * dc + 128, t0:t0 + 512], ev2, (ev2_b,), (x1f_b,),
                "x1s" + sfx)
            vcopy(x1n_sl(dc, t0, 512), x1n_bufs[dc], ev2, ev2_b, "x1c" + sfx)

    if "dbg_x1" in c["debug"] and call == 0:
        dma("g", O["dbg_x1"][0], x1f, (x1f_b,), (O["dbg_x1"][1],), "dbg")

    if CUT == "b2":
        return
    gw_sb, gw_b = sbt_get("gw", [128, 64], BF16)
    gwt, gwt_b = SB["ldtmp"]
    dma("y", gwt[:, 0:64], I["gWr"][0], (I["gWr"][1],), (gwt_b,), "gwl")
    vcopy(gw_sb, gw_b, gwt[:, 0:64], gwt_b, "gwc")
    gbr, gbr_b = sbt_get("gbr", [1, 8])
    dma("y", gbr, I["gb"][0], (I["gb"][1],), (gbr_b,), "gbl")
    gbbc, gbbc_b = sbt_get("gbbc", [128, 8])
    pbc, pbc_b = bcast(gbr, gbr_b, 8, "pss0", 0, "gbb")
    sact(gbbc, gbbc_b, pbc, pbc_b, AF.Copy, "gbe")
    wT8, wT8_b = sbt_get("wT8bf", [8, TOK], BF16)
    sel8, sel8_b = sbt_get("sel8s", [8, 1024], BF16)
    dma("y", sel8, I["sel8"][0], (I["sel8"][1],), (sel8_b,), "sel8")
    g8, g8_b = sbt_get("g8", [128, 8])
    g8bv, g8b_b = sbt_get("g8b", [128, 8])
    g8c, g8c_b = sbt_get("g8c", [128, 8])
    gs, gs_b = sbt_get("gsml", [128, 4])

    for ts in range(16):
        pst, psb = PS["pw0"]
        ps = pst[:, 0:8]
        for kc in range(8):
            mm(ps, psb, x1n_sl(kc, 128 * ts, 128), gw_sb[:, 8 * kc:8 * kc + 8],
               (x1n_bufs[kc], gw_b), kc == 0, kc == 7, tag="gmm" + sfx)
        vtt(ALU.add, g8, g8_b, ps, psb, gbbc, gbbc_b, "gpre" + sfx)
        vrmax(gs[:, 0:1], gs_b, g8, g8_b, "gm" + sfx)
        vts(gs[:, 1:2], gs_b, gs[:, 0:1], gs_b, -1.0, None, ALU.mult, "gnm" + sfx)
        sact(g8bv, g8b_b, g8, g8_b, AF.Exp, "gex" + sfx, bias=gs[:, 1:2], biasb=gs_b,
             acc=gs[:, 2:3], accb=gs_b)
        vrecip(gs[:, 3:4], gs_b, gs[:, 2:3], gs_b, "grc" + sfx)
        vts(g8bv, g8b_b, g8bv, g8b_b, gs[:, 3:4], gs_b, ALU.mult, "gnr" + sfx)
        vrmax(gs[:, 0:1], gs_b, g8bv, g8b_b, "gm1" + sfx)
        vts(g8c, g8c_b, g8bv, g8b_b, gs[:, 0:1], gs_b, ALU.is_ge, "gge" + sfx)
        vstt(g8c, g8c_b, g8c, g8c_b, -1e30, None, g8bv, g8b_b, ALU.mult, ALU.add,
             "gmsk" + sfx)
        vrmax(gs[:, 1:2], gs_b, g8c, g8c_b, "gm2" + sfx)
        vts(g8c, g8c_b, g8bv, g8b_b, gs[:, 1:2], gs_b, ALU.is_ge, "gge2" + sfx)
        vtt(ALU.mult, g8, g8_b, g8bv, g8b_b, g8c, g8c_b, "gw2" + sfx)
        ptr, ptr_b = PS["pw1"]
        op("t", lambda t, ptr=ptr, g8=g8, idf=idf: t.transpose(
            ptr[0:8, 0:128], g8, idf),
           reads=(g8_b, idf_b), writes=(ptr_b,), tag="gtr" + sfx)
        sact(wT8[:, 128 * ts:128 * ts + 128], wT8_b, ptr[0:8, 0:128], ptr_b, AF.Copy,
             "gte" + sfx)

    if "dbg_w" in c["debug"] and call == 0:
        dma("g", O["dbg_w"][0], wT8, (wT8_b,), (O["dbg_w"][1],), "dbg")

    if CUT == "b3":
        return
    pg.barrier()

    orbf, orbf_b = SB["onesrow_bf"]
    def w1s(par):
        return AR_s[:, 8192 * par:8192 * par + 8192]
    def w2s(par):
        return AR_a[:, 8192 * par:8192 * par + 8192]
    hbf = AR_q[:, 0:8192]
    w1a, w1a_b = DR["dr_w1all"]; w2a, w2a_b = DR["dr_w2all"]
    SBH = c["SBH"]
    yq32 = SBH["AR_q"].bitcast(F32).ap()
    FAap, _ = SB["FA"]
    yacc_bufs = [pg.buf(f"ya{call}_{i}") for i in range(16)]
    def yacc_sl(idx):
        if idx < 8:
            return yq32[:, 4096 + 512 * idx: 4096 + 512 * idx + 512]
        return FAap[:, 512 * (idx - 8): 512 * (idx - 8) + 512]
    wbc, wbc_b = SB["bvbc"]
    b1row = SBH["ldtmp"].bitcast(BF16).ap()[0:1, 0:1024]
    b2bf = SB["P0"][0][0:8, 0:1024]
    b2bf_b = pg.buf(f"b2bf{call}")
    b2t, b2t_b = SB["ldtmp"]
    dma("y", b2t[0:8, 0:D], I["b2e"][0], (I["b2e"][1],), (b2t_b,), "b2l")
    vcopy(b2bf, b2bf_b, b2t[0:8, 0:D], b2t_b, "b2c")
    g2_sb, g2_b = sbt_get("g2s", [128, 8])
    bt2_sb, bt2_b = sbt_get("bt2s", [128, 8])
    dma("y", g2_sb, I["ln2g"][0], (I["ln2g"][1],), (g2_b,), "g2")
    dma("y", bt2_sb, I["ln2b"][0], (I["ln2b"][1],), (bt2_b,), "bt2")
    l2A, l2A_b = SB["rwA"]
    l2B, l2B_b = SB["rwB"]

    ydst, ydst_b = (DR["dr_y1"] if call == 0 else O["out_fm"])
    b1r_buf = pg.buf(f"b1r{call}")
    w1_bufs = [pg.buf(f"w1{call}_{i}") for i in range(2)]
    w2_bufs = [pg.buf(f"w2{call}_{i}") for i in range(2)]
    hb_bufs = [pg.buf(f"h{call}_{fr}") for fr in range(8)]

    def load_wexp(tp, e, half):
        i = (tp * 8 + e) * 2 + half
        par = i % 2
        for dc in range(8):
            dma("y", w1s(par)[:, 1024 * dc:1024 * dc + 1024],
                w1a[1024 * e + 128 * dc: 1024 * e + 128 * dc + 128,
                    1024 * half:1024 * half + 1024],
                (w1a_b,), (w1_bufs[par],), "w1l" + sfx)
        for fr in range(8):
            dma("s", w2s(par)[:, 1024 * fr:1024 * fr + 1024],
                w2a[FF * e + 128 * (8 * half + fr):
                    FF * e + 128 * (8 * half + fr) + 128, :],
                (w2a_b,), (w2_bufs[par],), "w2l" + sfx)

    load_wexp(0, 0, 0)
    for tp in range(2):
        tb0 = 1024 * tp
        for e in range(8):
            for j in range(2):
                pst0, pst0_b = PS["pss0"]
                pb = pst0[:, 0:512]
                mm(pb, pst0_b, sel8[:, 128 * e:128 * e + 128],
                   wT8[0:8, tb0 + 512 * j: tb0 + 512 * j + 512], (sel8_b, wT8_b),
                   True, True, tag="wbc" + sfx)
                sact(wbc[:, 512 * j:512 * j + 512], wbc_b, pb, pst0_b, AF.Copy,
                     "wbe" + sfx)
            for half in range(2):
                b1rb = b1r_buf
                dma("g", b1row, I["b1r"][0][0:1, FF * e + 1024 * half:
                                            FF * e + 1024 * half + 1024],
                    (I["b1r"][1], b2t_b), (b1rb,), "b1l" + sfx)
                i_seq = (tp * 8 + e) * 2 + half
                par = i_seq % 2
                w1_b, w2_b = w1_bufs[par], w2_bufs[par]
                if i_seq + 1 < 32:
                    nxt = i_seq + 1
                    load_wexp(nxt // 16, (nxt // 2) % 8, nxt % 2)
                for fr in range(8):
                    for th in range(2):
                        pst, psb = PS[pwn[(2 * fr + th) % 4]]
                        ps = pst[:, 0:512]
                        for dc in range(8):
                            mm(ps, psb,
                               w1s(par)[:, 1024 * dc + 128 * fr:
                                        1024 * dc + 128 * fr + 128],
                               x1n_sl(dc, tb0 + 512 * th, 512), (w1_b, x1n_bufs[dc]),
                               dc == 0, False, tag="hmm" + sfx)
                        mm(ps, psb, b1row[0:1, 128 * fr: 128 * fr + 128],
                           orbf[0:1, 0:512], (b1rb, orbf_b), False, True,
                           tag="hb1" + sfx)
                        dst = hbf[:, 1024 * fr + 512 * th: 1024 * fr + 512 * th + 512]
                        vstt(dst, hb_bufs[fr], ps, psb, 0.0, None,
                             wbc[:, 512 * th:512 * th + 512], wbc_b, ALU.max, ALU.mult,
                             "hev" + sfx)
                for dc in range(8):
                    for th in range(2):
                        pst, psb = PS[pwn[(2 * dc + th) % 2]]
                        ps = pst[:, 0:512]
                        for fr in range(8):
                            mm(ps, psb,
                               w2s(par)[:, 1024 * fr + 128 * dc:
                                        1024 * fr + 128 * dc + 128],
                               hbf[:, 1024 * fr + 512 * th: 1024 * fr + 512 * th + 512],
                               (w2_b, hb_bufs[fr]), fr == 0, fr == 7, tag="ymm" + sfx)
                        ysl = yacc_sl(2 * dc + th)
                        yb = yacc_bufs[2 * dc + th]
                        if e == 0 and half == 0:
                            vcopy(ysl, yb, ps, psb, "yc" + sfx)
                        else:
                            vtt(ALU.add, ysl, yb, ysl, yb, ps, psb, "ya" + sfx)
        for dc in range(8):
            for th in range(2):
                pst, psb = PS["pw2"]
                ps = pst[:, 0:512]
                mm(ps, psb, b2bf[:, 128 * dc:128 * dc + 128],
                   wT8[0:8, tb0 + 512 * th: tb0 + 512 * th + 512], (b2bf_b, wT8_b),
                   True, True, tag="b2m" + sfx)
                ysl = yacc_sl(2 * dc + th)
                yb = yacc_bufs[2 * dc + th]
                vtt(ALU.add, ysl, yb, ysl, yb, ps, psb, "yb2" + sfx)
        for th in range(2):
            t0 = tb0 + 512 * th
            chunks = []
            for dc in range(8):
                xc = x1sb[:, 512 * (dc % 2):512 * (dc % 2) + 512]
                xcb = x1sb_bufs[dc % 2]
                dma("g", xc, x1f[128 * dc:128 * dc + 128, t0:t0 + 512], (x1f_b,), (xcb,),
                    "x1r" + sfx)
                ysl = yacc_sl(2 * dc + th)
                yb = yacc_bufs[2 * dc + th]
                vtt(ALU.add, ysl, yb, xc, xcb, ysl, yb, "ffp" + sfx)
                chunks.append((ysl, yb))
            ln_rows(chunks, 512, D, (l2A, l2A_b), (l2B, l2B_b), "ln2" + sfx)
            Abc, Abc_b = bcast(l2A, l2A_b, 512, "pss1", 0, "l2a" + sfx)
            Bbc, Bbc_b = bcast(l2B, l2B_b, 512, "pss1", 512, "l2b" + sfx)
            for dc in range(8):
                cap, cb = chunks[dc]
                ev, ev_b = SB["evA"]
                vtt(ALU.mult, ev, ev_b, cap, cb, Abc, Abc_b, "f2a" + sfx)
                vtt(ALU.add, ev, ev_b, ev, ev_b, Bbc, Bbc_b, "f2b" + sfx)
                ev2, ev2_b = SB["evB"]
                sact(ev2, ev2_b, ev, ev_b, AF.Copy, "f2g" + sfx,
                     bias=bt2_sb[:, dc:dc + 1], biasb=bt2_b,
                     scale=g2_sb[:, dc:dc + 1], scaleb=g2_b)
                xc = xsb[:, 0:512]
                xcb = xsb_bufs[0]
                dma("g", xc, xsrc[128 * dc:128 * dc + 128, t0:t0 + 512], (xsrc_b,), (xcb,),
                    "xr2" + sfx)
                ev3, ev3_b = SB["sq0"]
                vstt(ev3, ev3_b, ev2, ev2_b, 0.5, None, xc, xcb, ALU.mult, ALU.add,
                     "yf" + sfx)
                dma("g", ydst[128 * dc:128 * dc + 128, t0:t0 + 512], ev3, (ev3_b,),
                    (ydst_b,), "yst" + sfx)

    if call == 0:
        pg.barrier()
        moe_tail_call0(c, sfx)


def moe_tail_call0(c, sfx):
    nc, pg, op, dma, mm = c["nc"], c["pg"], c["op"], c["dma"], c["mm"]
    I, O, DR, SB, PS = c["I"], c["O"], c["DR"], c["SB"], c["PS"]
    ln_rows, bcast, sbt_get = c["ln_rows"], c["bcast"], c["sbt_get"]
    vtt, vcopy, sact = c["vtt"], c["vcopy"], c["sact"]
    oc, oc_b = SB["onescol"]
    y1, y1_b = DR["dr_y1"]
    hin, hin_b = DR["dr_hin"]

    wproj, _ = SB["wproj"]
    AR_a, _ = SB["AR_a"]
    dw_b = pg.buf("dwb")
    dwt, dwt_b = SB["ldtmp"]
    for kc in range(8):
        dma("y", dwt[:, 0:256], I["down_w"][0][128 * kc:128 * kc + 128, :],
            (I["down_w"][1],), (dwt_b,), "dwl")
        vcopy(wproj[:, 256 * kc:256 * kc + 256], dw_b, dwt[:, 0:256], dwt_b, "dwc")
    db_sb, db_b = sbt_get("dbs", [128, 2])
    dma("y", db_sb, I["down_b"][0], (I["down_b"][1],), (db_b,), "dbl")

    FA, _ = SB["FA"]
    AR_x, _ = SB["AR_x"]
    lyA, lyA_b = c["sbt_get"]("rwA", [1, 512])
    lyB, lyB_b = c["sbt_get"]("rwB", [1, 512])
    hev = AR_a[:, 0:512]
    hev_b = pg.buf("hevb")
    pwn = ["pw0", "pw1", "pw2", "pw3"]
    y1n_bufs = [pg.buf(f"y1n_{i}") for i in range(8)]
    y1l_bufs = [pg.buf(f"y1l_{i}") for i in range(8)]
    for tc in range(4):
        t0 = 512 * tc
        chunks = []
        for kc in range(8):
            fa_sl = FA[:, 512 * kc:512 * kc + 512]
            fb = y1l_bufs[kc]
            dma("y", fa_sl, y1[128 * kc:128 * kc + 128, t0:t0 + 512], (y1_b,), (fb,), "y1l")
            chunks.append((fa_sl, fb))
        ln_rows(chunks, 512, D, (lyA, lyA_b), (lyB, lyB_b), "lny" + sfx)
        Abc, Abc_b = bcast(lyA, lyA_b, 512, "pss0", 0, "lya")
        Bbc, Bbc_b = bcast(lyB, lyB_b, 512, "pss1", 0, "lyb")
        ev, ev_b = SB["evA"]
        for kc in range(8):
            cap, cb = chunks[kc]
            vtt(ALU.mult, ev, ev_b, cap, cb, Abc, Abc_b, "lya2")
            vtt(ALU.add, AR_x[:, 2048 * kc + t0:2048 * kc + t0 + 512], y1n_bufs[kc],
                ev, ev_b, Bbc, Bbc_b, "lyb2")
        for oc2 in range(2):
            pst, psb = PS[pwn[oc2]]
            ps = pst[:, 0:512]
            for kc in range(8):
                mm(ps, psb, wproj[:, 256 * kc + 128 * oc2: 256 * kc + 128 * oc2 + 128],
                   AR_x[:, 2048 * kc + t0:2048 * kc + t0 + 512], (dw_b, y1n_bufs[kc]),
                   kc == 0, kc == 7, tag="dmm")
            dst = hev
            sact(dst, hev_b, ps, psb, AF.Silu, "dsl",
                 bias=db_sb[:, oc2:oc2 + 1], biasb=db_b)
            dma("g", hin[128 * oc2:128 * oc2 + 128, t0:t0 + 512], dst, (hev_b,), (hin_b,),
                "dst")
    hina, hina_b = DR["dr_hin_all"]
    c["cc_barrier"]([[0, 1, 2, 3], [4, 5, 6, 7]], (hin_b,), "hin")
    pg.cc(lambda g, a=hin, b=hina: g.collective_compute(
        "AllGather", ALU.bypass, replica_groups=[[0, 1, 2, 3], [4, 5, 6, 7]],
        ins=[a], outs=[b]), reads=(hin_b,), writes=(hina_b,), tag="ag_hin")
    if "dbg_y1" in c["debug"]:
        dma("g", O["dbg_y1"][0], y1, (y1_b,), (O["dbg_y1"][1],), "dbg")
    if "dbg_hinall" in c["debug"]:
        dma("g", O["dbg_hinall"][0], hina, (hina_b,), (O["dbg_hinall"][1],), "dbg")


def ttt_prestage(c):
    nc, pg, op, dma, mm = c["nc"], c["pg"], c["op"], c["dma"], c["mm"]
    I, O, DR, SB, PS, SBT = c["I"], c["O"], c["DR"], c["SB"], c["PS"], c["SBT"]
    sbt_get = c["carve"]
    vtt, vts, vstt, vcopy = c["vtt"], c["vts"], c["vstt"], c["vcopy"]
    vrecip, vmemset, sact = c["vrecip"], c["vmemset"], c["sact"]
    oc, oc_b = SB["onescol"]
    orow, orow_b = SB["onesrow"]
    AR_q, _ = SB["AR_q"]; AR_s, _ = SB["AR_s"]; AR_a, _ = SB["AR_a"]
    FA, _ = SB["FA"]
    evA, evA_b = SB["evA"]; evB, evB_b = SB["evB"]
    sq0, sq0_b = SB["sq0"]; sq1, sq1_b = SB["sq1"]
    pwn = ["pw0", "pw1", "pw2", "pw3"]

    hin_sb = AR_q
    hin_b = pg.buf("hin_sb")
    hina, hina_b = DR["dr_hin_all"]
    for r in range(4):
        for c2 in range(2):
            dma("y", hin_sb[:, 2048 * (2 * r + c2): 2048 * (2 * r + c2) + 2048],
                hina[256 * r + 128 * c2: 256 * r + 128 * c2 + 128, :],
                (hina_b,), (hin_b,), "hld")

    def hin_sl(c2, t0, w):
        r = t0 // 2048
        return hin_sb[:, 2048 * (2 * r + c2) + (t0 % 2048): 2048 * (2 * r + c2) + (t0 % 2048) + w]

    sw = {}
    for nm in ["scan_wq", "scan_wk", "scan_wv"]:
        t, tb = sbt_get(nm + "b", [128, 256], BF16)
        dma("g", t, I[nm][0].rearrange("(cc p) f -> p cc f", p=128), (I[nm][1],),
            (tb,), nm)
        sw[nm] = (t, tb)

    srow, srow_b = sbt_get("sgrow", [1, 384])
    dma("s", srow[:, 0:128], I["scan_g"][0], (I["scan_g"][1],), (srow_b,), "sg")
    dma("s", srow[:, 128:256], I["scan_gb"][0], (I["scan_gb"][1],), (srow_b,), "sgb")
    vtt(ALU.mult, srow[:, 256:384], srow_b, srow[:, 0:128], srow_b,
        srow[:, 0:128], srow_b, "g2r")
    gco, gco_b = sbt_get("gconst", [128, 384])
    for i in range(3):
        pb, pb_b = PS["pss0"]
        mm(pb[:, 0:128], pb_b, orow, srow[:, 128 * i:128 * i + 128], (orow_b, srow_b),
           True, True, tag="gbc")
        sact(gco[:, 128 * i:128 * i + 128], gco_b, pb[:, 0:128], pb_b, AF.Copy, "gbe")

    XQ = AR_s[:, 0:8192]; XK = AR_s[:, 8192:16384]
    XQ_b = pg.buf("XQfm"); XK_b = pg.buf("XKfm")
    for (w_nm, dst, dstb) in [("scan_wq", XQ, XQ_b), ("scan_wk", XK, XK_b)]:
        wt, wtb = sw[w_nm]
        for tt in range(16):
            pst, psb = PS[pwn[tt % 4]]
            ps = pst[:, 0:512]
            for c2 in range(2):
                mm(ps, psb, wt[:, 128 * c2:128 * c2 + 128], hin_sl(c2, 512 * tt, 512),
                   (wtb, hin_b), c2 == 0, c2 == 1, tag="xfm")
            sact(dst[:, 512 * tt:512 * tt + 512], dstb, ps, psb, AF.Copy, "xfme")

    xkt_d, xkt_db = DR["dr_xktok"]
    tn_d, tn_db = DR["dr_tn"]
    pn_d, pn_db = DR["dr_pn"]
    for ts in range(64):
        t0 = 128 * ts
        outs = {}
        for w_nm, ev, evb in [("scan_wk", evA, evA_b), ("scan_wv", evB, evB_b)]:
            wt, wtb = sw[w_nm]
            pst, psb = PS[pwn[ts % 2]]
            ps = pst[:, 0:128]
            for c2 in range(2):
                mm(ps, psb, hin_sl(c2, t0, 128), wt[:, 128 * c2:128 * c2 + 128],
                   (hin_b, wtb), c2 == 0, c2 == 1, tag="xtk")
            sact(ev[:, 0:128], evb, ps, psb, AF.Copy, "xtke")
        xkbf, xkbf_b = sbt_get("xkbf", [128, 128], BF16)
        vcopy(xkbf, xkbf_b, evA[:, 0:128], evA_b, "xkb")
        dma("g", xkt_d[t0:t0 + 128, :], xkbf, (xkbf_b,), (xkt_db,), "xks")
        vtt(ALU.subtract, sq0[:, 0:128], sq0_b, evA[:, 0:128], evA_b, evB[:, 0:128],
            evB_b, "tn1")
        vtt(ALU.add, sq0[:, 0:128], sq0_b, sq0[:, 0:128], sq0_b, gco[:, 128:256],
            gco_b, "tn2")
        vtt(ALU.mult, sq0[:, 0:128], sq0_b, sq0[:, 0:128], sq0_b, gco[:, 0:128],
            gco_b, "tn3")
        dma("g", tn_d[t0:t0 + 128, :], sq0[:, 0:128], (sq0_b,), (tn_db,), "tns")
        wt, wtb = sw["scan_wq"]
        pst, psb = PS[pwn[2 + ts % 2]]
        ps = pst[:, 0:128]
        for c2 in range(2):
            mm(ps, psb, hin_sl(c2, t0, 128), wt[:, 128 * c2:128 * c2 + 128],
               (hin_b, wtb), c2 == 0, c2 == 1, tag="xqt")
        sact(evA[:, 0:128], evA_b, ps, psb, AF.Copy, "xqte")
        vtt(ALU.add, sq1[:, 0:128], sq1_b, evA[:, 0:128], evA_b, gco[:, 128:256],
            gco_b, "pn1")
        dma("g", pn_d[t0:t0 + 128, :], sq1[:, 0:128], (sq1_b,), (pn_db,), "pns")

    ti, ti_b = sbt_get("sti", [1, 16])
    dma("s", ti, I["rarange"][0], (I["rarange"][1],), (ti_b,), "rar")
    tix, tix_b = sbt_get("stix", [1, 16])
    dma("s", tix, I["tok_idx"][0], (I["tok_idx"][1],), (tix_b,), "tix")
    vtt(ALU.add, ti, ti_b, ti, ti_b, tix, tix_b, "ti1")
    vts(ti, ti_b, ti, ti_b, 0.0, None, ALU.max, "ti2")
    tid, tid_b = sbt_get("stid", [1, 16])
    vts(tid, tid_b, ti, ti_b, 1.0 / 128.0, None, ALU.mult, "tid")
    lrb_r, lrb_rb = sbt_get("slrb", [1, 1])
    dma("s", lrb_r, I["scan_lrb"][0], (I["scan_lrb"][1],), (lrb_rb,), "lrb")
    pst, psb = PS["pss1"]
    mm(pst[0:16, 0:1], psb, orow[:, 0:16], tid[:, 15:16], (orow_b, tid_b), True, True,
       tag="t15b")
    ti15c, ti15c_b = sbt_get("sti15", [16, 1])
    sact(ti15c, ti15c_b, pst[0:16, 0:1], psb, AF.Copy, "t15e")
    mm(pst[0:16, 1:2], psb, orow[:, 0:16], lrb_r, (orow_b, lrb_rb), True, True,
       tag="lrbb")
    lrb16, lrb16_b = sbt_get("slrb16", [16, 1])
    sact(lrb16, lrb16_b, pst[0:16, 1:2], psb, AF.Copy, "lrbe")
    mm(pst[0:16, 16:32], psb, orow[:, 0:16], tid, (orow_b, tid_b), True, True,
       tag="tidr")
    tidrep, tidrep_b = sbt_get("stidrep", [16, 16])
    sact(tidrep, tidrep_b, pst[0:16, 16:32], psb, AF.Copy, "tidre")

    lrw, lrw_b = sbt_get("slrw", [128, 2], BF16)
    dma("g", lrw, I["scan_lrw"][0], (I["scan_lrw"][1],), (lrw_b,), "lrw")
    pls, pls_b = PS["pss0"]
    for n in range(512):
        for c2 in range(2):
            mm(pls[0:16, n % 512: n % 512 + 1], pls_b,
               hin_sl(c2, 16 * n, 16), lrw[:, c2:c2 + 1], (hin_b, lrw_b),
               c2 == 0, c2 == 1, tag="lrm")
    lr_sm, lr_sm_b = sbt_get("slrsm", [16, 512])
    sact(lr_sm, lr_sm_b, pls[0:16, 0:512], pls_b, AF.Sigmoid, "lre",
         bias=lrb16, biasb=lrb16_b)
    le_sm, le_sm_b = sbt_get("slesm", [16, 512])
    vts(le_sm, le_sm_b, lr_sm, lr_sm_b, ti15c, ti15c_b, ALU.mult, "lesm")
    le_bf, le_bf_b = sbt_get("slesmb", [16, 512], BF16)
    vcopy(le_bf, le_bf_b, le_sm, le_sm_b, "lesb")

    g12t = AR_a[0:16, 0:8192]
    g12t_b = pg.buf("g12t")
    mk = evA[0:16, 0:512]
    mk_b = pg.buf("smk")
    for g in range(16):
        pA, pA_b = PS["pss1"]
        for j in range(32):
            n = 32 * g + j
            mm(pA[0:16, 16 * j:16 * j + 16], pA_b, XK[:, 16 * n:16 * n + 16],
               XQ[:, 16 * n:16 * n + 16], (XK_b, XQ_b), True, True, tag="atT")
        aT = sq0[0:16, 0:512]
        sact(aT, sq0_b, pA[0:16, 0:512], pA_b, AF.Copy, "atTe")
        for j in range(32):
            n = 32 * g + j
            vts(sq1[0:16, 16 * j:16 * j + 16], sq1_b, tidrep, tidrep_b,
                lr_sm[:, n:n + 1], lr_sm_b, ALU.mult, "etaT")
        vts(aT, sq0_b, aT, sq0_b, 1.0, None, ALU.add, "g12a")
        vtt(ALU.mult, sq1[0:16, 0:512], sq1_b, sq1[0:16, 0:512], sq1_b, aT, sq0_b,
            "g12b")
        dma("s", mk, I["masku"][0][:, 512 * g:512 * g + 512], (I["masku"][1], evA_b),
            (mk_b,), "mkl")
        vtt(ALU.mult, g12t[:, 512 * g:512 * g + 512], g12t_b, sq1[0:16, 0:512], sq1_b,
            mk, mk_b, "g12c")

    st = {"XQ": (XQ, XQ_b), "XK": (XK, XK_b), "g12t": (g12t, g12t_b),
          "lr_sm": (lr_sm, lr_sm_b), "le_sm": (le_sm, le_sm_b),
          "le_bf": (le_bf, le_bf_b), "gco": (gco, gco_b)}
    c["scan_state"] = st

    if "dbg_xq" in c["debug"]:
        dma("g", O["dbg_xq"][0], XQ, (XQ_b,), (O["dbg_xq"][1],), "dbg")
    if "dbg_xkt" in c["debug"]:
        dma("g", O["dbg_xkt"][0], xkt_d, (xkt_db,), (O["dbg_xkt"][1],), "dbg")
    if "dbg_tn" in c["debug"]:
        dma("g", O["dbg_tn"][0], tn_d, (tn_db,), (O["dbg_tn"][1],), "dbg")
    if "dbg_pn" in c["debug"]:
        dma("g", O["dbg_pn"][0], pn_d, (pn_db,), (O["dbg_pn"][1],), "dbg")
    if "dbg_g12" in c["debug"]:
        dma("g", O["dbg_g12"][0], g12t, (g12t_b,), (O["dbg_g12"][1],), "dbg")
    if "dbg_le" in c["debug"]:
        dma("g", O["dbg_le"][0], le_sm, (le_sm_b,), (O["dbg_le"][1],), "dbg")


def ttt_scan(c):
    nc, pg, op, dma, mm = c["nc"], c["pg"], c["op"], c["dma"], c["mm"]
    I, O, DR, SB, PS, SBT = c["I"], c["O"], c["DR"], c["SB"], c["PS"], c["SBT"]
    sbt_get = c["carve"]
    vtt, vts, vstt, vcopy = c["vtt"], c["vts"], c["vstt"], c["vcopy"]
    vrecip, vmemset, sact = c["vrecip"], c["vmemset"], c["sact"]
    st = c["scan_state"]
    XQ, XQ_b = st["XQ"]; XK, XK_b = st["XK"]
    g12t, g12t_b = st["g12t"]
    le_sm, le_sm_b = st["le_sm"]
    le_bf, le_bf_b = st["le_bf"]
    gco, gco_b = st["gco"]
    orbf, orbf_b = SB["onesrow_bf"]
    eps6r, eps6r_b = SB["eps6r"]
    AR_a, _ = SB["AR_a"]
    FA, _ = SB["FA"]

    W1, W1_b = sbt_get("sW1", [128, 128])
    W1bf, W1bf_b = sbt_get("sW1b", [128, 256], BF16)
    b1, b1_b = sbt_get("sb1", [1, 128])
    b1bf, b1bf_b = sbt_get("sb1b", [1, 256], BF16)
    st6, st6_b = sbt_get("sst6", [16, 6])
    mv, mv_b = sbt_get("smv", [16, 2])
    ssm, ssm_b = sbt_get("sssm", [16, 12])
    sxh, sxh_b = sbt_get("ssxh", [16, 128])
    sgy, sgy_b = sbt_get("ssgy", [16, 128])
    sdp, sdp_b = sbt_get("ssdp", [16, 128])
    sgr, sgr_b = sbt_get("ssgr", [16, 128], BF16)
    slk, slk_b = sbt_get("sslk", [16, 128], BF16)
    sot, sot_b = sbt_get("ssot", [16, 256])
    sot_bufs = [pg.buf(f"sot{i}") for i in range(2)]

    dma("y", W1, I["scan_W1"][0], (I["scan_W1"][1],), (W1_b,), "w1i")
    dma("y", b1, I["scan_b1"][0], (I["scan_b1"][1],), (b1_b,), "b1i")
    vcopy(W1bf[:, 0:128], W1bf_b, W1, W1_b, "w1bi")
    vcopy(b1bf[:, 0:128], b1bf_b, b1, b1_b, "b1bi")
    vcopy(W1bf[:, 128:256], W1bf_b, W1, W1_b, "w1bi2")
    vcopy(b1bf[:, 128:256], b1bf_b, b1, b1_b, "b1bi2")
    vts(le_sm, le_sm_b, le_sm, le_sm_b, -1.0, None, ALU.mult, "snle")
    corrT, corrT_b = sbt_get("scorT", [16, 16], BF16)

    xkt_d, xkt_db = DR["dr_xktok"]
    tn_d, tn_db = DR["dr_tn"]
    pn_d, pn_db = DR["dr_pn"]
    out_d, out_db = DR["dr_out"]
    xkc = AR_a[0:16, 8192:10240]
    xkc_bufs = [pg.buf(f"xkc{i}") for i in range(16)]
    tnc = FA[0:16, 0:2048]
    tnc_bufs = [pg.buf(f"tnc{i}") for i in range(16)]
    pnc = FA[0:16, 2048:4096]
    pnc_bufs = [pg.buf(f"pnc{i}") for i in range(16)]

    def load_step(m):
        s = m % 16
        dma("y", xkc[:, 128 * s:128 * s + 128], xkt_d[16 * m:16 * m + 16, :],
            (xkt_db,), (xkc_bufs[s],), "xkcl")
        dma("s", tnc[:, 128 * s:128 * s + 128], tn_d[16 * m:16 * m + 16, :],
            (tn_db,), (tnc_bufs[s],), "tncl")
        dma("g", pnc[:, 128 * s:128 * s + 128], pn_d[16 * m:16 * m + 16, :],
            (pn_db,), (pnc_bufs[s],), "pncl")

    for m in range(8):
        load_step(m)

    pz_t, pz_b = PS["pw0"]; pz = pz_t[0:16, 0:128]
    pz2_t, pz2_b = PS["pw1"]; pz2 = pz2_t[0:16, 0:128]
    pwd_t, pwd_b = PS["pw2"]; pwd = pwd_t[:, 0:128]
    pbd_t, pbd_b = PS["pw3"]; pbd = pbd_t[0:1, 0:128]

    pcor_t, pcor_b = PS["pss0"]
    pcor = pcor_t[0:16, 0:16]

    for n in range(512):
        p = n % 2
        s = n % 16
        sl = slice(128 * s, 128 * s + 128)
        xk_sl, xk_slb = xkc[:, sl], xkc_bufs[s]
        tn_sl, tn_slb = tnc[:, sl], tnc_bufs[s]
        pn_sl, pn_slb = pnc[:, sl], pnc_bufs[s]
        mm(pz, pz_b, XK[:, 16 * n:16 * n + 16],
           W1bf[:, 128 * (1 - p):128 * (1 - p) + 128], (XK_b, W1bf_b), True, False,
           tag="sz1")
        mm(pz, pz_b, orbf[0:1, 0:16], b1bf[0:1, 128 * (1 - p):128 * (1 - p) + 128],
           (orbf_b, b1bf_b), False, n == 0, tag="sz1b")
        if n >= 1:
            mm(pz, pz_b, corrT, sgr, (corrT_b, sgr_b), False, True, tag="scor")
        op("v", lambda v, pz=pz: v.bn_stats(st6, pz), reads=(pz_b,), writes=(st6_b,),
           tag="sbn")
        op("v", lambda v: v.bn_aggr(mv, st6), reads=(st6_b,), writes=(mv_b,), tag="sba")
        sact(ssm[:, 0:1], ssm_b, mv[:, 1:2], mv_b, AF.Sqrt, "sstd",
             bias=eps6r[0:16, :], biasb=eps6r_b)
        vrecip(ssm[:, 1:2], ssm_b, ssm[:, 0:1], ssm_b, "srst")
        vts(ssm[:, 2:3], ssm_b, mv[:, 0:1], mv_b, ssm[:, 1:2], ssm_b, ALU.mult,
            "snmr", s2=-1.0, s2b=None, op1=ALU.mult)
        vts(ssm[:, 5:6], ssm_b, ssm[:, 1:2], ssm_b, 1.0 / 128.0, None, ALU.mult, "srd")
        sact(sxh, sxh_b, pz, pz_b, AF.Copy, "sxh",
             scale=ssm[:, 1:2], scaleb=ssm_b, bias=ssm[:, 2:3], biasb=ssm_b)
        if n + 1 <= 511:
            mm(pcor, pcor_b, XK[:, 16 * n:16 * n + 16],
               XK[:, 16 * n + 16:16 * n + 32], (XK_b,), True, False, tag="spP")
            mm(pcor, pcor_b, orbf[0:1, 0:16], orbf[0:1, 0:16],
               (orbf_b,), False, True, tag="spP1")
            sact(corrT, corrT_b, pcor, pcor_b, AF.Copy, "scev",
                 scale=le_sm[:, n:n + 1], scaleb=le_sm_b)
        vtt(ALU.mult, sgy, sgy_b, sxh, sxh_b, gco[0:16, 256:384], gco_b, "sgy1")
        vstt(sgy, sgy_b, sgy, sgy_b, 1.0, None, tn_sl, tn_slb, ALU.mult, ALU.add,
             "sgy2", acc=ssm[:, 3:4], accb=ssm_b)
        vstt(sdp, sdp_b, sgy, sgy_b, 1.0, None, sxh, sxh_b, ALU.mult, ALU.mult,
             "ss2", acc=ssm[:, 4:5], accb=ssm_b)
        vts(ssm[:, 6:8], ssm_b, ssm[:, 3:5], ssm_b, ssm[:, 5:6], ssm_b, ALU.mult,
            "sa23", s2=-1.0, s2b=None, op1=ALU.mult)
        vts(sgy, sgy_b, sgy, sgy_b, ssm[:, 1:2], ssm_b, ALU.mult, "sq1",
            s2=ssm[:, 6:7], s2b=ssm_b, op1=ALU.add)
        vstt(sgr, sgr_b, sxh, sxh_b, ssm[:, 7:8], ssm_b, sgy, sgy_b, ALU.mult,
             ALU.add, "sgrad")
        vts(slk, slk_b, xk_sl, xk_slb, le_sm[:, n:n + 1], le_sm_b, ALU.mult, "slex")
        mm(pwd, pwd_b, slk, sgr, (slk_b, sgr_b), True, True, tag="swd")
        mm(pbd, pbd_b, le_bf[:, n:n + 1], sgr, (le_bf_b, sgr_b), True, True, tag="sbd")
        mm(pz2, pz2_b, XQ[:, 16 * n:16 * n + 16], W1bf[:, 128 * p:128 * p + 128],
           (XQ_b, W1bf_b), True, False, tag="szb")
        mm(pz2, pz2_b, g12t[0:16, 16 * n:16 * n + 16], sgr, (g12t_b, sgr_b),
           False, False, tag="szbg")
        mm(pz2, pz2_b, orbf[0:1, 0:16], b1bf[0:1, 128 * p:128 * p + 128],
           (orbf_b, b1bf_b), False, True, tag="szbb")
        vtt(ALU.add, W1, W1_b, W1, W1_b, pwd, pwd_b, "sw1u")
        pq = 1 - p
        sact(W1bf[:, 128 * pq:128 * pq + 128], W1bf_b, W1, W1_b, AF.Copy, "sw1c")
        vtt(ALU.subtract, b1, b1_b, b1, b1_b, pbd, pbd_b, "sb1u")
        sact(b1bf[0:1, 128 * pq:128 * pq + 128], b1bf_b, b1, b1_b, AF.Copy, "sb1c")
        op("v", lambda v, pz2=pz2: v.bn_stats(st6, pz2), reads=(pz2_b,), writes=(st6_b,),
           tag="obn")
        op("v", lambda v: v.bn_aggr(mv, st6), reads=(st6_b,), writes=(mv_b,), tag="oba")
        sact(ssm[:, 8:9], ssm_b, mv[:, 1:2], mv_b, AF.Sqrt, "ostd",
             bias=eps6r[0:16, :], biasb=eps6r_b)
        vrecip(ssm[:, 9:10], ssm_b, ssm[:, 8:9], ssm_b, "orst")
        vts(ssm[:, 10:11], ssm_b, mv[:, 0:1], mv_b, ssm[:, 9:10], ssm_b, ALU.mult,
            "onmr", s2=-1.0, s2b=None, op1=ALU.mult)
        sact(sxh, sxh_b, pz2, pz2_b, AF.Copy, "oxh",
             scale=ssm[:, 9:10], scaleb=ssm_b, bias=ssm[:, 10:11], biasb=ssm_b)
        vtt(ALU.mult, sgy, sgy_b, sxh, sxh_b, gco[0:16, 0:128], gco_b, "oo1",
            eng="g")
        osl = sot[:, 128 * (n % 2):128 * (n % 2) + 128]
        op("g", lambda g, osl=osl, sgy=sgy, pn_sl=pn_sl: g.tensor_add(osl, sgy, pn_sl),
           reads=(sgy_b, pn_slb), writes=(sot_bufs[n % 2],), tag="oo2")
        dma("g", out_d[16 * n:16 * n + 16, :], osl, (sot_bufs[n % 2],), (out_db,),
            "osto")
        if n + 8 < 512:
            load_step(n + 8)

    if "dbg_scanout" in c["debug"]:
        dma("g", O["dbg_scanout"][0], out_d, (out_db,), (O["dbg_scanout"][1],), "dbg")


def post_ttt(c):
    nc, pg, op, dma, mm = c["nc"], c["pg"], c["op"], c["dma"], c["mm"]
    I, O, DR, SB, PS, SBT = c["I"], c["O"], c["DR"], c["SB"], c["PS"], c["SBT"]
    sbt_get = c["carve"]
    ln_rows, bcast = c["ln_rows"], c["bcast"]
    vtt, vts, vstt, vcopy = c["vtt"], c["vts"], c["vstt"], c["vcopy"]
    vrecip, vmemset, sact = c["vrecip"], c["vmemset"], c["sact"]
    oc, oc_b = SB["onescol"]
    AR_q, _ = SB["AR_q"]
    FA, _ = SB["FA"]
    evA, evA_b = SB["evA"]; evB, evB_b = SB["evB"]
    sq0, sq0_b = SB["sq0"]
    idf, idf_b = SB["ident_f"]
    wproj, _ = SB["wproj"]
    pwn = ["pw0", "pw1", "pw2", "pw3"]

    out_d, out_db = DR["dr_out"]
    a2a, a2a_b = DR["dr_out_all"]
    grp = [[0, 1, 2, 3], [4, 5, 6, 7]]
    c["cc_barrier"](grp, (out_db,), "oag")
    pg.cc(lambda g, a=out_d, b=a2a, grp=grp: g.collective_compute(
        "AllGather", ALU.bypass, replica_groups=grp, ins=[a], outs=[b]),
        reads=(out_db,), writes=(a2a_b,), tag="oag")

    oidx, oidx_b = sbt_get("soidx", [128, 32], mybir.dt.int32)
    dma("s", oidx, I["oidx"][0], (I["oidx"][1],), (oidx_b,), "oidx")
    mix_bufs = [pg.buf(f"mix{i}") for i in range(2)]
    ldb = pg.buf("postld")
    from concourse.bass import IndirectOffsetOnAxis
    for h in range(2):
        for tt in range(16):
            j = 16 * h + tt
            pg.dma("g", lambda e, j=j: e.indirect_dma_start(
                evA[:, 0:128], None, a2a[0:128, :],
                IndirectOffsetOnAxis(oidx[:, j:j + 1], 0)),
                reads=(a2a_b, oidx_b), writes=(ldb,), tag="pgth")
            pst, psb = PS[pwn[tt % 2]]
            op("t", lambda t, pst=pst: t.transpose(pst[:, 0:128], evA[:, 0:128], idf),
               reads=(ldb, idf_b), writes=(psb,), tag="ptr")
            sact(FA[:, 2048 * h + 128 * tt: 2048 * h + 128 * tt + 128], mix_bufs[h],
                 pst[:, 0:128], psb, AF.Copy, "ptre")

    if "dbg_p2" in c["debug"]:
        dma("g", O["dbg_p2"][0], FA[:, 0:2048], (mix_bufs[0],), (O["dbg_p2"][1],), "dbg")
    if "dbg_p3" in c["debug"]:
        dma("g", O["dbg_p3"][0], FA[:, 2048:4096], (mix_bufs[1],), (O["dbg_p3"][1],), "dbg")
    pw_sb, pw_b = sbt_get("postws", [128, 2])
    pb_sb, pb_b = sbt_get("postbs", [128, 2])
    dma("s", pw_sb, I["post_w"][0], (I["post_w"][1],), (pw_b,), "pw")
    dma("s", pb_sb, I["post_b"][0], (I["post_b"][1],), (pb_b,), "pb")
    lw_sb, lw_b = sbt_get("lntws", [128, 2])
    lb_sb, lb_b = sbt_get("lntbs", [128, 2])
    dma("s", lw_sb, I["lnt_w"][0], (I["lnt_w"][1],), (lw_b,), "lw")
    dma("s", lb_sb, I["lnt_b"][0], (I["lnt_b"][1],), (lb_b,), "lb")
    rwA, rwA_b = SB["rwA"]; rwB, rwB_b = SB["rwB"]
    postn_bufs = [pg.buf(f"postn{i}") for i in range(2)]
    wo32_bufs = mix_bufs
    lnout_bufs = [pg.buf(f"lnout{i}") for i in range(2)]

    for tc in range(4):
        t0 = 512 * tc
        chunks = [(FA[:, 2048 * ch + t0: 2048 * ch + t0 + 512], mix_bufs[ch])
                  for ch in range(2)]
        ln_rows(chunks, 512, 256, (rwA, rwA_b), (rwB, rwB_b), "pln")
        Abc, Abc_b = bcast(rwA, rwA_b, 512, "pss0", 0, "plA")
        Bbc, Bbc_b = bcast(rwB, rwB_b, 512, "pss0", 512, "plB")
        for ch in range(2):
            cap, cb = chunks[ch]
            vtt(ALU.mult, evB, evB_b, cap, cb, Abc, Abc_b, "pl1")
            vtt(ALU.add, evB, evB_b, evB, evB_b, Bbc, Bbc_b, "pl2")
            sact(AR_q[:, 2048 * ch + t0: 2048 * ch + t0 + 512], postn_bufs[ch],
                 evB, evB_b, AF.Copy, "pl3", bias=pb_sb[:, ch:ch + 1], biasb=pb_b,
                 scale=pw_sb[:, ch:ch + 1], scaleb=pw_b)

    if "dbg_p4" in c["debug"]:
        dma("g", O["dbg_p4"][0], AR_q[:, 0:4096],
            (postn_bufs[0], postn_bufs[1]), (O["dbg_p4"][1],), "dbg")
    wot_b = pg.buf("wotb")
    dma("g", wproj[:, 0:512], I["wo_tr"][0], (I["wo_tr"][1],), (wot_b,), "wot")
    for oc2 in range(2):
        for tc in range(4):
            t0 = 512 * tc
            pst, psb = PS[pwn[tc % 2]]
            ps = pst[:, 0:512]
            for kc in range(2):
                mm(ps, psb, wproj[:, 256 * kc + 128 * oc2: 256 * kc + 128 * oc2 + 128],
                   AR_q[:, 2048 * kc + t0: 2048 * kc + t0 + 512],
                   (wot_b, postn_bufs[kc]), kc == 0, kc == 1, tag="wom")
            sact(FA[:, 2048 * oc2 + t0: 2048 * oc2 + t0 + 512], wo32_bufs[oc2],
                 ps, psb, AF.Copy, "woe")
    if "dbg_p5" in c["debug"]:
        dma("g", O["dbg_p5"][0], FA[:, 0:2048], (wo32_bufs[0],), (O["dbg_p5"][1],), "dbg")
        dma("g", O["dbg_p6"][0], FA[:, 2048:4096], (wo32_bufs[1],), (O["dbg_p6"][1],), "dbg")
    for tc in range(4):
        t0 = 512 * tc
        chunks = [(FA[:, 2048 * ch + t0: 2048 * ch + t0 + 512], wo32_bufs[ch])
                  for ch in range(2)]
        ln_rows(chunks, 512, 256, (rwA, rwA_b), (rwB, rwB_b), "tln")
        Abc, Abc_b = bcast(rwA, rwA_b, 512, "pss0", 0, "tlA")
        Bbc, Bbc_b = bcast(rwB, rwB_b, 512, "pss0", 512, "tlB")
        for ch in range(2):
            cap, cb = chunks[ch]
            vtt(ALU.mult, evB, evB_b, cap, cb, Abc, Abc_b, "tl1")
            vtt(ALU.add, evB, evB_b, evB, evB_b, Bbc, Bbc_b, "tl2")
            sact(AR_q[:, 4096 + 2048 * ch + t0: 4096 + 2048 * ch + t0 + 512],
                 lnout_bufs[ch], evB, evB_b, AF.Copy, "tl3",
                 bias=lb_sb[:, ch:ch + 1], biasb=lb_b,
                 scale=lw_sb[:, ch:ch + 1], scaleb=lw_b)

    if "dbg_p8" in c["debug"]:
        dma("g", O["dbg_p8"][0], AR_q[:, 4096:8192],
            (lnout_bufs[0], lnout_bufs[1]), (O["dbg_p8"][1],), "dbg")
    upw_b = wot_b
    dma("g", wproj[:, 0:2048], I["up_wr"][0], (I["up_wr"][1],), (upw_b,), "upw")
    upb_sb, upb_b = sbt_get("upbs", [128, 8])
    dma("s", upb_sb, I["up_b"][0], (I["up_b"][1],), (upb_b,), "upb")
    y1d, y1d_b = DR["dr_y1"]
    y2d, y2d_b = DR["dr_y2"]
    xsb, _ = SB["xsb"]
    y1l_b = pg.buf("py1l")
    for dc in range(8):
        for tc in range(4):
            t0 = 512 * tc
            pst, psb = PS[pwn[tc % 2]]
            ps = pst[:, 0:512]
            for kc in range(2):
                mm(ps, psb, wproj[:, 1024 * kc + 128 * dc: 1024 * kc + 128 * dc + 128],
                   AR_q[:, 4096 + 2048 * kc + t0: 4096 + 2048 * kc + t0 + 512],
                   (upw_b, lnout_bufs[kc]), kc == 0, kc == 1, tag="upm")
            sact(evB, evB_b, ps, psb, AF.Copy, "upe",
                 bias=upb_sb[:, dc:dc + 1], biasb=upb_b)
            xc = xsb[:, 0:512]
            dma("y", xc, y1d[128 * dc:128 * dc + 128, t0:t0 + 512], (y1d_b,), (y1l_b,),
                "py1")
            vtt(ALU.add, sq0, sq0_b, evB, evB_b, xc, y1l_b, "py2")
            dma("g", y2d[128 * dc:128 * dc + 128, t0:t0 + 512], sq0[:, 0:512],
                (sq0_b,), (y2d_b,), "py2s")

    if "dbg_y2" in c["debug"]:
        dma("g", O["dbg_y2"][0], y2d, (y2d_b,), (O["dbg_y2"][1],), "dbg")


def finalize_debug(c):
    pass



_CACHED = {}

def _mk_sel8():
    a = np.zeros((8, 1024), dtype=np.float32)
    for e in range(8):
        a[e, 128 * e:128 * e + 128] = 1.0
    return a.astype(ml_dtypes.bfloat16)


def kernel(**inputs):
    key = "nc"
    if key not in _CACHED:
        _CACHED[key] = build_program(debug=DEBUG)
    nc = _CACHED[key]

    f = lambda a: np.ascontiguousarray(np.asarray(a, dtype=np.float32))
    x = f(inputs["L0_attn_x"]).reshape(B * L, D)

    wq = f(inputs["L0_attn_wq"]); wk = f(inputs["L0_attn_wk"]); wv = f(inputs["L0_attn_wv"])
    W1 = f(inputs["L0_attn_W1"]); b1t = f(inputs["L0_attn_b1"])
    lnw = f(inputs["L0_attn_ttt_ln_w"]); lnb = f(inputs["L0_attn_ttt_ln_b"])
    lrw = f(inputs["L0_attn_lr_w"]); lrb = f(inputs["L0_attn_lr_b"])

    def colchunks(a, nch):
        return np.ascontiguousarray(a.reshape(nch, 128).T)

    common = {
        "fWq": f(inputs["L0_ffn_Wq"]), "fWk": f(inputs["L0_ffn_Wk"]),
        "fWv": f(inputs["L0_ffn_Wv"]), "fWo": f(inputs["L0_ffn_Wo"]),
        "fbq": colchunks(f(inputs["L0_ffn_bq"]), 8),
        "fbk": colchunks(f(inputs["L0_ffn_bk"]), 8),
        "fbv": f(inputs["L0_ffn_bv"]).reshape(1, D),
        "fbo": colchunks(f(inputs["L0_ffn_bo"]), 8),
        "gWr": np.ascontiguousarray(
            f(inputs["L0_ffn_gW"]).reshape(8, 128, 8).transpose(1, 0, 2).reshape(128, 64)),
        "gb": f(inputs["L0_ffn_gb"]).reshape(1, 8),
        "b1r": f(inputs["L0_ffn_b1"]).reshape(1, NE * FF),
        "b2e": f(inputs["L0_ffn_b2"]),
        "ln1g": colchunks(f(inputs["L0_ffn_ln1_g"]), 8),
        "ln1b": colchunks(f(inputs["L0_ffn_ln1_b"]), 8),
        "ln2g": colchunks(f(inputs["L0_ffn_ln2_g"]), 8),
        "ln2b": colchunks(f(inputs["L0_ffn_ln2_b"]), 8),
        "down_w": f(inputs["L0_attn_down_w"]),
        "down_b": colchunks(f(inputs["L0_attn_down_b"]), 2),
        "up_wr": np.ascontiguousarray(
            f(inputs["L0_attn_up_w"]).reshape(2, 128, D).transpose(1, 0, 2).reshape(128, 2 * D)),
        "up_b": colchunks(f(inputs["L0_attn_up_b"]), 8),
        "wo_tr": np.ascontiguousarray(
            f(inputs["L0_attn_wo"]).reshape(2, 128, 256).transpose(1, 0, 2).reshape(128, 512)),
        "post_w": colchunks(f(inputs["L0_attn_post_w"]), 2),
        "post_b": colchunks(f(inputs["L0_attn_post_b"]), 2),
        "lnt_w": colchunks(f(inputs["L0_attn_ln_w"]), 2),
        "lnt_b": colchunks(f(inputs["L0_attn_ln_b"]), 2),
        "tok_idx": f(inputs["L0_attn_tok_idx"]).reshape(1, 16),
        "ident": np.eye(128, dtype=np.float32),
        "sel8": _mk_sel8(),
        "rarange": (1.0 / np.arange(1, 17, dtype=np.float32)).reshape(1, 16),
        "masku": np.ascontiguousarray(
            np.tile(-np.triu(np.ones((16, 16), dtype=np.float32)), (1, 512))),
    }
    eW1 = f(inputs["L0_ffn_W1"]); eW2 = f(inputs["L0_ffn_W2"])

    in_maps = []
    for core in range(8):
        h = core % 2
        m = dict(common)
        m["x_fm"] = np.ascontiguousarray(x[core * TOK:(core + 1) * TOK, :].T)
        tok0 = 2048 * (core % 4)
        oi = np.zeros((128, 32), dtype=np.int32)
        for hh in range(2):
            for tt in range(16):
                oi[:, 16 * hh + tt] = 8192 * hh + tok0 + 128 * tt + np.arange(128)
        m["oidx"] = oi
        m["eW1own"] = eW1[core]
        m["eW2own"] = eW2[core]
        m["scan_wq"] = np.ascontiguousarray(wq[:, 128 * h:128 * h + 128])
        m["scan_wk"] = np.ascontiguousarray(wk[:, 128 * h:128 * h + 128])
        m["scan_wv"] = np.ascontiguousarray(wv[:, 128 * h:128 * h + 128])
        m["scan_W1"] = W1[h]
        m["scan_b1"] = b1t[h].reshape(1, 128)
        m["scan_g"] = lnw[h].reshape(1, 128)
        m["scan_gb"] = lnb[h].reshape(1, 128)
        m["scan_lrw"] = colchunks(lrw[h, 0], 2)
        m["scan_lrb"] = np.array([[f(inputs["L0_attn_lr_b"])[h]]], dtype=np.float32)
        in_maps.append(m)

    from concourse.aot_env import allow_ici_all_to_all
    with allow_ici_all_to_all():
        res = run_bass_kernel_spmd(nc, in_maps, core_ids=list(range(8)))
    out = np.concatenate([np.asarray(r["out_fm"]).T for r in res.results], axis=0)
    kernel._last_results = res
    return np.ascontiguousarray(out.reshape(B, L, D), dtype=np.float32)


# version 07
# speedup vs baseline: 1.0258x; 1.0258x over previous
import numpy as np
import ml_dtypes
import concourse.bass as bass
import concourse.mybir as mybir
from concourse.bass_utils import run_bass_kernel_spmd
from contextlib import ExitStack

F32 = mybir.dt.float32
BF16 = mybir.dt.bfloat16
AX = mybir.AxisListType.X
AF = mybir.ActivationFunctionType
ALU = mybir.AluOpType

B, L, D = 2, 8192, 1024
TOK = 2048
FF = 2048
NE = 8

PHASES = 9
CUT = ""
DEBUG = ()



class Buf:
    __slots__ = ("name", "wop", "rops")
    def __init__(self, name):
        self.name = name
        self.wop = None
        self.rops = []

class Op:
    __slots__ = ("oid", "eng", "fn", "deps", "is_dma", "q", "marked", "sem", "val", "tag")
    def __init__(self, oid, eng, fn, is_dma, q, tag):
        self.oid = oid; self.eng = eng; self.fn = fn
        self.deps = set(); self.is_dma = is_dma; self.q = q
        self.marked = False; self.sem = None; self.val = None; self.tag = tag

class Pg:
    ENGS = ("t", "v", "s", "g", "y")

    def __init__(self, nc):
        self.nc = nc
        self.ops = []
        self.per_eng = {e: [] for e in self.ENGS}
        self._barrier_tail = None
        self.last_on = {e: None for e in self.ENGS}

    def buf(self, name):
        return Buf(name)

    def _record(self, op, reads, writes, accum):
        for b in reads:
            if b.wop is not None:
                op.deps.add(b.wop)
            b.rops.append(op.oid)
        for b in writes:
            if b.wop is not None and not accum:
                op.deps.add(b.wop)
            if not accum:
                for r in b.rops:
                    op.deps.add(r)
                b.rops = []
            b.wop = op.oid
        if self._barrier_tail is not None and self.last_on[op.eng] is None:
            op.deps.update(self._barrier_tail)
        op.deps.discard(op.oid)
        self.last_on[op.eng] = op.oid
        self.ops.append(op)
        self.per_eng[op.eng].append(op)

    def op(self, eng, fn, reads=(), writes=(), accum=False, tag=""):
        o = Op(len(self.ops), eng, fn, False, None, tag)
        self._record(o, reads, writes, accum)
        return o

    def dma(self, q, fn, reads=(), writes=(), tag=""):
        o = Op(len(self.ops), q, fn, True, q, tag)
        self._record(o, reads, writes, False)
        return o

    def cc(self, fn, reads=(), writes=(), tag="cc"):
        o = Op(len(self.ops), "g", fn, True, "cc", tag)
        self._record(o, reads, writes, False)
        return o

    def barrier(self):
        tail = set(o for o in (self.last_on[e] for e in self.ENGS) if o is not None)
        self._barrier_tail = tail
        for e in self.ENGS:
            self.last_on[e] = None

    def emit(self, block, es):
        nc = self.nc
        sems = {}
        for e in ("t", "v", "s", "g"):
            sems["tk_" + e] = es.enter_context(nc.semaphore("tk_" + e))
        for q in ("y", "g", "v", "s"):
            sems["dm_" + q] = es.enter_context(nc.semaphore("dm_" + q))
        sems["cc"] = es.enter_context(nc.semaphore("cc"))

        byid = self.ops
        for o in self.ops:
            for d in o.deps:
                a = byid[d]
                if a.is_dma:
                    continue
                if a.eng == "t" and o.eng == "t":
                    continue
                a.marked = True

        tick = {e: 0 for e in ("t", "v", "s", "g")}
        dmac = {q: 0 for q in ("y", "g", "v", "s")}
        ccc = 0
        for o in self.ops:
            if o.is_dma:
                if o.q == "cc":
                    ccc += 1
                    o.sem, o.val = sems["cc"], ccc
                else:
                    dmac[o.q] += 16
                    o.sem, o.val = sems["dm_" + o.q], dmac[o.q]
            elif o.marked:
                assert o.eng != "y", f"sync-engine compute op as dep: {o.tag}"
                tick[o.eng] += 1
                o.sem, o.val = sems["tk_" + o.eng], tick[o.eng]

        waits = {}
        wm = {e: {} for e in self.ENGS}
        for o in self.ops:
            need = {}
            if o.is_dma and o.q in ("y", "g", "v", "s") and o.val > 16:
                need[id(o.sem)] = (o.sem, o.val - 16)
            for d in sorted(o.deps):
                a = byid[d]
                if a.sem is None:
                    continue
                val = a.val
                key = id(a.sem)
                if wm[o.eng].get(key, 0) >= val:
                    continue
                if key not in need or need[key][1] < val:
                    need[key] = (a.sem, val)
            wl = list(need.values())
            for (sem, val) in wl:
                wm[o.eng][id(sem)] = val
            waits[o.oid] = wl

        finals = [("tk_t", tick["t"]), ("tk_v", tick["v"]), ("tk_s", tick["s"]),
                  ("tk_g", tick["g"]), ("dm_y", dmac["y"]), ("dm_g", dmac["g"]),
                  ("dm_v", dmac["v"]), ("dm_s", dmac["s"]), ("cc", ccc)]

        eng_attr = {"t": "tensor", "v": "vector", "s": "scalar", "g": "gpsimd", "y": "sync"}
        for e in self.ENGS:
            ops = self.per_eng[e]
            if not ops and e != "g":
                continue

            def body(engine, ops=ops, e=e):
                for o in ops:
                    for (sem, val) in waits[o.oid]:
                        engine.wait_ge(sem, val)
                    ins = o.fn(engine)
                    if o.sem is not None:
                        ins.then_inc(o.sem, 16 if (o.is_dma and o.q != "cc") else 1)
                if e == "g":
                    for nm, v in finals:
                        if v > 0 and nm != "tk_g":
                            engine.wait_ge(sems[nm], v)
            getattr(block, eng_attr[e])(body)



def build_program(debug=()):
    nc = bass.Bass("TRN2", target_bir_lowering=False)
    pg = Pg(nc)
    c = {}
    c["nc"], c["pg"] = nc, pg
    c["debug"] = set(debug)

    I = {}
    def IN(name, shape, dt=F32):
        I[name] = (nc.dram_tensor(name, shape, dt, kind="ExternalInput").ap(), pg.buf(name))
    c["I"] = I

    IN("x_fm", [D, TOK])
    IN("fWq", [D, D]); IN("fWk", [D, D]); IN("fWv", [D, D]); IN("fWo", [D, D])
    IN("fbq", [128, 8]); IN("fbk", [128, 8]); IN("fbv", [1, D]); IN("fbo", [128, 8])
    IN("gWr", [128, 64]); IN("gb", [1, 8])
    IN("eW1own", [D, FF]); IN("eW2own", [FF, D])
    IN("b1r", [1, NE * FF]); IN("b2e", [NE, D])
    IN("ln1g", [128, 8]); IN("ln1b", [128, 8]); IN("ln2g", [128, 8]); IN("ln2b", [128, 8])
    IN("down_w", [D, 256]); IN("down_b", [128, 2])
    IN("up_wr", [128, 2 * D]); IN("up_b", [128, 8])
    IN("wo_tr", [128, 2 * 256])
    IN("post_w", [128, 2]); IN("post_b", [128, 2])
    IN("lnt_w", [128, 2]); IN("lnt_b", [128, 2])
    IN("scan_wq", [256, 128]); IN("scan_wk", [256, 128]); IN("scan_wv", [256, 128])
    IN("scan_W1", [128, 128]); IN("scan_b1", [1, 128])
    IN("scan_g", [1, 128]); IN("scan_gb", [1, 128])
    IN("scan_lrw", [128, 2]); IN("scan_lrb", [1, 1])
    IN("tok_idx", [1, 16])
    IN("ident", [128, 128])
    IN("sel8", [8, 1024], BF16)
    IN("rarange", [1, 16])
    IN("masku", [16, L])
    IN("oidx", [128, 32], mybir.dt.int32)

    O = {}
    def ON(name, shape, dt=F32):
        O[name] = (nc.dram_tensor(name, shape, dt, kind="ExternalOutput").ap(), pg.buf(name))
    c["O"] = O
    ON("out_fm", [D, TOK])
    DBG_SHAPES = {
        "dbg_src": ([D, TOK], BF16), "dbg_q": ([D, TOK], BF16),
        "dbg_kall": ([4 * D, TOK], BF16), "dbg_vall": ([L, D], BF16),
        "dbg_attn": ([D, TOK], BF16), "dbg_x1": ([D, TOK], F32),
        "dbg_y1": ([D, TOK], F32), "dbg_hinall": ([D, TOK], BF16),
        "dbg_scanout": ([L, 128], F32), "dbg_y2": ([D, TOK], F32),
        "dbg_w": ([8, TOK], BF16),
        "dbg_xq": ([128, L], BF16), "dbg_xkt": ([L, 128], BF16),
        "dbg_tn": ([L, 128], F32), "dbg_pn": ([L, 128], F32),
        "dbg_g12": ([16, L], BF16), "dbg_le": ([16, 512], F32), "dbg_probe": ([128, 2048], F32), "dbg_kin": ([D, TOK], BF16), "dbg_p2": ([128, 2048], F32), "dbg_p3": ([128, 2048], F32), "dbg_p4": ([128, 4096], BF16), "dbg_p5": ([128, 2048], F32), "dbg_p6": ([128, 2048], F32), "dbg_p8": ([128, 4096], BF16),
    }
    for name in debug:
        sh, dt = DBG_SHAPES[name]
        O[name] = (nc.dram_tensor(name, sh, dt, kind="ExternalOutput").ap(), pg.buf(name))

    DR = {}
    def DRT(name, shape, dt=F32):
        DR[name] = (nc.dram_tensor(name, shape, dt, kind="Internal").ap(), pg.buf(name))
    c["DR"] = DR

    DRT("dr_w1own", [D, FF], BF16)
    DRT("dr_w2own", [FF, D], BF16)
    DRT("dr_w1all", [NE * D, FF], BF16)
    DRT("dr_w2all", [NE * FF, D], BF16)
    DRT("dr_wattn", [4 * D, D], BF16)
    DRT("dr_k_in", [D, TOK], BF16)
    DRT("dr_k_all", [4 * D, TOK], BF16)
    DRT("dr_v_in", [TOK, D], BF16)
    DRT("dr_v_all", [L, D], BF16)
    DRT("dr_x1", [D, TOK])
    DRT("dr_y1", [D, TOK])
    DRT("dr_y2", [D, TOK])
    DRT("dr_hin", [256, TOK], BF16)
    DRT("dr_hin_all", [D, TOK], BF16)
    DRT("dr_xktok", [L, 128], BF16)
    DRT("dr_tn", [L, 128])
    DRT("dr_pn", [L, 128])
    DRT("dr_out", [L, 128])
    DRT("dr_out_all", [4 * L, 128])
    DRT("dr_bar_in", [1, 8])
    DRT("dr_bar_out", [1, 8])

    SB = {}
    SBH = {}
    def SBT(name, shape, dt=F32):
        h = nc.alloc_sbuf_tensor(name, shape, dt)
        SBH[name] = h
        SB[name] = (h.ap(), pg.buf(name))
        return SB[name]
    c["SB"], c["SBT"], c["SBH"] = SB, SBT, SBH

    PS = {}
    def PST(name, shape, dt=F32):
        PS[name] = (nc.alloc_psum_tensor(name, shape, dt).ap(), pg.buf(name))
    c["PS"] = PS
    PST("pss0", [128, 1024])
    PST("pss1", [128, 1024])
    PST("pw0", [128, 512])
    PST("pw1", [128, 512])
    PST("pw2", [128, 512])
    PST("pw3", [128, 512])

    SBT("AR_q", [128, 16384], BF16)
    SBT("AR_s", [128, 16384], BF16)
    SBT("AR_a", [128, 16384], BF16)
    SBT("AR_x", [128, 16384], BF16)
    SBT("FA", [128, 4096])
    SBT("xsb", [128, 1024])
    SBT("wproj", [128, 8192], BF16)
    SBT("P0", [128, 1024], BF16); SBT("P1", [128, 1024], BF16)

    SBT("onescol", [128, 1])
    SBT("onesrow", [1, 128])
    SBT("onesrow_bf", [1, 512], BF16)
    SBT("eps5", [1, 1]); SBT("eps6r", [128, 1])
    SBT("ident_f", [128, 128])
    SBT("sq0", [128, 512]); SBT("sq1", [128, 512])
    SBT("r_mu", [1, 512]); SBT("r_t", [1, 512]); SBT("r_t2", [1, 512])
    SBT("evA", [128, 512]); SBT("evB", [128, 512])
    SBT("ldtmp", [128, 1024])

    build_all(c)

    with ExitStack() as es:
        block = es.enter_context(nc.Block())
        pg.emit(block, es)
    return nc



def helpers(c):
    nc, pg = c["nc"], c["pg"]

    def op(eng, fn, reads=(), writes=(), accum=False, tag=""):
        return pg.op(eng, fn, reads, writes, accum, tag)

    def dmac(q, dst, src, reads, writes, tag="dma"):
        return pg.dma(q, lambda e, dst=dst, src=src: e.dma_start(out=dst, in_=src),
                      reads, writes, tag)

    def mm(psd, psb, lhsT, rhs, rbufs, start, stop, tp=None, tag="mm"):
        return op("t", lambda t, psd=psd, lhsT=lhsT, rhs=rhs, start=start, stop=stop,
                  tp=tp: t.matmul(psd, lhsT, rhs, start=start, stop=stop, tile_position=tp),
                  reads=rbufs, writes=(psb,), accum=not start, tag=tag)

    def vtt(alu, dst, db, a, ab, b2, bb, tag, eng="v"):
        return op(eng, lambda v, dst=dst, a=a, b2=b2, alu=alu: v.tensor_tensor(dst, a, b2, alu),
                  reads=(ab, bb), writes=(db,), tag=tag)

    def vts(dst, db, a, ab, s1, s1b, op0, tag, s2=None, s2b=None, op1=ALU.mult):
        rd = [ab] + ([s1b] if s1b is not None else []) + ([s2b] if s2b is not None else [])
        if s2 is None:
            return op("v", lambda v, dst=dst, a=a, s1=s1, op0=op0:
                      v.tensor_single_scalar(dst, a, s1, op0),
                      reads=tuple(rd), writes=(db,), tag=tag)
        return op("v", lambda v, dst=dst, a=a, s1=s1, s2=s2, op0=op0, op1=op1:
                  v.tensor_scalar(dst, a, s1, s2, op0, op1),
                  reads=tuple(rd), writes=(db,), tag=tag)

    def vstt(dst, db, a, ab, sc, scb, b2, bb, op0, op1, tag, acc=None, accb=None):
        rd = [ab, bb] + ([scb] if scb is not None else [])
        wr = [db] + ([accb] if accb is not None else [])
        return op("v", lambda v, dst=dst, a=a, sc=sc, b2=b2, op0=op0, op1=op1, acc=acc:
                  v.scalar_tensor_tensor(dst, a, sc, b2, op0, op1, accum_out=acc),
                  reads=tuple(rd), writes=tuple(wr), tag=tag)

    def vcopy(dst, db, a, ab, tag, eng="v"):
        return op(eng, lambda v, dst=dst, a=a: v.tensor_copy(dst, a),
                  reads=(ab,), writes=(db,), tag=tag)

    def vrecip(dst, db, a, ab, tag):
        return op("v", lambda v, dst=dst, a=a: v.reciprocal(dst, a),
                  reads=(ab,), writes=(db,), tag=tag)

    def vrmax(dst, db, a, ab, tag):
        return op("v", lambda v, dst=dst, a=a: v.reduce_max(dst, a, AX),
                  reads=(ab,), writes=(db,), tag=tag)

    def vmemset(dst, db, val, tag):
        return op("v", lambda v, dst=dst, val=val: v.memset(dst, val),
                  writes=(db,), tag=tag)

    def sact(dst, db, a, ab, func, tag, bias=None, biasb=None, scale=None, scaleb=None,
             acc=None, accb=None):
        rd = [ab] + ([biasb] if biasb is not None else []) + \
             ([scaleb] if scaleb is not None else [])
        wr = [db] + ([accb] if accb is not None else [])
        kw = {}
        if bias is not None:
            kw["bias"] = bias
        if scale is not None:
            kw["scale"] = scale
        if acc is not None:
            kw["accum_out"] = acc
        if func == AF.Copy and ("bias" in kw or "scale" in kw):
            func = AF.Identity
        return op("s", lambda s, dst=dst, a=a, func=func, kw=kw:
                  s.activation(dst, a, func, **kw),
                  reads=tuple(rd), writes=tuple(wr), tag=tag)

    for nm, f in [("op", op), ("dma", dmac), ("mm", mm), ("vtt", vtt), ("vts", vts),
                  ("vstt", vstt), ("vcopy", vcopy), ("vrecip", vrecip), ("vrmax", vrmax),
                  ("vmemset", vmemset), ("sact", sact)]:
        c[nm] = f

    SB, PS = c["SB"], c["PS"]

    def ln_rows(chunks, w, nfeat, outA, outB, tag, ps0="pss0", ps1="pss1"):
        oc, oc_b = SB["onescol"]
        eps5, eps5_b = SB["eps5"]
        psum_t, psum_b = PS[ps0]; psq_t, psq_b = PS[ps1]
        psum = psum_t[0:1, 0:w]; psq = psq_t[0:1, 0:w]
        sqs = [SB["sq0"], SB["sq1"]]
        mu, mu_b = SB["r_mu"]; t1, t1_b = SB["r_t"]; t2, t2_b = SB["r_t2"]
        n = len(chunks)
        for i, ch in enumerate(chunks):
            cap, cb = ch() if callable(ch) else ch
            mm(psum, psum_b, oc, cap, (oc_b, cb), i == 0, i == n - 1, tag=tag + ".s")
            sq, sq_b = sqs[i % 2]
            sact(sq[:, 0:w], sq_b, cap, cb, AF.Square, tag + ".sq")
            mm(psq, psq_b, oc, sq[:, 0:w], (oc_b, sq_b), i == 0, i == n - 1, tag=tag + ".q")
        inv = 1.0 / nfeat
        vts(mu[:, 0:w], mu_b, psum, psum_b, inv, None, ALU.mult, tag + ".mu")
        vts(t1[:, 0:w], t1_b, psq, psq_b, inv, None, ALU.mult, tag + ".ss")
        vtt(ALU.mult, t2[:, 0:w], t2_b, mu[:, 0:w], mu_b, mu[:, 0:w], mu_b, tag + ".m2")
        vtt(ALU.subtract, t1[:, 0:w], t1_b, t1[:, 0:w], t1_b, t2[:, 0:w], t2_b, tag + ".var")
        sact(t2[:, 0:w], t2_b, t1[:, 0:w], t1_b, AF.Sqrt, tag + ".std",
             bias=eps5, biasb=eps5_b)
        Aap, Ab = outA
        Bap, Bb = outB
        vrecip(Aap, Ab, t2[:, 0:w], t2_b, tag + ".A")
        vstt(Bap, Bb, mu[:, 0:w], mu_b, -1.0, None, Aap, Ab, ALU.mult, ALU.mult, tag + ".B")
    c["ln_rows"] = ln_rows

    DR = c["DR"]
    def cc_barrier(grp, reads, tag):
        bi, bi_b = DR["dr_bar_in"]
        bo, bo_b = DR["dr_bar_out"]
        pg.cc(lambda g, bi=bi, bo=bo, grp=grp: g.collective_compute(
            "AllReduce", ALU.add, replica_groups=grp, ins=[bi], outs=[bo]),
            reads=tuple(reads) + (bi_b,), writes=(bo_b,), tag="bar." + tag)
    c["cc_barrier"] = cc_barrier

    def bcast(row_ap, row_b, w, psname, lo=0, tag="bc"):
        orow, orow_b = SB["onesrow"]
        pst, psb = PS[psname]
        dst = pst[:, lo:lo + w]
        mm(dst, psb, orow, row_ap, (orow_b, row_b), True, True, tag=tag)
        return dst, psb
    c["bcast"] = bcast

    def sbt_get(name, shape, dt=F32):
        if name not in SB:
            c["SBT"](name, shape, dt)
        return SB[name]
    c["sbt_get"] = sbt_get

    def carve(name, shape, dt=F32):
        key = "cv_" + name
        if key in SB:
            return SB[key]
        SBH = c["SBH"]
        if dt != BF16:
            view = SBH["AR_x"].bitcast(dt).ap()
            cur = c.setdefault("carve_f32", 4160)
            n = shape[1]
            ap = view[0:shape[0], cur:cur + n]
            c["carve_f32"] = cur + n
            assert c["carve_f32"] <= 7040, f"carve f32 overflow {name}"
        else:
            view = SBH["AR_x"].ap()
            cur = c.setdefault("carve_bf16", 14080)
            n = shape[1]
            ap = view[0:shape[0], cur:cur + n]
            c["carve_bf16"] = cur + n
            assert c["carve_bf16"] <= 16384, f"carve bf16 overflow {name}"
        SB[key] = (ap, pg.buf(key))
        return SB[key]
    c["carve"] = carve


def build_all(c):
    helpers(c)
    nc, pg, op, dma = c["nc"], c["pg"], c["op"], c["dma"]
    I, SB = c["I"], c["SB"]
    vmemset = c["vmemset"]

    for nm, val in [("onescol", 1.0), ("onesrow", 1.0), ("onesrow_bf", 1.0),
                    ("eps5", 1e-5), ("eps6r", 1e-6)]:
        ap, b = SB[nm]
        vmemset(ap, b, val, "c." + nm)
    idf, idf_b = SB["ident_f"]
    dma("y", idf, I["ident"][0], (I["ident"][1],), (idf_b,), "ident")

    phase_weights(c)
    if PHASES >= 1:
        ffn_call(c, 0)
        pg.barrier()
    if PHASES >= 2:
        ttt_prestage(c)
        pg.barrier()
    if PHASES >= 3:
        ttt_scan(c)
        pg.barrier()
    if PHASES >= 4:
        post_ttt(c)
        pg.barrier()
    if PHASES >= 5:
        ffn_call(c, 1)
        pg.barrier()
    finalize_debug(c)


def phase_weights(c):
    nc, pg, op, dma = c["nc"], c["pg"], c["op"], c["dma"]
    I, DR, SB = c["I"], c["DR"], c["SB"]

    wat, wat_b = DR["dr_wattn"]
    for i, nm in enumerate(["fWq", "fWk", "fWv", "fWo"]):
        src, sb = I[nm]
        dma("g", wat[i * D:(i + 1) * D, :], src, (sb,), (wat_b,), "wat")
    w1o, w1o_b = DR["dr_w1own"]; w2o, w2o_b = DR["dr_w2own"]
    dma("g", w1o, I["eW1own"][0], (I["eW1own"][1],), (w1o_b,), "w1c")
    dma("g", w2o, I["eW2own"][0], (I["eW2own"][1],), (w2o_b,), "w2c")
    w1a, w1a_b = DR["dr_w1all"]; w2a, w2a_b = DR["dr_w2all"]
    c["cc_barrier"]([list(range(8))], (w1o_b, w2o_b), "w")
    pg.cc(lambda g, a=w1o, b=w1a: g.collective_compute(
        "AllGather", ALU.bypass, replica_groups=[list(range(8))], ins=[a], outs=[b]),
        reads=(w1o_b,), writes=(w1a_b,), tag="ag_w1")
    pg.cc(lambda g, a=w2o, b=w2a: g.collective_compute(
        "AllGather", ALU.bypass, replica_groups=[list(range(8))], ins=[a], outs=[b]),
        reads=(w2o_b,), writes=(w2a_b,), tag="ag_w2")



def ffn_call(c, call):
    nc, pg, op, dma, mm = c["nc"], c["pg"], c["op"], c["dma"], c["mm"]
    I, O, DR, SB, PS, SBT = c["I"], c["O"], c["DR"], c["SB"], c["PS"], c["SBT"]
    ln_rows, bcast, sbt_get = c["ln_rows"], c["bcast"], c["sbt_get"]
    vtt, vts, vstt, vcopy = c["vtt"], c["vts"], c["vstt"], c["vcopy"]
    vrecip, vrmax, vmemset, sact = c["vrecip"], c["vrmax"], c["vmemset"], c["sact"]
    sfx = f".{call}"

    xsrc, xsrc_b = I["x_fm"] if call == 0 else DR["dr_y2"]
    oc, oc_b = SB["onescol"]
    orow, orow_b = SB["onesrow"]

    AR_q, _ = SB["AR_q"]; AR_s, _ = SB["AR_s"]; AR_a, _ = SB["AR_a"]; AR_x, _ = SB["AR_x"]
    FA, _ = SB["FA"]
    q_bufs = [pg.buf(f"q{call}_{i}") for i in range(8)]
    src_bufs = [pg.buf(f"src{call}_{i}") for i in range(8)]
    attn_bufs = [pg.buf(f"attn{call}_{i}") for i in range(8)]
    x1n_bufs = [pg.buf(f"x1n{call}_{i}") for i in range(8)]
    xsb, _ = SB["xsb"]
    xsb_bufs = [pg.buf(f"xsb{call}_{i}") for i in range(2)]
    x1sb = xsb
    x1sb_bufs = xsb_bufs
    wproj, _ = SB["wproj"]
    rwA, rwA_b = sbt_get("rwA", [1, 512])
    rwB, rwB_b = sbt_get("rwB", [1, 512])

    def src_sl(kc, t0, w):
        return AR_s[:, 2048 * kc + t0: 2048 * kc + t0 + w]
    def q_sl(kc, t0, w):
        return AR_q[:, 2048 * kc + t0: 2048 * kc + t0 + w]
    def at_sl(kc, t0, w):
        return AR_a[:, 2048 * kc + t0: 2048 * kc + t0 + w]
    def x1n_sl(kc, t0, w):
        return AR_x[:, 2048 * kc + t0: 2048 * kc + t0 + w]

    fa_bufs = [pg.buf(f"fa{call}_{i}") for i in range(8)]
    wproj_buf = pg.buf(f"wproj{call}")

    wat, wat_b = DR["dr_wattn"]
    pwn = ["pw0", "pw1", "pw2", "pw3"]

    for tc in range(4):
        t0 = 512 * tc
        chunks = []
        for kc in range(8):
            fa_sl = FA[:, 512 * kc: 512 * kc + 512]
            fb = fa_bufs[kc]
            dma("y", fa_sl, xsrc[128 * kc:128 * kc + 128, t0:t0 + 512],
                (xsrc_b,), (fb,), "xld" + sfx)
            chunks.append((fa_sl, fb))
        ln_rows(chunks, 512, D, (rwA, rwA_b), (rwB, rwB_b), "mln" + sfx)
        Abc, Abc_b = bcast(rwA, rwA_b, 512, "pss0", 0, "mlA" + sfx)
        Bbc, Bbc_b = bcast(rwB, rwB_b, 512, "pss0", 512, "mlB" + sfx)
        if tc == 0 and call == 0 and "dbg_probe" in c["debug"]:
            dp, dpb = O["dbg_probe"]
            dma("g", dp[0:128, 0:512], chunks[0][0], (chunks[0][1],), (dpb,), "dbgp")
            dma("g", dp[0:1, 512:1024], rwA, (rwA_b,), (dpb,), "dbgp")
            dma("g", dp[1:2, 512:1024], rwB, (rwB_b,), (dpb,), "dbgp")
            dma("g", dp[0:1, 1024:1536], SB["r_mu"][0], (SB["r_mu"][1],), (dpb,), "dbgp")
            dma("g", dp[1:2, 1024:1536], SB["r_t"][0], (SB["r_t"][1],), (dpb,), "dbgp")
        ev, ev_b = SB["evA"]
        for kc in range(8):
            cap, cb = chunks[kc]
            vtt(ALU.mult, ev, ev_b, cap, cb, Abc, Abc_b, "mla" + sfx)
            vtt(ALU.add, src_sl(kc, t0, 512), src_bufs[kc], ev, ev_b, Bbc, Bbc_b,
                "mlb" + sfx)

    if "dbg_src" in c["debug"] and call == 0:
        dap, db = O["dbg_src"]
        for kc in range(8):
            dma("g", dap[128 * kc:128 * (kc + 1), :], src_sl(kc, 0, 2048),
                (src_bufs[kc],), (db,), "dbg")

    if CUT == "a1":
        return
    def load_w(widx, tag):
        wb = wproj_buf
        src = wat[widx * D: (widx + 1) * D, :].rearrange("(kc p) d -> p kc d", p=128)
        dma("y", wproj, src, (wat_b,), (wb,), "wld." + tag)
        return wb

    wb = load_w(0, "q" + sfx)
    bq_sb, bq_b = sbt_get(f"bq{call}", [128, 8])
    dma("y", bq_sb, I["fbq"][0], (I["fbq"][1],), (bq_b,), "bq")
    qn, qn_b = sbt_get("qn", [1, 16])
    kn, kn_b = sbt_get("kn", [1, 16])
    nrm, nrm_b = sbt_get("nrm", [1, 4])
    vmemset(qn, qn_b, -1e30, "qn0")
    vmemset(kn, kn_b, -1e30, "kn0")

    for dc in range(8):
        for tc in range(4):
            pst, psb = PS[pwn[tc % 4]]
            ps = pst[:, 0:512]
            for kc in range(8):
                mm(ps, psb, wproj[:, 1024 * kc + 128 * dc: 1024 * kc + 128 * dc + 128],
                   src_sl(kc, 512 * tc, 512), (wb, src_bufs[kc]), kc == 0, kc == 7,
                   tag="qmm" + sfx)
            dst = q_sl(dc, 512 * tc, 512)
            sact(dst, q_bufs[dc], ps, psb, AF.Copy, "qev" + sfx,
                 bias=bq_sb[:, dc:dc + 1], biasb=bq_b)
            sq, sq_b = SB["sq0"]
            sact(sq, sq_b, dst, q_bufs[dc], AF.Square, "qsq" + sfx)
            for h in range(2):
                pn2t, pn2b = PS["pss0"]
                pn2 = pn2t[0:1, 0:512]
                mm(pn2, pn2b, oc[64 * h:64 * h + 64, :], sq[64 * h:64 * h + 64, :],
                   (oc_b, sq_b), True, True, tag="qn" + sfx)
                vrmax(nrm[:, 0:1], nrm_b, pn2, pn2b, "qnm" + sfx)
                hh = 2 * dc + h
                vtt(ALU.max, qn[:, hh:hh + 1], qn_b, qn[:, hh:hh + 1], qn_b,
                    nrm[:, 0:1], nrm_b, "qnu" + sfx)

    if "dbg_q" in c["debug"] and call == 0:
        dap, db = O["dbg_q"]
        for kc in range(8):
            dma("g", dap[128 * kc:128 * (kc + 1), :], q_sl(kc, 0, 2048),
                (q_bufs[kc],), (db,), "dbg")

    if CUT == "q":
        return
    wb = load_w(1, "k" + sfx)
    bk_sb, bk_b = sbt_get(f"bk{call}", [128, 8])
    dma("y", bk_sb, I["fbk"][0], (I["fbk"][1],), (bk_b,), "bk")
    kin, kin_b = DR["dr_k_in"]
    kev = AR_a[:, 0:2048]
    kev_b = pg.buf(f"kev{call}")
    for dc in range(8):
        for tc in range(4):
            pst, psb = PS[pwn[tc % 4]]
            ps = pst[:, 0:512]
            for kc in range(8):
                mm(ps, psb, wproj[:, 1024 * kc + 128 * dc: 1024 * kc + 128 * dc + 128],
                   src_sl(kc, 512 * tc, 512), (wb, src_bufs[kc]), kc == 0, kc == 7,
                   tag="kmm" + sfx)
            dst = kev[:, 512 * (tc % 4): 512 * (tc % 4) + 512]
            sact(dst, kev_b, ps, psb, AF.Copy, "kev" + sfx,
                 bias=bk_sb[:, dc:dc + 1], biasb=bk_b)
            dma("g", kin[128 * dc:128 * dc + 128, 512 * tc:512 * tc + 512], dst,
                (kev_b,), (kin_b,), "kst" + sfx)

    if CUT == "k":
        return
    wb = load_w(2, "v" + sfx)
    ldt, _ = SB["ldtmp"]
    bvbc, bvbc_b = sbt_get("bvbc", [128, 1024])
    bvr_b = pg.buf(f"bvr{call}")
    for j in range(2):
        bvr = ldt[0:1, 0:512]
        dma("y", bvr, I["fbv"][0][0:1, 512 * j:512 * j + 512], (I["fbv"][1],), (bvr_b,), "bv")
        pbc, pbc_b = bcast(bvr, bvr_b, 512, "pss0", 0, "bvb")
        sact(bvbc[:, 512 * j:512 * j + 512], bvbc_b, pbc, pbc_b, AF.Copy, "bve")
    vin, vin_b = DR["dr_v_in"]
    for ts in range(16):
        for j in range(2):
            pst, psb = PS[pwn[(2 * ts + j) % 4]]
            ps = pst[:, 0:512]
            for kc in range(8):
                mm(ps, psb, src_sl(kc, 128 * ts, 128),
                   wproj[:, 1024 * kc + 512 * j: 1024 * kc + 512 * j + 512],
                   (wb, src_bufs[kc]), kc == 0, kc == 7, tag="vmm" + sfx)
            dst = kev[:, 1024 * (ts % 2) + 512 * j: 1024 * (ts % 2) + 512 * j + 512]
            vtt(ALU.add, dst, kev_b, ps, psb, bvbc[:, 512 * j:512 * j + 512], bvbc_b,
                "vev" + sfx)
            dma("g", vin[128 * ts:128 * ts + 128, 512 * j:512 * j + 512], dst,
                (kev_b,), (vin_b,), "vst" + sfx)

    if "dbg_kin" in c["debug"] and call == 0:
        dma("g", O["dbg_kin"][0], kin, (kin_b,), (O["dbg_kin"][1],), "dbg")
    if CUT == "v":
        return
    kall, kall_b = DR["dr_k_all"]; vall, vall_b = DR["dr_v_all"]
    grp = [[0, 1, 2, 3], [4, 5, 6, 7]]
    c["cc_barrier"](grp, (kin_b, vin_b), "kv" + sfx)
    pg.cc(lambda g, a=kin, b=kall, grp=grp: g.collective_compute(
        "AllGather", ALU.bypass, replica_groups=grp, ins=[a], outs=[b]),
        reads=(kin_b,), writes=(kall_b,), tag="ag_k" + sfx)
    pg.cc(lambda g, a=vin, b=vall, grp=grp: g.collective_compute(
        "AllGather", ALU.bypass, replica_groups=grp, ins=[a], outs=[b]),
        reads=(vin_b,), writes=(vall_b,), tag="ag_v" + sfx)

    if "dbg_kall" in c["debug"] and call == 0:
        dma("g", O["dbg_kall"][0], kall, (kall_b,), (O["dbg_kall"][1],), "dbg")
    if "dbg_vall" in c["debug"] and call == 0:
        dma("g", O["dbg_vall"][0], vall, (vall_b,), (O["dbg_vall"][1],), "dbg")

    if CUT == "ag":
        return
    ksq, ksq_b = SB["sq1"]
    kb = pg.buf(f"knl{call}")
    for r in range(4):
        for dc in range(8):
            ksl = AR_a[:, 0:2048]
            dma("y", ksl, kall[1024 * r + 128 * dc: 1024 * r + 128 * dc + 128, :],
                (kall_b,), (kb,), "knl" + sfx)
            for tc in range(4):
                sact(ksq, ksq_b, ksl[:, 512 * tc:512 * tc + 512], kb, AF.Square,
                     "ksq" + sfx)
                for h in range(2):
                    pn2t, pn2b = PS["pss0"]
                    pn2 = pn2t[0:1, 0:512]
                    mm(pn2, pn2b, oc[64 * h:64 * h + 64, :], ksq[64 * h:64 * h + 64, :],
                       (oc_b, ksq_b), True, True, tag="kn" + sfx)
                    vrmax(nrm[:, 0:1], nrm_b, pn2, pn2b, "knm" + sfx)
                    hh = 2 * dc + h
                    vtt(ALU.max, kn[:, hh:hh + 1], kn_b, kn[:, hh:hh + 1], kn_b,
                        nrm[:, 0:1], nrm_b, "knu" + sfx)
    ncr, ncr_b = sbt_get("ncr", [1, 16])
    vtt(ALU.mult, ncr, ncr_b, qn, qn_b, kn, kn_b, "cc1")
    sact(ncr, ncr_b, ncr, ncr_b, AF.Sqrt, "cc2")
    vts(ncr, ncr_b, ncr, ncr_b, -0.125, None, ALU.mult, "cc3")
    negCb, negCb_b = sbt_get("negCb", [128, 16])
    pbc, pbc_b = bcast(ncr, ncr_b, 16, "pss0", 0, "ccb")
    sact(negCb, negCb_b, pbc, pbc_b, AF.Copy, "cce")

    if CUT == "norm":
        return
    pg.barrier()

    khp = AR_s
    vhp = AR_x
    P_sb = [SB["P0"], SB["P1"]]
    idf, idf_b = SB["ident_f"]

    khp_bufs = [pg.buf(f"khp{call}_{r}") for r in range(4)]
    vhp_bufs = [pg.buf(f"vhp{call}_{r}") for r in range(4)]
    vhpo_b = pg.buf(f"vhpo{call}")
    pv_bufs = [pg.buf(f"pv{call}_{i}") for i in range(2)]
    p_bufs = [pg.buf(f"pp{call}_{i}") for i in range(4)]
    sc_bufs = [pg.buf(f"sc{call}_{i}") for i in range(4)]
    sct, sct0_b = PS["pss0"]
    sct1, sct1_b = PS["pss1"]
    sc_slots = [sct[:, 0:512], sct[:, 512:1024], sct1[:, 0:512], sct1[:, 512:1024]]
    pvt = [PS["pw0"], PS["pw1"]]

    for hp in range(8):
        vh = vhp[:, 0:8320].rearrange("p (kt ss) -> p kt ss", ss=130)
        vmemset(vh[:, :, 64:65], vhpo_b, 1.0, "vone")
        vmemset(vh[:, :, 129:130], vhpo_b, 1.0, "vone")
        for r in range(4):
            dma("y", khp[:, 2048 * r:2048 * r + 2048],
                kall[1024 * r + 128 * hp: 1024 * r + 128 * hp + 128, :],
                (kall_b,), (khp_bufs[r],), "khl" + sfx)
            vsrc = vall[2048 * r:2048 * r + 2048, 128 * hp:128 * hp + 128]
            vsrc_r = vsrc.rearrange("(kt p) d -> p kt d", p=128)
            for h in range(2):
                dmao = vhp[:, 0:8320].rearrange("p (kt ss) -> p kt ss", ss=130)[
                    :, 16 * r:16 * r + 16, 65 * h: 65 * h + 64]
                dma("g", dmao, vsrc_r[:, :, 64 * h:64 * h + 64],
                    (vall_b, vhpo_b), (vhp_bufs[r],), "vhl" + sfx)
        for qt in range(4):
            rhs_base = 2048 * hp + 512 * qt
            def smm(kt, h):
                si = (2 * kt + h) % 4
                sc = sc_slots[si]
                rhs = AR_q[64 * h:64 * h + 64, rhs_base:rhs_base + 512]
                mm(sc, sc_bufs[si],
                   khp[64 * h:64 * h + 64, 128 * kt:128 * kt + 128], rhs,
                   (khp_bufs[kt // 16], q_bufs[hp]), True, True, tp=(64 * h, 0),
                   tag="smm" + sfx)

            def expo(kt, h):
                si = (2 * kt + h) % 4
                P, _ = P_sb[h]
                Pd = P[:, 512 * (kt % 2):512 * (kt % 2) + 512]
                pi = 2 * h + kt % 2
                hh = 2 * hp + h
                sact(Pd, p_bufs[pi], sc_slots[si], sc_bufs[si], AF.Exp, "exp" + sfx,
                     bias=negCb[:, hh:hh + 1], biasb=negCb_b, scale=0.125)

            def pvm(kt, h):
                P, _ = P_sb[h]
                Pd = P[:, 512 * (kt % 2):512 * (kt % 2) + 512]
                pi = 2 * h + kt % 2
                pv, _ = pvt[h]
                mm(pv[0:65, :], pv_bufs[h],
                   vhp[:, 130 * kt + 65 * h: 130 * kt + 65 * h + 65], Pd,
                   (vhp_bufs[kt // 16], vhpo_b, p_bufs[pi]),
                   kt == 0, kt == 63, tag="pvm" + sfx)

            smm(0, 0); smm(0, 1)
            expo(0, 0); expo(0, 1)
            for kt in range(1, 64):
                smm(kt, 0); smm(kt, 1)
                expo(kt, 0); expo(kt, 1)
                pvm(kt - 1, 0); pvm(kt - 1, 1)
            pvm(63, 0); pvm(63, 1)
            for h in range(2):
                pv, _ = pvt[h]
                rr, rr_b = SB["r_t"]
                vrecip(rr[:, 0:512], rr_b, pv[64:65, :], pv_bufs[h], "arc" + sfx)
                rbct, rbcb = PS["pw2"]
                mm(rbct[0:64, 0:512], rbcb, orow[:, 0:64], rr[:, 0:512],
                   (orow_b, rr_b), True, True, tag="abc" + sfx)
                ev, ev_b = SB["evA"]
                sact(ev[0:64, :], ev_b, pv[0:64, :], pv_bufs[h], AF.Copy,
                     "aev" + sfx)
                dst = AR_a[64 * h:64 * h + 64, rhs_base:rhs_base + 512]
                vtt(ALU.mult, dst, attn_bufs[hp], ev[0:64, :], ev_b,
                    rbct[0:64, 0:512], rbcb, "adv" + sfx)

    if "dbg_attn" in c["debug"] and call == 0:
        dap, db = O["dbg_attn"]
        for kc in range(8):
            dma("g", dap[128 * kc:128 * (kc + 1), :], at_sl(kc, 0, 2048),
                (attn_bufs[kc],), (db,), "dbg")

    if CUT == "attn":
        return
    pg.barrier()

    wb = load_w(3, "o" + sfx)
    bo_sb, bo_b = sbt_get(f"bo{call}", [128, 8])
    dma("y", bo_sb, I["fbo"][0], (I["fbo"][1],), (bo_b,), "bo")
    g1_sb, g1_b = sbt_get("g1s", [128, 8])
    b1_sb, b1s_b = sbt_get("b1s", [128, 8])
    dma("y", g1_sb, I["ln1g"][0], (I["ln1g"][1],), (g1_b,), "g1")
    dma("y", b1_sb, I["ln1b"][0], (I["ln1b"][1],), (b1s_b,), "b1")
    l1A, l1A_b = sbt_get("l1A", [1, 512])
    l1B, l1B_b = sbt_get("l1B", [1, 512])
    x1pre_bufs = [pg.buf(f"x1p{call}_{i}") for i in range(8)]

    x1f, x1f_b = DR["dr_x1"]
    for tc in range(4):
        t0 = 512 * tc
        def mk_thunk(kc, t0=t0):
            def th():
                xc = xsb[:, 512 * (kc % 2):512 * (kc % 2) + 512]
                xcb = xsb_bufs[kc % 2]
                dma("y", xc, xsrc[128 * kc:128 * kc + 128, t0:t0 + 512], (xsrc_b,),
                    (xcb,), "xst" + sfx)
                return xc, xcb
            return th
        ln_rows([mk_thunk(kc) for kc in range(8)], 512, D, (rwA, rwA_b), (rwB, rwB_b),
                "mr" + sfx)
        Abc, Abc_b = bcast(rwA, rwA_b, 512, "pss0", 0, "wA" + sfx)
        Bbc, Bbc_b = bcast(rwB, rwB_b, 512, "pss0", 512, "wB" + sfx)
        x1chunks = []
        for dc in range(8):
            pst, psb = PS[pwn[dc % 2]]
            ps = pst[:, 0:512]
            for kc in range(8):
                mm(ps, psb, wproj[:, 1024 * kc + 128 * dc: 1024 * kc + 128 * dc + 128],
                   at_sl(kc, t0, 512), (wb, attn_bufs[kc]), kc == 0, kc == 7,
                   tag="omm" + sfx)
            ev, ev_b = SB["evA"]
            sact(ev, ev_b, ps, psb, AF.Copy, "oev" + sfx,
                 bias=bo_sb[:, dc:dc + 1], biasb=bo_b)
            xc = xsb[:, 512 * (dc % 2):512 * (dc % 2) + 512]
            xcb = xsb_bufs[dc % 2]
            dma("y", xc, xsrc[128 * dc:128 * dc + 128, t0:t0 + 512], (xsrc_b,), (xcb,),
                "xrl" + sfx)
            ev2, ev2_b = SB["evB"]
            vtt(ALU.mult, ev2, ev2_b, xc, xcb, Abc, Abc_b, "sr1" + sfx)
            vtt(ALU.add, ev2, ev2_b, ev2, ev2_b, Bbc, Bbc_b, "sr2" + sfx)
            dst = FA[:, 512 * dc:512 * dc + 512]
            vtt(ALU.add, dst, x1pre_bufs[dc], ev, ev_b, ev2, ev2_b, "x1p" + sfx)
            x1chunks.append((dst, x1pre_bufs[dc]))
        ln_rows(x1chunks, 512, D, (l1A, l1A_b), (l1B, l1B_b), "ln1" + sfx)
        Abc1, Abc1_b = bcast(l1A, l1A_b, 512, "pss1", 0, "l1a" + sfx)
        Bbc1, Bbc1_b = bcast(l1B, l1B_b, 512, "pss1", 512, "l1b" + sfx)
        for dc in range(8):
            cap, cb = x1chunks[dc]
            ev, ev_b = SB["evA"]
            vtt(ALU.mult, ev, ev_b, cap, cb, Abc1, Abc1_b, "x1a" + sfx)
            vtt(ALU.add, ev, ev_b, ev, ev_b, Bbc1, Bbc1_b, "x1b" + sfx)
            ev2, ev2_b = SB["evB"]
            sact(ev2, ev2_b, ev, ev_b, AF.Copy, "x1g" + sfx,
                 bias=b1_sb[:, dc:dc + 1], biasb=b1s_b,
                 scale=g1_sb[:, dc:dc + 1], scaleb=g1_b)
            dma("g", x1f[128 * dc:128 * dc + 128, t0:t0 + 512], ev2, (ev2_b,), (x1f_b,),
                "x1s" + sfx)
            vcopy(x1n_sl(dc, t0, 512), x1n_bufs[dc], ev2, ev2_b, "x1c" + sfx)

    if "dbg_x1" in c["debug"] and call == 0:
        dma("g", O["dbg_x1"][0], x1f, (x1f_b,), (O["dbg_x1"][1],), "dbg")

    if CUT == "b2":
        return
    gw_sb, gw_b = sbt_get("gw", [128, 64], BF16)
    gwt, gwt_b = SB["ldtmp"]
    dma("y", gwt[:, 0:64], I["gWr"][0], (I["gWr"][1],), (gwt_b,), "gwl")
    vcopy(gw_sb, gw_b, gwt[:, 0:64], gwt_b, "gwc")
    gbr, gbr_b = sbt_get("gbr", [1, 8])
    dma("y", gbr, I["gb"][0], (I["gb"][1],), (gbr_b,), "gbl")
    gbbc, gbbc_b = sbt_get("gbbc", [128, 8])
    pbc, pbc_b = bcast(gbr, gbr_b, 8, "pss0", 0, "gbb")
    sact(gbbc, gbbc_b, pbc, pbc_b, AF.Copy, "gbe")
    wT8, wT8_b = sbt_get("wT8bf", [8, TOK], BF16)
    sel8, sel8_b = sbt_get("sel8s", [8, 1024], BF16)
    dma("y", sel8, I["sel8"][0], (I["sel8"][1],), (sel8_b,), "sel8")
    g8, g8_b = sbt_get("g8", [128, 8])
    g8bv, g8b_b = sbt_get("g8b", [128, 8])
    g8c, g8c_b = sbt_get("g8c", [128, 8])
    gs, gs_b = sbt_get("gsml", [128, 4])

    for ts in range(16):
        pst, psb = PS["pw0"]
        ps = pst[:, 0:8]
        for kc in range(8):
            mm(ps, psb, x1n_sl(kc, 128 * ts, 128), gw_sb[:, 8 * kc:8 * kc + 8],
               (x1n_bufs[kc], gw_b), kc == 0, kc == 7, tag="gmm" + sfx)
        vtt(ALU.add, g8, g8_b, ps, psb, gbbc, gbbc_b, "gpre" + sfx)
        vrmax(gs[:, 0:1], gs_b, g8, g8_b, "gm" + sfx)
        vts(gs[:, 1:2], gs_b, gs[:, 0:1], gs_b, -1.0, None, ALU.mult, "gnm" + sfx)
        sact(g8bv, g8b_b, g8, g8_b, AF.Exp, "gex" + sfx, bias=gs[:, 1:2], biasb=gs_b,
             acc=gs[:, 2:3], accb=gs_b)
        vrecip(gs[:, 3:4], gs_b, gs[:, 2:3], gs_b, "grc" + sfx)
        vts(g8bv, g8b_b, g8bv, g8b_b, gs[:, 3:4], gs_b, ALU.mult, "gnr" + sfx)
        vrmax(gs[:, 0:1], gs_b, g8bv, g8b_b, "gm1" + sfx)
        vts(g8c, g8c_b, g8bv, g8b_b, gs[:, 0:1], gs_b, ALU.is_ge, "gge" + sfx)
        vstt(g8c, g8c_b, g8c, g8c_b, -1e30, None, g8bv, g8b_b, ALU.mult, ALU.add,
             "gmsk" + sfx)
        vrmax(gs[:, 1:2], gs_b, g8c, g8c_b, "gm2" + sfx)
        vts(g8c, g8c_b, g8bv, g8b_b, gs[:, 1:2], gs_b, ALU.is_ge, "gge2" + sfx)
        vtt(ALU.mult, g8, g8_b, g8bv, g8b_b, g8c, g8c_b, "gw2" + sfx)
        ptr, ptr_b = PS["pw1"]
        op("t", lambda t, ptr=ptr, g8=g8, idf=idf: t.transpose(
            ptr[0:8, 0:128], g8, idf),
           reads=(g8_b, idf_b), writes=(ptr_b,), tag="gtr" + sfx)
        sact(wT8[:, 128 * ts:128 * ts + 128], wT8_b, ptr[0:8, 0:128], ptr_b, AF.Copy,
             "gte" + sfx)

    if "dbg_w" in c["debug"] and call == 0:
        dma("g", O["dbg_w"][0], wT8, (wT8_b,), (O["dbg_w"][1],), "dbg")

    if CUT == "b3":
        return
    pg.barrier()

    orbf, orbf_b = SB["onesrow_bf"]
    def w1s(par):
        return AR_s[:, 8192 * par:8192 * par + 8192]
    def w2s(par):
        return AR_a[:, 8192 * par:8192 * par + 8192]
    hbf = AR_q[:, 0:8192]
    w1a, w1a_b = DR["dr_w1all"]; w2a, w2a_b = DR["dr_w2all"]
    SBH = c["SBH"]
    yq32 = SBH["AR_q"].bitcast(F32).ap()
    FAap, _ = SB["FA"]
    yacc_bufs = [pg.buf(f"ya{call}_{i}") for i in range(16)]
    def yacc_sl(idx):
        if idx < 8:
            return yq32[:, 4096 + 512 * idx: 4096 + 512 * idx + 512]
        return FAap[:, 512 * (idx - 8): 512 * (idx - 8) + 512]
    wbc, wbc_b = SB["bvbc"]
    b1row = SBH["ldtmp"].bitcast(BF16).ap()[0:1, 0:1024]
    b2bf = SB["P0"][0][0:8, 0:1024]
    b2bf_b = pg.buf(f"b2bf{call}")
    b2t, b2t_b = SB["ldtmp"]
    dma("y", b2t[0:8, 0:D], I["b2e"][0], (I["b2e"][1],), (b2t_b,), "b2l")
    vcopy(b2bf, b2bf_b, b2t[0:8, 0:D], b2t_b, "b2c")
    g2_sb, g2_b = sbt_get("g2s", [128, 8])
    bt2_sb, bt2_b = sbt_get("bt2s", [128, 8])
    dma("y", g2_sb, I["ln2g"][0], (I["ln2g"][1],), (g2_b,), "g2")
    dma("y", bt2_sb, I["ln2b"][0], (I["ln2b"][1],), (bt2_b,), "bt2")
    l2A, l2A_b = SB["rwA"]
    l2B, l2B_b = SB["rwB"]

    ydst, ydst_b = (DR["dr_y1"] if call == 0 else O["out_fm"])
    b1r_buf = pg.buf(f"b1r{call}")
    w1_bufs = [pg.buf(f"w1{call}_{i}") for i in range(2)]
    w2_bufs = [pg.buf(f"w2{call}_{i}") for i in range(2)]
    hb_bufs = [pg.buf(f"h{call}_{fr}") for fr in range(8)]

    def load_wexp(tp, e, half):
        i = (tp * 8 + e) * 2 + half
        par = i % 2
        q3 = ("y", "g", "s")
        for dc in range(8):
            dma(q3[dc % 3], w1s(par)[:, 1024 * dc:1024 * dc + 1024],
                w1a[1024 * e + 128 * dc: 1024 * e + 128 * dc + 128,
                    1024 * half:1024 * half + 1024],
                (w1a_b,), (w1_bufs[par],), "w1l" + sfx)
        for fr in range(8):
            dma(q3[(fr + 2) % 3], w2s(par)[:, 1024 * fr:1024 * fr + 1024],
                w2a[FF * e + 128 * (8 * half + fr):
                    FF * e + 128 * (8 * half + fr) + 128, :],
                (w2a_b,), (w2_bufs[par],), "w2l" + sfx)

    load_wexp(0, 0, 0)
    for tp in range(2):
        tb0 = 1024 * tp
        for e in range(8):
            for j in range(2):
                pst0, pst0_b = PS["pss0"]
                pb = pst0[:, 0:512]
                mm(pb, pst0_b, sel8[:, 128 * e:128 * e + 128],
                   wT8[0:8, tb0 + 512 * j: tb0 + 512 * j + 512], (sel8_b, wT8_b),
                   True, True, tag="wbc" + sfx)
                sact(wbc[:, 512 * j:512 * j + 512], wbc_b, pb, pst0_b, AF.Copy,
                     "wbe" + sfx)
            for half in range(2):
                b1rb = b1r_buf
                dma("g", b1row, I["b1r"][0][0:1, FF * e + 1024 * half:
                                            FF * e + 1024 * half + 1024],
                    (I["b1r"][1], b2t_b), (b1rb,), "b1l" + sfx)
                i_seq = (tp * 8 + e) * 2 + half
                par = i_seq % 2
                w1_b, w2_b = w1_bufs[par], w2_bufs[par]
                if i_seq + 1 < 32:
                    nxt = i_seq + 1
                    load_wexp(nxt // 16, (nxt // 2) % 8, nxt % 2)
                for fr in range(8):
                    for th in range(2):
                        pst, psb = PS[pwn[(2 * fr + th) % 4]]
                        ps = pst[:, 0:512]
                        for dc in range(8):
                            mm(ps, psb,
                               w1s(par)[:, 1024 * dc + 128 * fr:
                                        1024 * dc + 128 * fr + 128],
                               x1n_sl(dc, tb0 + 512 * th, 512), (w1_b, x1n_bufs[dc]),
                               dc == 0, False, tag="hmm" + sfx)
                        mm(ps, psb, b1row[0:1, 128 * fr: 128 * fr + 128],
                           orbf[0:1, 0:512], (b1rb, orbf_b), False, True,
                           tag="hb1" + sfx)
                        dst = hbf[:, 1024 * fr + 512 * th: 1024 * fr + 512 * th + 512]
                        vstt(dst, hb_bufs[fr], ps, psb, 0.0, None,
                             wbc[:, 512 * th:512 * th + 512], wbc_b, ALU.max, ALU.mult,
                             "hev" + sfx)
                for dc in range(8):
                    for th in range(2):
                        pst, psb = PS[pwn[(2 * dc + th) % 2]]
                        ps = pst[:, 0:512]
                        for fr in range(8):
                            mm(ps, psb,
                               w2s(par)[:, 1024 * fr + 128 * dc:
                                        1024 * fr + 128 * dc + 128],
                               hbf[:, 1024 * fr + 512 * th: 1024 * fr + 512 * th + 512],
                               (w2_b, hb_bufs[fr]), fr == 0, fr == 7, tag="ymm" + sfx)
                        ysl = yacc_sl(2 * dc + th)
                        yb = yacc_bufs[2 * dc + th]
                        if e == 0 and half == 0:
                            vcopy(ysl, yb, ps, psb, "yc" + sfx)
                        else:
                            vtt(ALU.add, ysl, yb, ysl, yb, ps, psb, "ya" + sfx)
        for dc in range(8):
            for th in range(2):
                pst, psb = PS["pw2"]
                ps = pst[:, 0:512]
                mm(ps, psb, b2bf[:, 128 * dc:128 * dc + 128],
                   wT8[0:8, tb0 + 512 * th: tb0 + 512 * th + 512], (b2bf_b, wT8_b),
                   True, True, tag="b2m" + sfx)
                ysl = yacc_sl(2 * dc + th)
                yb = yacc_bufs[2 * dc + th]
                vtt(ALU.add, ysl, yb, ysl, yb, ps, psb, "yb2" + sfx)
        for th in range(2):
            t0 = tb0 + 512 * th
            chunks = []
            for dc in range(8):
                xc = x1sb[:, 512 * (dc % 2):512 * (dc % 2) + 512]
                xcb = x1sb_bufs[dc % 2]
                dma("g", xc, x1f[128 * dc:128 * dc + 128, t0:t0 + 512], (x1f_b,), (xcb,),
                    "x1r" + sfx)
                ysl = yacc_sl(2 * dc + th)
                yb = yacc_bufs[2 * dc + th]
                vtt(ALU.add, ysl, yb, xc, xcb, ysl, yb, "ffp" + sfx)
                chunks.append((ysl, yb))
            ln_rows(chunks, 512, D, (l2A, l2A_b), (l2B, l2B_b), "ln2" + sfx)
            Abc, Abc_b = bcast(l2A, l2A_b, 512, "pss1", 0, "l2a" + sfx)
            Bbc, Bbc_b = bcast(l2B, l2B_b, 512, "pss1", 512, "l2b" + sfx)
            for dc in range(8):
                cap, cb = chunks[dc]
                ev, ev_b = SB["evA"]
                vtt(ALU.mult, ev, ev_b, cap, cb, Abc, Abc_b, "f2a" + sfx)
                vtt(ALU.add, ev, ev_b, ev, ev_b, Bbc, Bbc_b, "f2b" + sfx)
                ev2, ev2_b = SB["evB"]
                sact(ev2, ev2_b, ev, ev_b, AF.Copy, "f2g" + sfx,
                     bias=bt2_sb[:, dc:dc + 1], biasb=bt2_b,
                     scale=g2_sb[:, dc:dc + 1], scaleb=g2_b)
                xc = xsb[:, 0:512]
                xcb = xsb_bufs[0]
                dma("g", xc, xsrc[128 * dc:128 * dc + 128, t0:t0 + 512], (xsrc_b,), (xcb,),
                    "xr2" + sfx)
                ev3, ev3_b = SB["sq0"]
                vstt(ev3, ev3_b, ev2, ev2_b, 0.5, None, xc, xcb, ALU.mult, ALU.add,
                     "yf" + sfx)
                dma("g", ydst[128 * dc:128 * dc + 128, t0:t0 + 512], ev3, (ev3_b,),
                    (ydst_b,), "yst" + sfx)

    if call == 0:
        pg.barrier()
        moe_tail_call0(c, sfx)


def moe_tail_call0(c, sfx):
    nc, pg, op, dma, mm = c["nc"], c["pg"], c["op"], c["dma"], c["mm"]
    I, O, DR, SB, PS = c["I"], c["O"], c["DR"], c["SB"], c["PS"]
    ln_rows, bcast, sbt_get = c["ln_rows"], c["bcast"], c["sbt_get"]
    vtt, vcopy, sact = c["vtt"], c["vcopy"], c["sact"]
    oc, oc_b = SB["onescol"]
    y1, y1_b = DR["dr_y1"]
    hin, hin_b = DR["dr_hin"]

    wproj, _ = SB["wproj"]
    AR_a, _ = SB["AR_a"]
    dw_b = pg.buf("dwb")
    dwt, dwt_b = SB["ldtmp"]
    for kc in range(8):
        dma("y", dwt[:, 0:256], I["down_w"][0][128 * kc:128 * kc + 128, :],
            (I["down_w"][1],), (dwt_b,), "dwl")
        vcopy(wproj[:, 256 * kc:256 * kc + 256], dw_b, dwt[:, 0:256], dwt_b, "dwc")
    db_sb, db_b = sbt_get("dbs", [128, 2])
    dma("y", db_sb, I["down_b"][0], (I["down_b"][1],), (db_b,), "dbl")

    FA, _ = SB["FA"]
    AR_x, _ = SB["AR_x"]
    lyA, lyA_b = c["sbt_get"]("rwA", [1, 512])
    lyB, lyB_b = c["sbt_get"]("rwB", [1, 512])
    hev = AR_a[:, 0:512]
    hev_b = pg.buf("hevb")
    pwn = ["pw0", "pw1", "pw2", "pw3"]
    y1n_bufs = [pg.buf(f"y1n_{i}") for i in range(8)]
    y1l_bufs = [pg.buf(f"y1l_{i}") for i in range(8)]
    for tc in range(4):
        t0 = 512 * tc
        chunks = []
        for kc in range(8):
            fa_sl = FA[:, 512 * kc:512 * kc + 512]
            fb = y1l_bufs[kc]
            dma("y", fa_sl, y1[128 * kc:128 * kc + 128, t0:t0 + 512], (y1_b,), (fb,), "y1l")
            chunks.append((fa_sl, fb))
        ln_rows(chunks, 512, D, (lyA, lyA_b), (lyB, lyB_b), "lny" + sfx)
        Abc, Abc_b = bcast(lyA, lyA_b, 512, "pss0", 0, "lya")
        Bbc, Bbc_b = bcast(lyB, lyB_b, 512, "pss1", 0, "lyb")
        ev, ev_b = SB["evA"]
        for kc in range(8):
            cap, cb = chunks[kc]
            vtt(ALU.mult, ev, ev_b, cap, cb, Abc, Abc_b, "lya2")
            vtt(ALU.add, AR_x[:, 2048 * kc + t0:2048 * kc + t0 + 512], y1n_bufs[kc],
                ev, ev_b, Bbc, Bbc_b, "lyb2")
        for oc2 in range(2):
            pst, psb = PS[pwn[oc2]]
            ps = pst[:, 0:512]
            for kc in range(8):
                mm(ps, psb, wproj[:, 256 * kc + 128 * oc2: 256 * kc + 128 * oc2 + 128],
                   AR_x[:, 2048 * kc + t0:2048 * kc + t0 + 512], (dw_b, y1n_bufs[kc]),
                   kc == 0, kc == 7, tag="dmm")
            dst = hev
            sact(dst, hev_b, ps, psb, AF.Silu, "dsl",
                 bias=db_sb[:, oc2:oc2 + 1], biasb=db_b)
            dma("g", hin[128 * oc2:128 * oc2 + 128, t0:t0 + 512], dst, (hev_b,), (hin_b,),
                "dst")
    hina, hina_b = DR["dr_hin_all"]
    c["cc_barrier"]([[0, 1, 2, 3], [4, 5, 6, 7]], (hin_b,), "hin")
    pg.cc(lambda g, a=hin, b=hina: g.collective_compute(
        "AllGather", ALU.bypass, replica_groups=[[0, 1, 2, 3], [4, 5, 6, 7]],
        ins=[a], outs=[b]), reads=(hin_b,), writes=(hina_b,), tag="ag_hin")
    if "dbg_y1" in c["debug"]:
        dma("g", O["dbg_y1"][0], y1, (y1_b,), (O["dbg_y1"][1],), "dbg")
    if "dbg_hinall" in c["debug"]:
        dma("g", O["dbg_hinall"][0], hina, (hina_b,), (O["dbg_hinall"][1],), "dbg")


def ttt_prestage(c):
    nc, pg, op, dma, mm = c["nc"], c["pg"], c["op"], c["dma"], c["mm"]
    I, O, DR, SB, PS, SBT = c["I"], c["O"], c["DR"], c["SB"], c["PS"], c["SBT"]
    sbt_get = c["carve"]
    vtt, vts, vstt, vcopy = c["vtt"], c["vts"], c["vstt"], c["vcopy"]
    vrecip, vmemset, sact = c["vrecip"], c["vmemset"], c["sact"]
    oc, oc_b = SB["onescol"]
    orow, orow_b = SB["onesrow"]
    AR_q, _ = SB["AR_q"]; AR_s, _ = SB["AR_s"]; AR_a, _ = SB["AR_a"]
    FA, _ = SB["FA"]
    evA, evA_b = SB["evA"]; evB, evB_b = SB["evB"]
    sq0, sq0_b = SB["sq0"]; sq1, sq1_b = SB["sq1"]
    pwn = ["pw0", "pw1", "pw2", "pw3"]

    hin_sb = AR_q
    hin_b = pg.buf("hin_sb")
    hina, hina_b = DR["dr_hin_all"]
    for r in range(4):
        for c2 in range(2):
            dma("y", hin_sb[:, 2048 * (2 * r + c2): 2048 * (2 * r + c2) + 2048],
                hina[256 * r + 128 * c2: 256 * r + 128 * c2 + 128, :],
                (hina_b,), (hin_b,), "hld")

    def hin_sl(c2, t0, w):
        r = t0 // 2048
        return hin_sb[:, 2048 * (2 * r + c2) + (t0 % 2048): 2048 * (2 * r + c2) + (t0 % 2048) + w]

    sw = {}
    for nm in ["scan_wq", "scan_wk", "scan_wv"]:
        t, tb = sbt_get(nm + "b", [128, 256], BF16)
        dma("g", t, I[nm][0].rearrange("(cc p) f -> p cc f", p=128), (I[nm][1],),
            (tb,), nm)
        sw[nm] = (t, tb)

    srow, srow_b = sbt_get("sgrow", [1, 384])
    dma("s", srow[:, 0:128], I["scan_g"][0], (I["scan_g"][1],), (srow_b,), "sg")
    dma("s", srow[:, 128:256], I["scan_gb"][0], (I["scan_gb"][1],), (srow_b,), "sgb")
    vtt(ALU.mult, srow[:, 256:384], srow_b, srow[:, 0:128], srow_b,
        srow[:, 0:128], srow_b, "g2r")
    gco, gco_b = sbt_get("gconst", [128, 384])
    for i in range(3):
        pb, pb_b = PS["pss0"]
        mm(pb[:, 0:128], pb_b, orow, srow[:, 128 * i:128 * i + 128], (orow_b, srow_b),
           True, True, tag="gbc")
        sact(gco[:, 128 * i:128 * i + 128], gco_b, pb[:, 0:128], pb_b, AF.Copy, "gbe")

    XQ = AR_s[:, 0:8192]; XK = AR_s[:, 8192:16384]
    XQ_b = pg.buf("XQfm"); XK_b = pg.buf("XKfm")
    for (w_nm, dst, dstb) in [("scan_wq", XQ, XQ_b), ("scan_wk", XK, XK_b)]:
        wt, wtb = sw[w_nm]
        for tt in range(16):
            pst, psb = PS[pwn[tt % 4]]
            ps = pst[:, 0:512]
            for c2 in range(2):
                mm(ps, psb, wt[:, 128 * c2:128 * c2 + 128], hin_sl(c2, 512 * tt, 512),
                   (wtb, hin_b), c2 == 0, c2 == 1, tag="xfm")
            sact(dst[:, 512 * tt:512 * tt + 512], dstb, ps, psb, AF.Copy, "xfme")

    xkt_d, xkt_db = DR["dr_xktok"]
    tn_d, tn_db = DR["dr_tn"]
    pn_d, pn_db = DR["dr_pn"]
    for ts in range(64):
        t0 = 128 * ts
        outs = {}
        for w_nm, ev, evb in [("scan_wk", evA, evA_b), ("scan_wv", evB, evB_b)]:
            wt, wtb = sw[w_nm]
            pst, psb = PS[pwn[ts % 2]]
            ps = pst[:, 0:128]
            for c2 in range(2):
                mm(ps, psb, hin_sl(c2, t0, 128), wt[:, 128 * c2:128 * c2 + 128],
                   (hin_b, wtb), c2 == 0, c2 == 1, tag="xtk")
            sact(ev[:, 0:128], evb, ps, psb, AF.Copy, "xtke")
        xkbf, xkbf_b = sbt_get("xkbf", [128, 128], BF16)
        vcopy(xkbf, xkbf_b, evA[:, 0:128], evA_b, "xkb")
        dma("g", xkt_d[t0:t0 + 128, :], xkbf, (xkbf_b,), (xkt_db,), "xks")
        vtt(ALU.subtract, sq0[:, 0:128], sq0_b, evA[:, 0:128], evA_b, evB[:, 0:128],
            evB_b, "tn1")
        vtt(ALU.add, sq0[:, 0:128], sq0_b, sq0[:, 0:128], sq0_b, gco[:, 128:256],
            gco_b, "tn2")
        vtt(ALU.mult, sq0[:, 0:128], sq0_b, sq0[:, 0:128], sq0_b, gco[:, 0:128],
            gco_b, "tn3")
        dma("g", tn_d[t0:t0 + 128, :], sq0[:, 0:128], (sq0_b,), (tn_db,), "tns")
        wt, wtb = sw["scan_wq"]
        pst, psb = PS[pwn[2 + ts % 2]]
        ps = pst[:, 0:128]
        for c2 in range(2):
            mm(ps, psb, hin_sl(c2, t0, 128), wt[:, 128 * c2:128 * c2 + 128],
               (hin_b, wtb), c2 == 0, c2 == 1, tag="xqt")
        sact(evA[:, 0:128], evA_b, ps, psb, AF.Copy, "xqte")
        vtt(ALU.add, sq1[:, 0:128], sq1_b, evA[:, 0:128], evA_b, gco[:, 128:256],
            gco_b, "pn1")
        dma("g", pn_d[t0:t0 + 128, :], sq1[:, 0:128], (sq1_b,), (pn_db,), "pns")

    ti, ti_b = sbt_get("sti", [1, 16])
    dma("s", ti, I["rarange"][0], (I["rarange"][1],), (ti_b,), "rar")
    tix, tix_b = sbt_get("stix", [1, 16])
    dma("s", tix, I["tok_idx"][0], (I["tok_idx"][1],), (tix_b,), "tix")
    vtt(ALU.add, ti, ti_b, ti, ti_b, tix, tix_b, "ti1")
    vts(ti, ti_b, ti, ti_b, 0.0, None, ALU.max, "ti2")
    tid, tid_b = sbt_get("stid", [1, 16])
    vts(tid, tid_b, ti, ti_b, 1.0 / 128.0, None, ALU.mult, "tid")
    lrb_r, lrb_rb = sbt_get("slrb", [1, 1])
    dma("s", lrb_r, I["scan_lrb"][0], (I["scan_lrb"][1],), (lrb_rb,), "lrb")
    pst, psb = PS["pss1"]
    mm(pst[0:16, 0:1], psb, orow[:, 0:16], tid[:, 15:16], (orow_b, tid_b), True, True,
       tag="t15b")
    ti15c, ti15c_b = sbt_get("sti15", [16, 1])
    sact(ti15c, ti15c_b, pst[0:16, 0:1], psb, AF.Copy, "t15e")
    mm(pst[0:16, 1:2], psb, orow[:, 0:16], lrb_r, (orow_b, lrb_rb), True, True,
       tag="lrbb")
    lrb16, lrb16_b = sbt_get("slrb16", [16, 1])
    sact(lrb16, lrb16_b, pst[0:16, 1:2], psb, AF.Copy, "lrbe")
    mm(pst[0:16, 16:32], psb, orow[:, 0:16], tid, (orow_b, tid_b), True, True,
       tag="tidr")
    tidrep, tidrep_b = sbt_get("stidrep", [16, 16])
    sact(tidrep, tidrep_b, pst[0:16, 16:32], psb, AF.Copy, "tidre")

    lrw, lrw_b = sbt_get("slrw", [128, 2], BF16)
    dma("g", lrw, I["scan_lrw"][0], (I["scan_lrw"][1],), (lrw_b,), "lrw")
    pls, pls_b = PS["pss0"]
    for n in range(512):
        for c2 in range(2):
            mm(pls[0:16, n % 512: n % 512 + 1], pls_b,
               hin_sl(c2, 16 * n, 16), lrw[:, c2:c2 + 1], (hin_b, lrw_b),
               c2 == 0, c2 == 1, tag="lrm")
    lr_sm, lr_sm_b = sbt_get("slrsm", [16, 512])
    sact(lr_sm, lr_sm_b, pls[0:16, 0:512], pls_b, AF.Sigmoid, "lre",
         bias=lrb16, biasb=lrb16_b)
    le_sm, le_sm_b = sbt_get("slesm", [16, 512])
    vts(le_sm, le_sm_b, lr_sm, lr_sm_b, ti15c, ti15c_b, ALU.mult, "lesm")
    le_bf, le_bf_b = sbt_get("slesmb", [16, 512], BF16)
    vcopy(le_bf, le_bf_b, le_sm, le_sm_b, "lesb")

    g12t = AR_a[0:16, 0:8192]
    g12t_b = pg.buf("g12t")
    mk = evA[0:16, 0:512]
    mk_b = pg.buf("smk")
    for g in range(16):
        pA, pA_b = PS["pss1"]
        for j in range(32):
            n = 32 * g + j
            mm(pA[0:16, 16 * j:16 * j + 16], pA_b, XK[:, 16 * n:16 * n + 16],
               XQ[:, 16 * n:16 * n + 16], (XK_b, XQ_b), True, True, tag="atT")
        aT = sq0[0:16, 0:512]
        sact(aT, sq0_b, pA[0:16, 0:512], pA_b, AF.Copy, "atTe")
        for j in range(32):
            n = 32 * g + j
            vts(sq1[0:16, 16 * j:16 * j + 16], sq1_b, tidrep, tidrep_b,
                lr_sm[:, n:n + 1], lr_sm_b, ALU.mult, "etaT")
        vts(aT, sq0_b, aT, sq0_b, 1.0, None, ALU.add, "g12a")
        vtt(ALU.mult, sq1[0:16, 0:512], sq1_b, sq1[0:16, 0:512], sq1_b, aT, sq0_b,
            "g12b")
        dma("s", mk, I["masku"][0][:, 512 * g:512 * g + 512], (I["masku"][1], evA_b),
            (mk_b,), "mkl")
        vtt(ALU.mult, g12t[:, 512 * g:512 * g + 512], g12t_b, sq1[0:16, 0:512], sq1_b,
            mk, mk_b, "g12c")

    st = {"XQ": (XQ, XQ_b), "XK": (XK, XK_b), "g12t": (g12t, g12t_b),
          "lr_sm": (lr_sm, lr_sm_b), "le_sm": (le_sm, le_sm_b),
          "le_bf": (le_bf, le_bf_b), "gco": (gco, gco_b)}
    c["scan_state"] = st

    if "dbg_xq" in c["debug"]:
        dma("g", O["dbg_xq"][0], XQ, (XQ_b,), (O["dbg_xq"][1],), "dbg")
    if "dbg_xkt" in c["debug"]:
        dma("g", O["dbg_xkt"][0], xkt_d, (xkt_db,), (O["dbg_xkt"][1],), "dbg")
    if "dbg_tn" in c["debug"]:
        dma("g", O["dbg_tn"][0], tn_d, (tn_db,), (O["dbg_tn"][1],), "dbg")
    if "dbg_pn" in c["debug"]:
        dma("g", O["dbg_pn"][0], pn_d, (pn_db,), (O["dbg_pn"][1],), "dbg")
    if "dbg_g12" in c["debug"]:
        dma("g", O["dbg_g12"][0], g12t, (g12t_b,), (O["dbg_g12"][1],), "dbg")
    if "dbg_le" in c["debug"]:
        dma("g", O["dbg_le"][0], le_sm, (le_sm_b,), (O["dbg_le"][1],), "dbg")


def ttt_scan(c):
    nc, pg, op, dma, mm = c["nc"], c["pg"], c["op"], c["dma"], c["mm"]
    I, O, DR, SB, PS, SBT = c["I"], c["O"], c["DR"], c["SB"], c["PS"], c["SBT"]
    sbt_get = c["carve"]
    vtt, vts, vstt, vcopy = c["vtt"], c["vts"], c["vstt"], c["vcopy"]
    vrecip, vmemset, sact = c["vrecip"], c["vmemset"], c["sact"]
    st = c["scan_state"]
    XQ, XQ_b = st["XQ"]; XK, XK_b = st["XK"]
    g12t, g12t_b = st["g12t"]
    le_sm, le_sm_b = st["le_sm"]
    le_bf, le_bf_b = st["le_bf"]
    gco, gco_b = st["gco"]
    orbf, orbf_b = SB["onesrow_bf"]
    eps6r, eps6r_b = SB["eps6r"]
    AR_a, _ = SB["AR_a"]
    FA, _ = SB["FA"]

    W1, W1_b = sbt_get("sW1", [128, 128])
    W1bf, W1bf_b = sbt_get("sW1b", [128, 256], BF16)
    b1, b1_b = sbt_get("sb1", [1, 128])
    b1bf, b1bf_b = sbt_get("sb1b", [1, 256], BF16)
    st6, st6_b = sbt_get("sst6", [16, 6])
    mv, mv_b = sbt_get("smv", [16, 2])
    ssm, ssm_b = sbt_get("sssm", [16, 12])
    sxh, sxh_b = sbt_get("ssxh", [16, 128])
    sgy, sgy_b = sbt_get("ssgy", [16, 128])
    sdp, sdp_b = sbt_get("ssdp", [16, 128])
    sgr, sgr_b = sbt_get("ssgr", [16, 128], BF16)
    slk, slk_b = sbt_get("sslk", [16, 128], BF16)
    sot, sot_b = sbt_get("ssot", [16, 256])
    sot_bufs = [pg.buf(f"sot{i}") for i in range(2)]

    dma("y", W1, I["scan_W1"][0], (I["scan_W1"][1],), (W1_b,), "w1i")
    dma("y", b1, I["scan_b1"][0], (I["scan_b1"][1],), (b1_b,), "b1i")
    vcopy(W1bf[:, 0:128], W1bf_b, W1, W1_b, "w1bi")
    vcopy(b1bf[:, 0:128], b1bf_b, b1, b1_b, "b1bi")
    vcopy(W1bf[:, 128:256], W1bf_b, W1, W1_b, "w1bi2")
    vcopy(b1bf[:, 128:256], b1bf_b, b1, b1_b, "b1bi2")
    vts(le_sm, le_sm_b, le_sm, le_sm_b, -1.0, None, ALU.mult, "snle")
    corrT, corrT_b = sbt_get("scorT", [16, 16], BF16)

    xkt_d, xkt_db = DR["dr_xktok"]
    tn_d, tn_db = DR["dr_tn"]
    pn_d, pn_db = DR["dr_pn"]
    out_d, out_db = DR["dr_out"]
    xkc = AR_a[0:16, 8192:10240]
    xkc_bufs = [pg.buf(f"xkc{i}") for i in range(16)]
    tnc = FA[0:16, 0:2048]
    tnc_bufs = [pg.buf(f"tnc{i}") for i in range(16)]
    pnc = FA[0:16, 2048:4096]
    pnc_bufs = [pg.buf(f"pnc{i}") for i in range(16)]

    def load_step(m):
        s = m % 16
        dma("y", xkc[:, 128 * s:128 * s + 128], xkt_d[16 * m:16 * m + 16, :],
            (xkt_db,), (xkc_bufs[s],), "xkcl")
        dma("s", tnc[:, 128 * s:128 * s + 128], tn_d[16 * m:16 * m + 16, :],
            (tn_db,), (tnc_bufs[s],), "tncl")
        dma("g", pnc[:, 128 * s:128 * s + 128], pn_d[16 * m:16 * m + 16, :],
            (pn_db,), (pnc_bufs[s],), "pncl")

    for m in range(8):
        load_step(m)

    pz_t, pz_b = PS["pw0"]; pz = pz_t[0:16, 0:128]
    pz2_t, pz2_b = PS["pw1"]; pz2 = pz2_t[0:16, 0:128]
    pwd_t, pwd_b = PS["pw2"]; pwd = pwd_t[:, 0:128]
    pbd_t, pbd_b = PS["pw3"]; pbd = pbd_t[0:1, 0:128]

    pcor_t, pcor_b = PS["pss0"]
    pcor = pcor_t[0:16, 0:16]

    for n in range(512):
        p = n % 2
        s = n % 16
        sl = slice(128 * s, 128 * s + 128)
        xk_sl, xk_slb = xkc[:, sl], xkc_bufs[s]
        tn_sl, tn_slb = tnc[:, sl], tnc_bufs[s]
        pn_sl, pn_slb = pnc[:, sl], pnc_bufs[s]
        mm(pz, pz_b, XK[:, 16 * n:16 * n + 16],
           W1bf[:, 128 * (1 - p):128 * (1 - p) + 128], (XK_b, W1bf_b), True, False,
           tag="sz1")
        mm(pz, pz_b, orbf[0:1, 0:16], b1bf[0:1, 128 * (1 - p):128 * (1 - p) + 128],
           (orbf_b, b1bf_b), False, n == 0, tag="sz1b")
        if n >= 1:
            mm(pz, pz_b, corrT, sgr, (corrT_b, sgr_b), False, True, tag="scor")
        op("v", lambda v, pz=pz: v.bn_stats(st6, pz), reads=(pz_b,), writes=(st6_b,),
           tag="sbn")
        op("v", lambda v: v.bn_aggr(mv, st6), reads=(st6_b,), writes=(mv_b,), tag="sba")
        sact(ssm[:, 0:1], ssm_b, mv[:, 1:2], mv_b, AF.Sqrt, "sstd",
             bias=eps6r[0:16, :], biasb=eps6r_b)
        vrecip(ssm[:, 1:2], ssm_b, ssm[:, 0:1], ssm_b, "srst")
        vts(ssm[:, 2:3], ssm_b, mv[:, 0:1], mv_b, ssm[:, 1:2], ssm_b, ALU.mult,
            "snmr", s2=-1.0, s2b=None, op1=ALU.mult)
        vts(ssm[:, 5:6], ssm_b, ssm[:, 1:2], ssm_b, 1.0 / 128.0, None, ALU.mult, "srd")
        sact(sxh, sxh_b, pz, pz_b, AF.Copy, "sxh",
             scale=ssm[:, 1:2], scaleb=ssm_b, bias=ssm[:, 2:3], biasb=ssm_b)
        if n + 1 <= 511:
            mm(pcor, pcor_b, XK[:, 16 * n:16 * n + 16],
               XK[:, 16 * n + 16:16 * n + 32], (XK_b,), True, False, tag="spP")
            mm(pcor, pcor_b, orbf[0:1, 0:16], orbf[0:1, 0:16],
               (orbf_b,), False, True, tag="spP1")
            sact(corrT, corrT_b, pcor, pcor_b, AF.Copy, "scev",
                 scale=le_sm[:, n:n + 1], scaleb=le_sm_b)
        vtt(ALU.mult, sgy, sgy_b, sxh, sxh_b, gco[0:16, 256:384], gco_b, "sgy1")
        vstt(sgy, sgy_b, sgy, sgy_b, 1.0, None, tn_sl, tn_slb, ALU.mult, ALU.add,
             "sgy2", acc=ssm[:, 3:4], accb=ssm_b)
        vstt(sdp, sdp_b, sgy, sgy_b, 1.0, None, sxh, sxh_b, ALU.mult, ALU.mult,
             "ss2", acc=ssm[:, 4:5], accb=ssm_b)
        vts(ssm[:, 6:8], ssm_b, ssm[:, 3:5], ssm_b, ssm[:, 5:6], ssm_b, ALU.mult,
            "sa23", s2=-1.0, s2b=None, op1=ALU.mult)
        vts(sgy, sgy_b, sgy, sgy_b, ssm[:, 1:2], ssm_b, ALU.mult, "sq1",
            s2=ssm[:, 6:7], s2b=ssm_b, op1=ALU.add)
        vstt(sgr, sgr_b, sxh, sxh_b, ssm[:, 7:8], ssm_b, sgy, sgy_b, ALU.mult,
             ALU.add, "sgrad")
        vts(slk, slk_b, xk_sl, xk_slb, le_sm[:, n:n + 1], le_sm_b, ALU.mult, "slex")
        mm(pwd, pwd_b, slk, sgr, (slk_b, sgr_b), True, True, tag="swd")
        mm(pbd, pbd_b, le_bf[:, n:n + 1], sgr, (le_bf_b, sgr_b), True, True, tag="sbd")
        mm(pz2, pz2_b, XQ[:, 16 * n:16 * n + 16], W1bf[:, 128 * p:128 * p + 128],
           (XQ_b, W1bf_b), True, False, tag="szb")
        mm(pz2, pz2_b, g12t[0:16, 16 * n:16 * n + 16], sgr, (g12t_b, sgr_b),
           False, False, tag="szbg")
        mm(pz2, pz2_b, orbf[0:1, 0:16], b1bf[0:1, 128 * p:128 * p + 128],
           (orbf_b, b1bf_b), False, True, tag="szbb")
        vtt(ALU.add, W1, W1_b, W1, W1_b, pwd, pwd_b, "sw1u")
        pq = 1 - p
        sact(W1bf[:, 128 * pq:128 * pq + 128], W1bf_b, W1, W1_b, AF.Copy, "sw1c")
        vtt(ALU.subtract, b1, b1_b, b1, b1_b, pbd, pbd_b, "sb1u")
        sact(b1bf[0:1, 128 * pq:128 * pq + 128], b1bf_b, b1, b1_b, AF.Copy, "sb1c")
        op("v", lambda v, pz2=pz2: v.bn_stats(st6, pz2), reads=(pz2_b,), writes=(st6_b,),
           tag="obn")
        op("v", lambda v: v.bn_aggr(mv, st6), reads=(st6_b,), writes=(mv_b,), tag="oba")
        sact(ssm[:, 8:9], ssm_b, mv[:, 1:2], mv_b, AF.Sqrt, "ostd",
             bias=eps6r[0:16, :], biasb=eps6r_b)
        vrecip(ssm[:, 9:10], ssm_b, ssm[:, 8:9], ssm_b, "orst")
        vts(ssm[:, 10:11], ssm_b, mv[:, 0:1], mv_b, ssm[:, 9:10], ssm_b, ALU.mult,
            "onmr", s2=-1.0, s2b=None, op1=ALU.mult)
        sact(sxh, sxh_b, pz2, pz2_b, AF.Copy, "oxh",
             scale=ssm[:, 9:10], scaleb=ssm_b, bias=ssm[:, 10:11], biasb=ssm_b)
        vtt(ALU.mult, sgy, sgy_b, sxh, sxh_b, gco[0:16, 0:128], gco_b, "oo1",
            eng="g")
        osl = sot[:, 128 * (n % 2):128 * (n % 2) + 128]
        op("g", lambda g, osl=osl, sgy=sgy, pn_sl=pn_sl: g.tensor_add(osl, sgy, pn_sl),
           reads=(sgy_b, pn_slb), writes=(sot_bufs[n % 2],), tag="oo2")
        dma("g", out_d[16 * n:16 * n + 16, :], osl, (sot_bufs[n % 2],), (out_db,),
            "osto")
        if n + 8 < 512:
            load_step(n + 8)

    if "dbg_scanout" in c["debug"]:
        dma("g", O["dbg_scanout"][0], out_d, (out_db,), (O["dbg_scanout"][1],), "dbg")


def post_ttt(c):
    nc, pg, op, dma, mm = c["nc"], c["pg"], c["op"], c["dma"], c["mm"]
    I, O, DR, SB, PS, SBT = c["I"], c["O"], c["DR"], c["SB"], c["PS"], c["SBT"]
    sbt_get = c["carve"]
    ln_rows, bcast = c["ln_rows"], c["bcast"]
    vtt, vts, vstt, vcopy = c["vtt"], c["vts"], c["vstt"], c["vcopy"]
    vrecip, vmemset, sact = c["vrecip"], c["vmemset"], c["sact"]
    oc, oc_b = SB["onescol"]
    AR_q, _ = SB["AR_q"]
    FA, _ = SB["FA"]
    evA, evA_b = SB["evA"]; evB, evB_b = SB["evB"]
    sq0, sq0_b = SB["sq0"]
    idf, idf_b = SB["ident_f"]
    wproj, _ = SB["wproj"]
    pwn = ["pw0", "pw1", "pw2", "pw3"]

    out_d, out_db = DR["dr_out"]
    a2a, a2a_b = DR["dr_out_all"]
    grp = [[0, 1, 2, 3], [4, 5, 6, 7]]
    c["cc_barrier"](grp, (out_db,), "oag")
    pg.cc(lambda g, a=out_d, b=a2a, grp=grp: g.collective_compute(
        "AllGather", ALU.bypass, replica_groups=grp, ins=[a], outs=[b]),
        reads=(out_db,), writes=(a2a_b,), tag="oag")

    oidx, oidx_b = sbt_get("soidx", [128, 32], mybir.dt.int32)
    dma("s", oidx, I["oidx"][0], (I["oidx"][1],), (oidx_b,), "oidx")
    mix_bufs = [pg.buf(f"mix{i}") for i in range(2)]
    ldb = pg.buf("postld")
    from concourse.bass import IndirectOffsetOnAxis
    for h in range(2):
        for tt in range(16):
            j = 16 * h + tt
            pg.dma("g", lambda e, j=j: e.indirect_dma_start(
                evA[:, 0:128], None, a2a[0:128, :],
                IndirectOffsetOnAxis(oidx[:, j:j + 1], 0)),
                reads=(a2a_b, oidx_b), writes=(ldb,), tag="pgth")
            pst, psb = PS[pwn[tt % 2]]
            op("t", lambda t, pst=pst: t.transpose(pst[:, 0:128], evA[:, 0:128], idf),
               reads=(ldb, idf_b), writes=(psb,), tag="ptr")
            sact(FA[:, 2048 * h + 128 * tt: 2048 * h + 128 * tt + 128], mix_bufs[h],
                 pst[:, 0:128], psb, AF.Copy, "ptre")

    if "dbg_p2" in c["debug"]:
        dma("g", O["dbg_p2"][0], FA[:, 0:2048], (mix_bufs[0],), (O["dbg_p2"][1],), "dbg")
    if "dbg_p3" in c["debug"]:
        dma("g", O["dbg_p3"][0], FA[:, 2048:4096], (mix_bufs[1],), (O["dbg_p3"][1],), "dbg")
    pw_sb, pw_b = sbt_get("postws", [128, 2])
    pb_sb, pb_b = sbt_get("postbs", [128, 2])
    dma("s", pw_sb, I["post_w"][0], (I["post_w"][1],), (pw_b,), "pw")
    dma("s", pb_sb, I["post_b"][0], (I["post_b"][1],), (pb_b,), "pb")
    lw_sb, lw_b = sbt_get("lntws", [128, 2])
    lb_sb, lb_b = sbt_get("lntbs", [128, 2])
    dma("s", lw_sb, I["lnt_w"][0], (I["lnt_w"][1],), (lw_b,), "lw")
    dma("s", lb_sb, I["lnt_b"][0], (I["lnt_b"][1],), (lb_b,), "lb")
    rwA, rwA_b = SB["rwA"]; rwB, rwB_b = SB["rwB"]
    postn_bufs = [pg.buf(f"postn{i}") for i in range(2)]
    wo32_bufs = mix_bufs
    lnout_bufs = [pg.buf(f"lnout{i}") for i in range(2)]

    for tc in range(4):
        t0 = 512 * tc
        chunks = [(FA[:, 2048 * ch + t0: 2048 * ch + t0 + 512], mix_bufs[ch])
                  for ch in range(2)]
        ln_rows(chunks, 512, 256, (rwA, rwA_b), (rwB, rwB_b), "pln")
        Abc, Abc_b = bcast(rwA, rwA_b, 512, "pss0", 0, "plA")
        Bbc, Bbc_b = bcast(rwB, rwB_b, 512, "pss0", 512, "plB")
        for ch in range(2):
            cap, cb = chunks[ch]
            vtt(ALU.mult, evB, evB_b, cap, cb, Abc, Abc_b, "pl1")
            vtt(ALU.add, evB, evB_b, evB, evB_b, Bbc, Bbc_b, "pl2")
            sact(AR_q[:, 2048 * ch + t0: 2048 * ch + t0 + 512], postn_bufs[ch],
                 evB, evB_b, AF.Copy, "pl3", bias=pb_sb[:, ch:ch + 1], biasb=pb_b,
                 scale=pw_sb[:, ch:ch + 1], scaleb=pw_b)

    if "dbg_p4" in c["debug"]:
        dma("g", O["dbg_p4"][0], AR_q[:, 0:4096],
            (postn_bufs[0], postn_bufs[1]), (O["dbg_p4"][1],), "dbg")
    wot_b = pg.buf("wotb")
    dma("g", wproj[:, 0:512], I["wo_tr"][0], (I["wo_tr"][1],), (wot_b,), "wot")
    for oc2 in range(2):
        for tc in range(4):
            t0 = 512 * tc
            pst, psb = PS[pwn[tc % 2]]
            ps = pst[:, 0:512]
            for kc in range(2):
                mm(ps, psb, wproj[:, 256 * kc + 128 * oc2: 256 * kc + 128 * oc2 + 128],
                   AR_q[:, 2048 * kc + t0: 2048 * kc + t0 + 512],
                   (wot_b, postn_bufs[kc]), kc == 0, kc == 1, tag="wom")
            sact(FA[:, 2048 * oc2 + t0: 2048 * oc2 + t0 + 512], wo32_bufs[oc2],
                 ps, psb, AF.Copy, "woe")
    if "dbg_p5" in c["debug"]:
        dma("g", O["dbg_p5"][0], FA[:, 0:2048], (wo32_bufs[0],), (O["dbg_p5"][1],), "dbg")
        dma("g", O["dbg_p6"][0], FA[:, 2048:4096], (wo32_bufs[1],), (O["dbg_p6"][1],), "dbg")
    for tc in range(4):
        t0 = 512 * tc
        chunks = [(FA[:, 2048 * ch + t0: 2048 * ch + t0 + 512], wo32_bufs[ch])
                  for ch in range(2)]
        ln_rows(chunks, 512, 256, (rwA, rwA_b), (rwB, rwB_b), "tln")
        Abc, Abc_b = bcast(rwA, rwA_b, 512, "pss0", 0, "tlA")
        Bbc, Bbc_b = bcast(rwB, rwB_b, 512, "pss0", 512, "tlB")
        for ch in range(2):
            cap, cb = chunks[ch]
            vtt(ALU.mult, evB, evB_b, cap, cb, Abc, Abc_b, "tl1")
            vtt(ALU.add, evB, evB_b, evB, evB_b, Bbc, Bbc_b, "tl2")
            sact(AR_q[:, 4096 + 2048 * ch + t0: 4096 + 2048 * ch + t0 + 512],
                 lnout_bufs[ch], evB, evB_b, AF.Copy, "tl3",
                 bias=lb_sb[:, ch:ch + 1], biasb=lb_b,
                 scale=lw_sb[:, ch:ch + 1], scaleb=lw_b)

    if "dbg_p8" in c["debug"]:
        dma("g", O["dbg_p8"][0], AR_q[:, 4096:8192],
            (lnout_bufs[0], lnout_bufs[1]), (O["dbg_p8"][1],), "dbg")
    upw_b = wot_b
    dma("g", wproj[:, 0:2048], I["up_wr"][0], (I["up_wr"][1],), (upw_b,), "upw")
    upb_sb, upb_b = sbt_get("upbs", [128, 8])
    dma("s", upb_sb, I["up_b"][0], (I["up_b"][1],), (upb_b,), "upb")
    y1d, y1d_b = DR["dr_y1"]
    y2d, y2d_b = DR["dr_y2"]
    xsb, _ = SB["xsb"]
    y1l_b = pg.buf("py1l")
    for dc in range(8):
        for tc in range(4):
            t0 = 512 * tc
            pst, psb = PS[pwn[tc % 2]]
            ps = pst[:, 0:512]
            for kc in range(2):
                mm(ps, psb, wproj[:, 1024 * kc + 128 * dc: 1024 * kc + 128 * dc + 128],
                   AR_q[:, 4096 + 2048 * kc + t0: 4096 + 2048 * kc + t0 + 512],
                   (upw_b, lnout_bufs[kc]), kc == 0, kc == 1, tag="upm")
            sact(evB, evB_b, ps, psb, AF.Copy, "upe",
                 bias=upb_sb[:, dc:dc + 1], biasb=upb_b)
            xc = xsb[:, 0:512]
            dma("y", xc, y1d[128 * dc:128 * dc + 128, t0:t0 + 512], (y1d_b,), (y1l_b,),
                "py1")
            vtt(ALU.add, sq0, sq0_b, evB, evB_b, xc, y1l_b, "py2")
            dma("g", y2d[128 * dc:128 * dc + 128, t0:t0 + 512], sq0[:, 0:512],
                (sq0_b,), (y2d_b,), "py2s")

    if "dbg_y2" in c["debug"]:
        dma("g", O["dbg_y2"][0], y2d, (y2d_b,), (O["dbg_y2"][1],), "dbg")


def finalize_debug(c):
    pass



_CACHED = {}

def _mk_sel8():
    a = np.zeros((8, 1024), dtype=np.float32)
    for e in range(8):
        a[e, 128 * e:128 * e + 128] = 1.0
    return a.astype(ml_dtypes.bfloat16)


def kernel(**inputs):
    key = "nc"
    if key not in _CACHED:
        _CACHED[key] = build_program(debug=DEBUG)
    nc = _CACHED[key]

    f = lambda a: np.ascontiguousarray(np.asarray(a, dtype=np.float32))
    x = f(inputs["L0_attn_x"]).reshape(B * L, D)

    wq = f(inputs["L0_attn_wq"]); wk = f(inputs["L0_attn_wk"]); wv = f(inputs["L0_attn_wv"])
    W1 = f(inputs["L0_attn_W1"]); b1t = f(inputs["L0_attn_b1"])
    lnw = f(inputs["L0_attn_ttt_ln_w"]); lnb = f(inputs["L0_attn_ttt_ln_b"])
    lrw = f(inputs["L0_attn_lr_w"]); lrb = f(inputs["L0_attn_lr_b"])

    def colchunks(a, nch):
        return np.ascontiguousarray(a.reshape(nch, 128).T)

    common = {
        "fWq": f(inputs["L0_ffn_Wq"]), "fWk": f(inputs["L0_ffn_Wk"]),
        "fWv": f(inputs["L0_ffn_Wv"]), "fWo": f(inputs["L0_ffn_Wo"]),
        "fbq": colchunks(f(inputs["L0_ffn_bq"]), 8),
        "fbk": colchunks(f(inputs["L0_ffn_bk"]), 8),
        "fbv": f(inputs["L0_ffn_bv"]).reshape(1, D),
        "fbo": colchunks(f(inputs["L0_ffn_bo"]), 8),
        "gWr": np.ascontiguousarray(
            f(inputs["L0_ffn_gW"]).reshape(8, 128, 8).transpose(1, 0, 2).reshape(128, 64)),
        "gb": f(inputs["L0_ffn_gb"]).reshape(1, 8),
        "b1r": f(inputs["L0_ffn_b1"]).reshape(1, NE * FF),
        "b2e": f(inputs["L0_ffn_b2"]),
        "ln1g": colchunks(f(inputs["L0_ffn_ln1_g"]), 8),
        "ln1b": colchunks(f(inputs["L0_ffn_ln1_b"]), 8),
        "ln2g": colchunks(f(inputs["L0_ffn_ln2_g"]), 8),
        "ln2b": colchunks(f(inputs["L0_ffn_ln2_b"]), 8),
        "down_w": f(inputs["L0_attn_down_w"]),
        "down_b": colchunks(f(inputs["L0_attn_down_b"]), 2),
        "up_wr": np.ascontiguousarray(
            f(inputs["L0_attn_up_w"]).reshape(2, 128, D).transpose(1, 0, 2).reshape(128, 2 * D)),
        "up_b": colchunks(f(inputs["L0_attn_up_b"]), 8),
        "wo_tr": np.ascontiguousarray(
            f(inputs["L0_attn_wo"]).reshape(2, 128, 256).transpose(1, 0, 2).reshape(128, 512)),
        "post_w": colchunks(f(inputs["L0_attn_post_w"]), 2),
        "post_b": colchunks(f(inputs["L0_attn_post_b"]), 2),
        "lnt_w": colchunks(f(inputs["L0_attn_ln_w"]), 2),
        "lnt_b": colchunks(f(inputs["L0_attn_ln_b"]), 2),
        "tok_idx": f(inputs["L0_attn_tok_idx"]).reshape(1, 16),
        "ident": np.eye(128, dtype=np.float32),
        "sel8": _mk_sel8(),
        "rarange": (1.0 / np.arange(1, 17, dtype=np.float32)).reshape(1, 16),
        "masku": np.ascontiguousarray(
            np.tile(-np.triu(np.ones((16, 16), dtype=np.float32)), (1, 512))),
    }
    eW1 = f(inputs["L0_ffn_W1"]); eW2 = f(inputs["L0_ffn_W2"])

    in_maps = []
    for core in range(8):
        h = core % 2
        m = dict(common)
        m["x_fm"] = np.ascontiguousarray(x[core * TOK:(core + 1) * TOK, :].T)
        tok0 = 2048 * (core % 4)
        oi = np.zeros((128, 32), dtype=np.int32)
        for hh in range(2):
            for tt in range(16):
                oi[:, 16 * hh + tt] = 8192 * hh + tok0 + 128 * tt + np.arange(128)
        m["oidx"] = oi
        m["eW1own"] = eW1[core]
        m["eW2own"] = eW2[core]
        m["scan_wq"] = np.ascontiguousarray(wq[:, 128 * h:128 * h + 128])
        m["scan_wk"] = np.ascontiguousarray(wk[:, 128 * h:128 * h + 128])
        m["scan_wv"] = np.ascontiguousarray(wv[:, 128 * h:128 * h + 128])
        m["scan_W1"] = W1[h]
        m["scan_b1"] = b1t[h].reshape(1, 128)
        m["scan_g"] = lnw[h].reshape(1, 128)
        m["scan_gb"] = lnb[h].reshape(1, 128)
        m["scan_lrw"] = colchunks(lrw[h, 0], 2)
        m["scan_lrb"] = np.array([[f(inputs["L0_attn_lr_b"])[h]]], dtype=np.float32)
        in_maps.append(m)

    from concourse.aot_env import allow_ici_all_to_all
    with allow_ici_all_to_all():
        res = run_bass_kernel_spmd(nc, in_maps, core_ids=list(range(8)))
    out = np.concatenate([np.asarray(r["out_fm"]).T for r in res.results], axis=0)
    kernel._last_results = res
    return np.ascontiguousarray(out.reshape(B, L, D), dtype=np.float32)
